# version 01
# baseline (speedup 1.0000x reference)
import math
import numpy as np
from contextlib import ExitStack

import concourse.bass as bass
import concourse.mybir as mybir
from concourse.bass_utils import run_bass_kernel_spmd

F32 = mybir.dt.float32
BF16 = mybir.dt.bfloat16
I32 = mybir.dt.int32
AF = mybir.ActivationFunctionType
ALU = mybir.AluOpType
AX = mybir.AxisListType

B, N, D, H, DH = 8, 2048, 1024, 16, 64
MEM, J = 16, 2064
INNER = H * DH
FF = 4096
NQS = 8
CHQ = 32
NCH = N // CHQ
SPC = CHQ // NQS
SHIFT = 16.0
TOPK = 64
NIT_BF, NIT_F32 = 8, 9
ULP16 = 0.0625
BIS_LO, BIS_HI = 8.0, 26.0
VK_INACT = 1.0
WMAX = J
SLW = 17 * 128
BTW = 17 * 16 * 32

N1, N2 = B * N * D, B * N * FF
P1, P2 = 0.995 * (N1 - 1), 0.995 * (N2 - 1)
KHI1 = N1 - 1 - math.floor(P1)
FRAC1 = P1 - math.floor(P1)
KHI2 = N2 - 1 - math.floor(P2)
FRAC2 = P2 - math.floor(P2)
QCAP1, QCAP2 = 8.0, 8.0
NIT_SEL = 64
KEXT = 3
NBIN = 16


def cdiv(a, b):
    return (a + b - 1) // b


def slice_W(c, s):
    return CHQ * c + NQS * s + 24


def build_LA(debug=False):
    nc = bass.Bass("TRN2", target_bir_lowering=False, debug=False)
    hx = nc.dram_tensor("x", [N, D], F32, kind="ExternalInput").ap()
    hxT = nc.dram_tensor("xT", [D, N], F32, kind="ExternalInput").ap()
    hWq = nc.dram_tensor("Wq", [8, 128, INNER], F32, kind="ExternalInput").ap()
    hWk = nc.dram_tensor("Wk", [8, 128, INNER], F32, kind="ExternalInput").ap()
    hWv = nc.dram_tensor("Wv", [8, 128, INNER], F32, kind="ExternalInput").ap()
    hWo = nc.dram_tensor("Wo", [8, 128, D], F32, kind="ExternalInput").ap()
    hbo = nc.dram_tensor("bo", [1, D], F32, kind="ExternalInput").ap()
    hpre = nc.dram_tensor("preM", [128, 128], F32, kind="ExternalInput").ap()
    hpost = nc.dram_tensor("postM", [128, 128], F32, kind="ExternalInput").ap()
    hmemk = nc.dram_tensor("memkP", [128, 8 * MEM], F32, kind="ExternalInput").ap()
    hmemv = nc.dram_tensor("memvT", [MEM, INNER], F32, kind="ExternalInput").ap()
    hmsk7 = nc.dram_tensor("mask7", [128, 8], F32, kind="ExternalInput").ap()
    hg = nc.dram_tensor("lng", [1, D], F32, kind="ExternalInput").ap()
    hb = nc.dram_tensor("lnb", [1, D], F32, kind="ExternalInput").ap()
    ox1 = nc.dram_tensor("x1", [N, D], F32, kind="ExternalOutput").ap()
    ohabs = nc.dram_tensor("habs", [N, D], F32, kind="ExternalOutput").ap()
    oh = nc.dram_tensor("hsig", [N, D], F32, kind="ExternalOutput").ap()
    ocnt = nc.dram_tensor("cnt1", [1, NBIN], F32, kind="ExternalOutput").ap()
    dbg = {}
    if debug:
        dbg["qp"] = nc.dram_tensor("dbg_qp", [128, N], F32, kind="ExternalOutput").ap()
        dbg["kp"] = nc.dram_tensor("dbg_kp", [128, J], F32, kind="ExternalOutput").ap()
        dbg["vt"] = nc.dram_tensor("dbg_vt", [128, INNER], F32, kind="ExternalOutput").ap()
        dbg["ds"] = nc.dram_tensor("dbg_ds", [128, WMAX], F32, kind="ExternalOutput").ap()
        dbg["vk"] = nc.dram_tensor("dbg_vk", [128, 24], F32, kind="ExternalOutput").ap()
        dbg["at"] = nc.dram_tensor("dbg_at", [128, WMAX], F32, kind="ExternalOutput").ap()
        dbg["y"] = nc.dram_tensor("dbg_y", [128, D], F32, kind="ExternalOutput").ap()
        dbg["bt"] = nc.dram_tensor("dbg_bt", [128, BTW], F32, kind="ExternalOutput").ap()

    ctx = ExitStack()
    with ctx:
        def sb(name, shape, dtype=F32, st=ctx):
            return st.enter_context(nc.sbuf_tensor(name, shape, dtype))
        def pt(name, shape, dtype=F32):
            return ctx.enter_context(nc.psum_tensor(name, shape, dtype))

        QP = [sb(f"QP{g}", [128, N], BF16) for g in range(8)]
        KP = [sb(f"KP{g}", [128, J], BF16) for g in range(8)]
        VT = [sb(f"VT{t}", [128, INNER], BF16) for t in range(16)] + [sb("VT16", [16, INNER], BF16)]
        WB = [sb(f"WB{d}", [128, INNER], BF16) for d in range(8)]
        preM = sb("preMs", [128, 128], BF16)
        postM = sb("postMs", [128, 128], BF16)
        bo_t = sb("bo_t", [1, D], F32)
        ones1 = sb("ones1", [1, 128], F32)
        mask7 = sb("mask7s", [128, 8], F32)
        gr = sb("gr", [128, D], F32)
        br = sb("br", [128, D], F32)
        shift_t = sb("shift_t", [128, 1], F32)
        eps_t = sb("eps_t", [128, 1], F32)
        sc8_t = sb("sc8_t", [128, 1], F32)
        scD_t = sb("scD_t", [128, 1], F32)
        idx8 = sb("idx8", [128, 8], F32)
        wk32 = sb("wk32", [128, WMAX], F32)

        pA = pt("pA", [128, 512], F32)
        pB = pt("pB", [128, 512], F32)
        pC = pt("pC", [128, 512], F32)
        pD = pt("pD", [128, 512], F32)
        pW = pt("pW", [128, D], F32)
        pV = pt("pV", [128, 128], F32)

        SEM = {}
        for k in ("dQ", "dS", "dG", "sT", "sV", "sA"):
            SEM[k] = ctx.enter_context(nc.semaphore(k))
        C = {k: 0 for k in SEM}
        block = ctx.enter_context(nc.Block())
        SY, SC, VE, TE, GP = block.sync, block.scalar, block.vector, block.tensor, block.gpsimd

        ph0 = ExitStack()
        xfb = [sb(f"xfb{d}", [128, N], BF16, st=ph0) for d in range(8)]
        xstg = [sb(f"xstg{i}", [128, N], F32, st=ph0) for i in range(2)]
        vtmp = sb("vtmp", [128, INNER], BF16, st=ph0)

        def _(sy):
            sy.dma_start(out=wk32[:, 0:128], in_=hpre).then_inc(SEM["dQ"], 16)
            sy.dma_start(out=wk32[:, 128:256], in_=hpost).then_inc(SEM["dQ"], 16)
            sy.dma_start(out=wk32[:, 256:256 + 8 * MEM], in_=hmemk).then_inc(SEM["dQ"], 16)
            sy.dma_start(out=xstg[0][0:MEM, 0:INNER], in_=hmemv).then_inc(SEM["dQ"], 16)
            sy.dma_start(out=bo_t[:, :], in_=hbo).then_inc(SEM["dQ"], 16)
            sy.dma_start(out=mask7[:, :], in_=hmsk7).then_inc(SEM["dQ"], 16)
            sy.dma_start(out=xstg[1][0:1, 0:D], in_=hg).then_inc(SEM["dQ"], 16)
        SY(_); C["dQ"] += 7 * 16
        def _(sy):
            sy.dma_start(out=xstg[1][0:1, D:2 * D], in_=hb).then_inc(SEM["dQ"], 16)
        SY(_); C["dQ"] += 16

        def _(v):
            v.wait_ge(SEM["dQ"], C["dQ"])
            v.tensor_copy(out=preM[:, :], in_=wk32[:, 0:128])
            v.tensor_copy(out=postM[:, :], in_=wk32[:, 128:256])
            for g in range(8):
                v.tensor_copy(out=KP[g][:, 0:MEM], in_=wk32[:, 256 + g * MEM:256 + (g + 1) * MEM])
            v.tensor_copy(out=VT[0][0:MEM, :], in_=xstg[0][0:MEM, 0:INNER])
            v.memset(ones1[:, :], 1.0)
            v.memset(shift_t[:, :], SHIFT)
            v.memset(eps_t[:, :], 1e-5)
            v.memset(sc8_t[:, :], 0.125)
            v.memset(scD_t[:, :], 1.0 / D)
            for i in range(8):
                v.memset(idx8[:, i:i + 1], float(i))
            v.engine_nop().then_inc(SEM["sV"], 1)
        VE(_); C["sV"] += 1

        def _(te):
            te.wait_ge(SEM["sV"], C["sV"])
            te.matmul(pW[:, 0:512], lhsT=ones1[:, :], rhs=xstg[1][0:1, 0:512], start=True, stop=True)
            te.matmul(pW[:, 512:1024], lhsT=ones1[:, :], rhs=xstg[1][0:1, 512:D], start=True, stop=True).then_inc(SEM["sT"], 1)
        TE(_); C["sT"] += 1
        def _(v):
            v.wait_ge(SEM["sT"], C["sT"])
            v.tensor_copy(out=gr[:, :], in_=pW[:, :]).then_inc(SEM["sV"], 1)
        VE(_); C["sV"] += 1
        def _(te):
            te.wait_ge(SEM["sV"], C["sV"])
            te.matmul(pW[:, 0:512], lhsT=ones1[:, :], rhs=xstg[1][0:1, D:D + 512], start=True, stop=True)
            te.matmul(pW[:, 512:1024], lhsT=ones1[:, :], rhs=xstg[1][0:1, D + 512:2 * D], start=True, stop=True).then_inc(SEM["sT"], 1)
        TE(_); C["sT"] += 1
        def _(v):
            v.wait_ge(SEM["sT"], C["sT"])
            v.tensor_copy(out=br[:, :], in_=pW[:, :]).then_inc(SEM["sV"], 1)
        VE(_); C["sV"] += 1

        for d in range(8):
            def _(sy, d=d):
                sy.wait_ge(SEM["sV"], C["sV"])
                sy.wait_ge(SEM["sT"], C["sT"])
                sy.dma_start(out=xstg[d % 2][:, :], in_=hxT[128 * d:128 * (d + 1), :]).then_inc(SEM["dQ"], 16)
            SY(_); C["dQ"] += 16
            dq = C["dQ"]
            def _(v, d=d, dq=dq):
                v.wait_ge(SEM["dQ"], dq)
                v.tensor_copy(out=xfb[d][:, :], in_=xstg[d % 2][:, :]).then_inc(SEM["sV"], 1)
            VE(_); C["sV"] += 1

        for proj, hW in (("q", hWq), ("k", hWk), ("v", hWv)):
            for d in range(8):
                def _(sc, d=d, hW=hW):
                    sc.wait_ge(SEM["sV"], C["sV"])
                    sc.dma_start(out=xstg[d % 2][:, 0:INNER], in_=hW[d]).then_inc(SEM["dS"], 16)
                SC(_); C["dS"] += 16
                dsc = C["dS"]
                def _(v, d=d, dsc=dsc):
                    v.wait_ge(SEM["dS"], dsc)
                    v.tensor_copy(out=WB[d][:, :], in_=xstg[d % 2][:, 0:INNER]).then_inc(SEM["sV"], 1)
                VE(_); C["sV"] += 1
            vAll = C["sV"]
            if proj in ("q", "k"):
                for g in range(8):
                    for qc in range(4):
                        ps = [pA, pB][(g * 4 + qc) % 2]
                        def _(te, g=g, qc=qc, ps=ps, vAll=vAll):
                            te.wait_ge(SEM["sV"], vAll)
                            te.wait_ge(SEM["sA"], C["sA"])
                            for d in range(8):
                                mm = te.matmul(ps[:, :], lhsT=WB[d][:, 128 * g:128 * (g + 1)],
                                               rhs=xfb[d][:, 512 * qc:512 * (qc + 1)],
                                               start=(d == 0), stop=(d == 7))
                            mm.then_inc(SEM["sT"], 1)
                        TE(_); C["sT"] += 1
                        tS = C["sT"]
                        if proj == "q":
                            def _(sc, g=g, qc=qc, ps=ps, tS=tS):
                                sc.wait_ge(SEM["sT"], tS)
                                sc.activation(out=QP[g][:, 512 * qc:512 * (qc + 1)], in_=ps[:, :],
                                              func=AF.Copy, scale=0.125).then_inc(SEM["sA"], 1)
                            SC(_); C["sA"] += 1
                        else:
                            def _(sc, g=g, qc=qc, ps=ps, tS=tS):
                                sc.wait_ge(SEM["sT"], tS)
                                sc.activation(out=KP[g][:, MEM + 512 * qc:MEM + 512 * (qc + 1)], in_=ps[:, :],
                                              func=AF.Copy).then_inc(SEM["sA"], 1)
                            SC(_); C["sA"] += 1
            else:
                for t in range(17):
                    tok0 = max(0, 128 * t - MEM)
                    tok1 = min(N, 128 * t + 128 - MEM)
                    M = tok1 - tok0
                    def _(te, t=t, tok0=tok0, M=M, vAll=vAll):
                        te.wait_ge(SEM["sV"], vAll)
                        te.wait_ge(SEM["sA"], C["sA"])
                        for half in range(2):
                            for d in range(8):
                                mm = te.matmul(pW[0:M, 512 * half:512 * (half + 1)],
                                               lhsT=xfb[d][:, tok0:tok0 + M],
                                               rhs=WB[d][:, 512 * half:512 * (half + 1)],
                                               start=(d == 0), stop=(d == 7))
                        mm.then_inc(SEM["sT"], 1)
                    TE(_); C["sT"] += 1
                    tS = C["sT"]
                    if t == 0:
                        def _(sc, tS=tS):
                            sc.wait_ge(SEM["sT"], tS)
                            sc.activation(out=xfb[0].bitcast(BF16)[0:112, 0:INNER] if False else vtmp[0:112, :],
                                          in_=pW[0:112, :], func=AF.Copy).then_inc(SEM["sA"], 1)
                        SC(_); C["sA"] += 1
                        aS0 = C["sA"]
                        def _(sy, aS0=aS0):
                            sy.wait_ge(SEM["sA"], aS0)
                            sy.dma_start(out=VT[0][MEM:128, :], in_=vtmp[0:112, :]).then_inc(SEM["dQ"], 16)
                        SY(_); C["dQ"] += 16
                    else:
                        def _(sc, t=t, M=M, tS=tS):
                            sc.wait_ge(SEM["sT"], tS)
                            sc.activation(out=VT[t][0:M, :], in_=pW[0:M, :], func=AF.Copy).then_inc(SEM["sA"], 1)
                        SC(_); C["sA"] += 1

        for d in range(8):
            def _(sc, d=d):
                sc.wait_ge(SEM["sV"], C["sV"])
                sc.dma_start(out=xstg[d % 2][:, 0:D], in_=hWo[d]).then_inc(SEM["dS"], 16)
            SC(_); C["dS"] += 16
            dsc = C["dS"]
            def _(v, d=d, dsc=dsc):
                v.wait_ge(SEM["dS"], dsc)
                v.wait_ge(SEM["sT"], C["sT"])
                v.tensor_copy(out=WB[d][:, :D], in_=xstg[d % 2][:, 0:D]).then_inc(SEM["sV"], 1)
            VE(_); C["sV"] += 1

        if debug:
            def _(sy):
                sy.wait_ge(SEM["sA"], C["sA"])
                sy.wait_ge(SEM["sV"], C["sV"])
            SY(_)
            def _(v, aCur=C["sA"]):
                v.wait_ge(SEM["sA"], aCur)
                v.tensor_copy(out=xstg[0][:, :], in_=QP[0][:, :]).then_inc(SEM["sV"], 1)
            VE(_); C["sV"] += 1
            def _(sy):
                sy.wait_ge(SEM["sV"], C["sV"])
                sy.dma_start(out=dbg["qp"], in_=xstg[0][:, :]).then_inc(SEM["dQ"], 16)
            SY(_); C["dQ"] += 16
            def _(v, aCur=C["sA"]):
                v.wait_ge(SEM["sA"], aCur)
                v.tensor_copy(out=wk32[:, 0:J], in_=KP[0][:, :]).then_inc(SEM["sV"], 1)
            VE(_); C["sV"] += 1
            def _(sy):
                sy.wait_ge(SEM["sV"], C["sV"])
                sy.dma_start(out=dbg["kp"], in_=wk32[:, 0:J]).then_inc(SEM["dQ"], 16)
            SY(_); C["dQ"] += 16
            def _(v, aCur=C["sA"]):
                v.wait_ge(SEM["sA"], aCur)
                v.tensor_copy(out=xstg[0][:, 0:INNER], in_=VT[1][:, :]).then_inc(SEM["sV"], 1)
            VE(_); C["sV"] += 1
            def _(sy):
                sy.wait_ge(SEM["sV"], C["sV"])
                sy.dma_start(out=dbg["vt"], in_=xstg[0][:, 0:INNER]).then_inc(SEM["dQ"], 16)
            SY(_); C["dQ"] += 16

        def _(sy):
            sy.wait_ge(SEM["sA"], C["sA"])
            sy.wait_ge(SEM["sV"], C["sV"])
            sy.wait_ge(SEM["sT"], C["sT"])
            sy.wait_ge(SEM["dQ"], C["dQ"])
        SY(_)
        ph0.close()

        qkbT = [sb(f"qkb{t}", [128, WMAX], BF16) for t in range(4)]
        qkb = [(qkbT[g // 2], 64 * (g % 2)) for g in range(8)]
        slb = sb("slb", [128, SPC * SLW], BF16)
        dsb = [sb(f"ds{i}", [128, WMAX], F32) for i in range(1)]
        d16 = [sb(f"d16_{i}", [128, WMAX], BF16) for i in range(1)]
        bt = sb("bt", [128, BTW], BF16)
        ctx_t = [sb(f"ctx{p}", [128, 128], BF16) for p in range(8)]
        qt = [sb(f"qt{i}", [128, 64], BF16) for i in range(2)]
        xres = sb("xres", [128, D], F32)
        sq_t = wk32[:, 1024:2048]
        qcnt = sb("qcnt", [128, 16 * NBIN], F32)
        qred = sb("qred", [128, NBIN], F32)
        qfin = sb("qfin", [1, NBIN], F32)
        st_lo = sb("st_lo", [128, 2], F32)
        st_w = sb("st_w", [128, 2], F32)
        st_mid = sb("st_mid", [128, 2], F32)
        st_cnt = sb("st_cnt", [128, 2], F32)
        st_u = sb("st_u", [128, 2], F32)
        st_hi = sb("st_hi", [128, 2], F32)
        chi_t = sb("chi_t", [128, 2], F32)
        vk_t = sb("vk_t", [128, 2], F32)
        nvk = sb("nvk", [128, 2], F32)
        zz = sb("zz", [128, 2], F32)
        zr = sb("zr", [128, 2], F32)
        m8v = sb("m8v", [128, 8], F32)
        sel8 = sb("sel8", [128, 8], F32)
        mu_t = sb("mu_t", [128, 1], F32)
        var_t = sb("var_t", [128, 1], F32)
        rs_t = sb("rs_t", [128, 1], F32)

        def _(v):
            v.memset(slb[:, :], 0.0)
            v.memset(qcnt[:, :], 0.0)
            for i in range(2):
                v.memset(qt[i][:, :], 0.0)
            v.engine_nop().then_inc(SEM["sV"], 1)
        VE(_); C["sV"] += 1

        for c in range(NCH):
            Wc = CHQ * c + 48
            q0 = CHQ * c
            njc = cdiv(Wc, 512)
            for g in range(8):
                def _(v, g=g, aCur=C["sA"]):
                    v.wait_ge(SEM["sT"], C["sT"])
                    v.wait_ge(SEM["sA"], aCur)
                    v.tensor_copy(out=qt[g % 2][0:64, 0:32], in_=QP[g][0:64, q0:q0 + 32])
                    v.tensor_copy(out=qt[g % 2][64:128, 32:64], in_=QP[g][64:128, q0:q0 + 32]).then_inc(SEM["sV"], 1)
                VE(_); C["sV"] += 1
                vS = C["sV"]
                for jc in range(njc):
                    jn = min(512, Wc - 512 * jc)
                    ps = [pA, pB][jc % 2]
                    def _(te, g=g, jc=jc, jn=jn, ps=ps, vS=vS):
                        te.wait_ge(SEM["sV"], vS)
                        te.wait_ge(SEM["sA"], C["sA"])
                        te.matmul(ps[0:64, 0:jn], lhsT=qt[g % 2][:, :], rhs=KP[g][:, 512 * jc:512 * jc + jn],
                                  start=True, stop=True).then_inc(SEM["sT"], 1)
                    TE(_); C["sT"] += 1
                    tS = C["sT"]
                    def _(sc, g=g, jc=jc, jn=jn, ps=ps, tS=tS, dg=C["dG"]):
                        sc.wait_ge(SEM["sT"], tS)
                        sc.wait_ge(SEM["dG"], dg)
                        qt_, ro_ = qkb[g]
                        sc.activation(out=qt_[ro_:ro_ + 64, 512 * jc:512 * jc + jn], in_=ps[0:64, 0:jn],
                                      func=AF.Copy).then_inc(SEM["sA"], 1)
                    SC(_); C["sA"] += 1
            qkA = C["sA"]

            for p2 in range(4):
                ss = [p2]
                Ws = [slice_W(c, s) for s in ss]
                active = [CHQ * c + NQS * s >= 48 for s in ss]
                for i, s in enumerate(ss):
                    def _(gp, i=i, s=s, qkA=qkA, dq=C["dQ"]):
                        gp.wait_ge(SEM["sA"], qkA)
                        gp.wait_ge(SEM["sV"], C["sV"])
                        gp.wait_ge(SEM["dQ"], dq)
                        for h in range(H):
                            qt_, ro_ = qkb[h // 2]
                            src = qt_[ro_ + 32 * (h % 2) + NQS * s:ro_ + 32 * (h % 2) + NQS * s + NQS, 0:Ws[i]]
                            gp.dma_start(out=slb[h:h + 113:16, SLW * s:SLW * s + Ws[i]], in_=src).then_inc(SEM["dG"], 16)
                    GP(_); C["dG"] += 16 * H
                    gS = C["dG"]
                    nj = cdiv(Ws[i], 512)
                    for jc in range(nj):
                        jn = min(512, Ws[i] - 512 * jc)
                        ps = [pC, pD][jc % 2]
                        def _(te, i=i, s=s, jc=jc, jn=jn, ps=ps, gS=gS):
                            te.wait_ge(SEM["dG"], gS)
                            te.wait_ge(SEM["sA"], C["sA"])
                            te.matmul(ps[:, 0:jn], lhsT=preM[:, :],
                                      rhs=slb[:, SLW * s + 512 * jc:SLW * s + 512 * jc + jn],
                                      start=True, stop=True).then_inc(SEM["sT"], 1)
                        TE(_); C["sT"] += 1
                        tS = C["sT"]
                        def _(sc, i=i, jc=jc, jn=jn, ps=ps, tS=tS):
                            sc.wait_ge(SEM["sT"], tS)
                            sc.wait_ge(SEM["sV"], C["sV"])
                            sc.activation(out=dsb[i][:, 512 * jc:512 * jc + jn], in_=ps[:, 0:jn],
                                          func=AF.Copy, bias=SHIFT)
                            sc.activation(out=d16[i][:, 512 * jc:512 * jc + jn], in_=ps[:, 0:jn],
                                          func=AF.Copy, bias=SHIFT).then_inc(SEM["sA"], 1)
                        SC(_); C["sA"] += 1
                preA = C["sA"]

                def _(v, Ws=Ws, preA=preA):
                    v.wait_ge(SEM["sA"], preA)
                    for i in range(len(ss)):
                        v.tensor_tensor(out=dsb[i][:, Ws[i] - 7:Ws[i]], in0=dsb[i][:, Ws[i] - 7:Ws[i]],
                                        in1=mask7[:, 0:7], op=ALU.mult)
                        v.tensor_tensor(out=d16[i][:, Ws[i] - 7:Ws[i]], in0=d16[i][:, Ws[i] - 7:Ws[i]],
                                        in1=mask7[:, 0:7], op=ALU.mult)
                    v.memset(st_lo[:, :], BIS_LO)
                    v.memset(st_w[:, :], (BIS_HI - BIS_LO) / 2)
                    v.memset(st_mid[:, :], (BIS_HI + BIS_LO) / 2).then_inc(SEM["sV"], 1)
                    C["sV"] += 1
                    v.wait_ge(SEM["sV"], C["sV"])
                    v.engine_nop().then_inc(SEM["sV"], 1)
                VE(_); C["sV"] += 1

                if debug and c == 16 and p2 == 0:
                    def _(sy, vCur=C["sV"]):
                        sy.wait_ge(SEM["sV"], vCur)
                        sy.dma_start(out=dbg["ds"], in_=dsb[0][:, :]).then_inc(SEM["dQ"], 16)
                    SY(_); C["dQ"] += 16
                if any(active):
                    for it in range(NIT_BF + NIT_F32):
                        kind = "bf" if it < NIT_BF else "f32"
                        def _(v, it=it, kind=kind, Ws=Ws, active=active, p2=p2):
                            for i in range(len(ss)):
                                if not active[i]:
                                    continue
                                src = d16[i] if kind == "bf" else dsb[i]
                                s = ss[i]
                                v.tensor_scalar(out=slb[:, SLW * s:SLW * s + Ws[i]],
                                                in0=src[:, 0:Ws[i]],
                                                scalar1=st_mid[:, i:i + 1], scalar2=0.0,
                                                op0=ALU.is_ge, op1=ALU.add,
                                                accum_out=st_cnt[:, i:i + 1]).then_inc(SEM["sV"], 1)
                            C["sV"] += 1
                            v.wait_ge(SEM["sV"], C["sV"])
                            v.scalar_tensor_tensor(out=st_u[:, :], in0=st_cnt[:, :], scalar=float(TOPK),
                                                   in1=st_w[:, :], op0=ALU.is_ge, op1=ALU.mult).then_inc(SEM["sV"], 1)
                            C["sV"] += 1
                            v.wait_ge(SEM["sV"], C["sV"])
                            v.tensor_tensor(out=st_lo[:, :], in0=st_lo[:, :], in1=st_u[:, :], op=ALU.add)
                            v.tensor_scalar(out=st_w[:, :], in0=st_w[:, :], scalar1=0.5, scalar2=None, op0=ALU.mult).then_inc(SEM["sV"], 1)
                            C["sV"] += 1
                            v.wait_ge(SEM["sV"], C["sV"])
                            v.tensor_tensor(out=st_mid[:, :], in0=st_lo[:, :], in1=st_w[:, :], op=ALU.add)
                            if it == NIT_BF - 1:
                                v.tensor_scalar(out=st_lo[:, :], in0=st_lo[:, :], scalar1=ULP16, scalar2=None, op0=ALU.subtract)
                                v.tensor_scalar(out=st_w[:, :], in0=st_w[:, :], scalar1=ULP16, scalar2=None, op0=ALU.add).then_inc(SEM["sV"], 1)
                                C["sV"] += 1
                                v.wait_ge(SEM["sV"], C["sV"])
                                v.tensor_tensor(out=st_mid[:, :], in0=st_lo[:, :], in1=st_w[:, :], op=ALU.add)
                            v.engine_nop().then_inc(SEM["sV"], 1)
                        VE(_); C["sV"] += 1
                        if debug and c == 16 and p2 == 0 and it in (0, 4, 8, 12, 16):
                            col = 4 + 4 * ((0, 4, 8, 12, 16).index(it))
                            def _(v):
                                v.tensor_copy(out=m8v[:, 0:1], in_=st_mid[:, 0:1])
                                v.tensor_copy(out=m8v[:, 1:2], in_=st_cnt[:, 0:1])
                                v.tensor_copy(out=m8v[:, 2:3], in_=st_lo[:, 0:1])
                                v.tensor_copy(out=m8v[:, 3:4], in_=st_w[:, 0:1]).then_inc(SEM["sV"], 1)
                            VE(_); C["sV"] += 1
                            def _(sy, vCur=C["sV"], col=col):
                                sy.wait_ge(SEM["sV"], vCur)
                                sy.dma_start(out=dbg["vk"][0:128, col:col + 4], in_=m8v[:, 0:4]).then_inc(SEM["dQ"], 16)
                            SY(_); C["dQ"] += 16

                def _(v, Ws=Ws, active=active, ss=ss, aCur=C["sA"]):
                    v.wait_ge(SEM["sA"], aCur)
                    v.tensor_tensor(out=st_hi[:, :], in0=st_w[:, :], in1=st_w[:, :], op=ALU.add).then_inc(SEM["sV"], 1)
                    C["sV"] += 1
                    v.wait_ge(SEM["sV"], C["sV"])
                    v.tensor_tensor(out=st_hi[:, :], in0=st_hi[:, :], in1=st_lo[:, :], op=ALU.add).then_inc(SEM["sV"], 1)
                    C["sV"] += 1
                    v.wait_ge(SEM["sV"], C["sV"])
                    for i in range(len(ss)):
                        s = ss[i]
                        if not active[i]:
                            v.memset(vk_t[:, i:i + 1], VK_INACT)
                            continue
                        v.tensor_scalar(out=slb[:, SLW * s:SLW * s + Ws[i]], in0=dsb[i][:, 0:Ws[i]],
                                        scalar1=st_hi[:, i:i + 1], scalar2=0.0,
                                        op0=ALU.is_ge, op1=ALU.add, accum_out=chi_t[:, i:i + 1]).then_inc(SEM["sV"], 1)
                        C["sV"] += 1
                        v.wait_ge(SEM["sV"], C["sV"])
                        v.scalar_tensor_tensor(out=wk32[:, 0:Ws[i]], in0=dsb[i][:, 0:Ws[i]],
                                               scalar=st_hi[:, i:i + 1], in1=dsb[i][:, 0:Ws[i]],
                                               op0=ALU.is_lt, op1=ALU.mult)
                        v.max(out=m8v[:, :], in_=wk32[:, 0:Ws[i]]).then_inc(SEM["sV"], 1)
                        C["sV"] += 1
                        v.wait_ge(SEM["sV"], C["sV"])
                        v.tensor_scalar(out=chi_t[:, i:i + 1], in0=chi_t[:, i:i + 1], scalar1=-1.0,
                                        scalar2=float(TOPK - 1), op0=ALU.mult, op1=ALU.add).then_inc(SEM["sV"], 1)
                        C["sV"] += 1
                        v.wait_ge(SEM["sV"], C["sV"])
                        v.tensor_scalar(out=chi_t[:, i:i + 1], in0=chi_t[:, i:i + 1], scalar1=0.0,
                                        scalar2=7.0, op0=ALU.max, op1=ALU.min).then_inc(SEM["sV"], 1)
                        C["sV"] += 1
                        v.wait_ge(SEM["sV"], C["sV"])
                        v.tensor_scalar(out=sel8[:, :], in0=idx8[:, :], scalar1=chi_t[:, i:i + 1],
                                        scalar2=None, op0=ALU.is_equal).then_inc(SEM["sV"], 1)
                        C["sV"] += 1
                        v.wait_ge(SEM["sV"], C["sV"])
                        v.tensor_tensor(out=sel8[:, :], in0=sel8[:, :], in1=m8v[:, :], op=ALU.mult).then_inc(SEM["sV"], 1)
                        C["sV"] += 1
                        v.wait_ge(SEM["sV"], C["sV"])
                        v.reduce_sum(out=vk_t[:, i:i + 1], in_=sel8[:, :], axis=AX.X)
                    v.engine_nop().then_inc(SEM["sV"], 1)
                    C["sV"] += 1
                    v.wait_ge(SEM["sV"], C["sV"])
                    v.tensor_scalar(out=nvk[:, :], in0=vk_t[:, :], scalar1=-1.0, scalar2=None, op0=ALU.mult)
                    v.engine_nop().then_inc(SEM["sV"], 1)
                VE(_); C["sV"] += 1
                vkS = C["sV"]
                if debug and c == 16 and p2 == 0:
                    def _(v, vkS=vkS):
                        v.wait_ge(SEM["sV"], vkS)
                        v.tensor_copy(out=sel8[:, 0:1], in_=vk_t[:, 0:1])
                        v.tensor_copy(out=sel8[:, 1:2], in_=st_hi[:, 0:1])
                        v.tensor_copy(out=sel8[:, 2:3], in_=chi_t[:, 0:1])
                        v.tensor_copy(out=sel8[:, 3:4], in_=m8v[:, 7:8]).then_inc(SEM["sV"], 1)
                    VE(_); C["sV"] += 1
                    def _(sy, vCur=C["sV"]):
                        sy.wait_ge(SEM["sV"], vCur)
                        sy.dma_start(out=dbg["vk"][0:128, 0:4], in_=sel8[:, 0:4]).then_inc(SEM["dQ"], 16)
                    SY(_); C["dQ"] += 16

                for i, s in enumerate(ss):
                    def _(sc, i=i, s=s, Ws=Ws, vkS=vkS):
                        sc.wait_ge(SEM["sV"], vkS)
                        sc.activation(out=d16[i][:, 0:Ws[i]], in_=dsb[i][:, 0:Ws[i]], func=AF.Exp,
                                      bias=nvk[:, i:i + 1], scale=1.0).then_inc(SEM["sA"], 1)
                    SC(_); C["sA"] += 1
                expA = C["sA"]
                def _(v, Ws=Ws, ss=ss, expA=expA):
                    v.wait_ge(SEM["sA"], expA)
                    for i, s in enumerate(ss):
                        v.scalar_tensor_tensor(out=slb[:, SLW * s:SLW * s + Ws[i]], in0=dsb[i][:, 0:Ws[i]],
                                               scalar=vk_t[:, i:i + 1], in1=d16[i][:, 0:Ws[i]],
                                               op0=ALU.is_ge, op1=ALU.mult, accum_out=zz[:, i:i + 1]).then_inc(SEM["sV"], 1)
                    C["sV"] += 1
                    v.wait_ge(SEM["sV"], C["sV"])
                    v.reciprocal(out=zr[:, :], in_=zz[:, :]).then_inc(SEM["sV"], 1)
                    C["sV"] += 1
                    v.wait_ge(SEM["sV"], C["sV"])
                    for i, s in enumerate(ss):
                        v.tensor_scalar(out=slb[:, SLW * s:SLW * s + Ws[i]],
                                        in0=slb[:, SLW * s:SLW * s + Ws[i]],
                                        scalar1=zr[:, i:i + 1], scalar2=None, op0=ALU.mult)
                    v.engine_nop().then_inc(SEM["sV"], 1)
                VE(_); C["sV"] += 1
                nS = C["sV"]
                if debug and c == 16 and p2 == 0:
                    def _(v, nS=nS):
                        v.wait_ge(SEM["sV"], nS)
                        v.tensor_copy(out=wk32[:, 0:SLW] if False else wk32[:, 0:WMAX], in_=slb[:, 0:WMAX]).then_inc(SEM["sV"], 1)
                    VE(_); C["sV"] += 1
                    def _(sy, vCur=C["sV"]):
                        sy.wait_ge(SEM["sV"], vCur)
                        sy.dma_start(out=dbg["at"], in_=wk32[:, 0:WMAX]).then_inc(SEM["dQ"], 16)
                    SY(_); C["dQ"] += 16

                for i, s in enumerate(ss):
                    nj = cdiv(Ws[i], 512)
                    for jc in range(nj):
                        jn = min(512, Ws[i] - 512 * jc)
                        ps = [pC, pD][jc % 2]
                        def _(te, s=s, jc=jc, jn=jn, ps=ps, nS=nS):
                            te.wait_ge(SEM["sV"], nS)
                            te.wait_ge(SEM["sA"], C["sA"])
                            te.matmul(ps[:, 0:jn], lhsT=postM[:, :],
                                      rhs=slb[:, SLW * s + 512 * jc:SLW * s + 512 * jc + jn],
                                      start=True, stop=True).then_inc(SEM["sT"], 1)
                        TE(_); C["sT"] += 1
                        tS = C["sT"]
                        def _(sc, s=s, jc=jc, jn=jn, ps=ps, tS=tS):
                            sc.wait_ge(SEM["sT"], tS)
                            sc.activation(out=slb[:, SLW * s + 512 * jc:SLW * s + 512 * jc + jn],
                                          in_=ps[:, 0:jn], func=AF.Copy).then_inc(SEM["sA"], 1)
                        SC(_); C["sA"] += 1
                pmA = C["sA"]

                for i, s in enumerate(ss):
                    njt = cdiv(Wc, 128)
                    def _(sy, i=i, s=s, njt=njt, pmA=pmA):
                        sy.wait_ge(SEM["sA"], pmA)
                        sy.wait_ge(SEM["sT"], C["sT"])
                        btv = bt[:, :].rearrange("p (jt s q) -> p jt s q", s=4, q=128)
                        for jt in range(njt):
                            sy.dma_start(out=btv[:, jt, s, :],
                                         in_=slb[:, SLW * s + 128 * jt:SLW * s + 128 * (jt + 1)],
                                         transpose=True).then_inc(SEM["dQ"], 16)
                    SY(_); C["dQ"] += 16 * njt

            trQ = C["dQ"]
            njt = cdiv(Wc, 128)
            for k in range(H):
                def _(te, k=k, njt=njt, trQ=trQ, c=c):
                    te.wait_ge(SEM["dQ"], trQ)
                    te.wait_ge(SEM["sA"], C["sA"])
                    btv = bt[:, :].rearrange("p (jt s q) -> p jt s q", s=4, q=128)
                    for jt in range(njt):
                        kk = min(128, Wc - 128 * jt)
                        mm = te.matmul(pV[0:64, 0:32], lhsT=VT[jt][0:kk, 64 * k:64 * (k + 1)],
                                       rhs=btv[0:kk, jt, :, 8 * k:8 * k + 8],
                                       start=(jt == 0), stop=(jt == njt - 1))
                    mm.then_inc(SEM["sT"], 1)
                TE(_); C["sT"] += 1
                tS = C["sT"]
                def _(sc, k=k, c=c, tS=tS):
                    sc.wait_ge(SEM["sT"], tS)
                    sc.activation(out=ctx_t[k // 2][64 * (k % 2):64 * (k % 2) + 64, 32 * (c % 4):32 * (c % 4) + 32],
                                  in_=pV[0:64, 0:32], func=AF.Copy).then_inc(SEM["sA"], 1)
                SC(_); C["sA"] += 1

            if (c + 1) % 4 == 0:
                tile = c // 4
                r0 = 128 * tile
                ctxA = C["sA"]
                def _(te, ctxA=ctxA):
                    te.wait_ge(SEM["sA"], ctxA)
                    for half in range(2):
                        for p in range(8):
                            te.matmul(pW[:, 512 * half:512 * (half + 1)], lhsT=ctx_t[p][:, :],
                                      rhs=WB[p][:, 512 * half:512 * (half + 1)], start=(p == 0), stop=False)
                        mm = te.matmul(pW[:, 512 * half:512 * (half + 1)], lhsT=ones1[:, :],
                                       rhs=bo_t[:, 512 * half:512 * (half + 1)], start=False, stop=True)
                    mm.then_inc(SEM["sT"], 1)
                TE(_); C["sT"] += 1
                tS = C["sT"]
                def _(sy, r0=r0):
                    sy.wait_ge(SEM["sV"], C["sV"])
                    sy.dma_start(out=xres[:, :], in_=hx[r0:r0 + 128, :]).then_inc(SEM["dQ"], 16)
                SY(_); C["dQ"] += 16
                xS = C["dQ"]
                def _(v, tS=tS, xS=xS):
                    v.wait_ge(SEM["sT"], tS)
                    v.wait_ge(SEM["dQ"], xS)
                    v.tensor_tensor(out=xres[:, :], in0=pW[:, :], in1=xres[:, :], op=ALU.add)
                    v.reduce_sum(out=mu_t[:, :], in_=xres[:, :], axis=AX.X).then_inc(SEM["sV"], 1)
                    C["sV"] += 1
                    v.wait_ge(SEM["sV"], C["sV"])
                    v.tensor_scalar(out=mu_t[:, :], in0=mu_t[:, :], scalar1=1.0 / D, scalar2=None, op0=ALU.mult).then_inc(SEM["sV"], 1)
                    C["sV"] += 1
                    v.wait_ge(SEM["sV"], C["sV"])
                    v.tensor_scalar(out=sq_t[:, :], in0=xres[:, :], scalar1=mu_t[:, :], scalar2=None,
                                    op0=ALU.subtract).then_inc(SEM["sV"], 1)
                VE(_); C["sV"] += 1
                vS = C["sV"]
                if debug and tile == 0:
                    def _(sy, vS=vS):
                        sy.wait_ge(SEM["sV"], vS)
                        sy.dma_start(out=dbg["y"], in_=xres[:, :]).then_inc(SEM["dQ"], 16)
                    SY(_); C["dQ"] += 16
                    def _(v):
                        v.tensor_copy(out=wk32[:, 0:2048], in_=bt[:, 0:2048]).then_inc(SEM["sV"], 1)
                    VE(_); C["sV"] += 1
                    def _(sy, vCur=C["sV"]):
                        sy.wait_ge(SEM["sV"], vCur)
                        sy.dma_start(out=dbg["bt"][0:128, 0:2048], in_=wk32[:, 0:2048]).then_inc(SEM["dQ"], 16)
                    SY(_); C["dQ"] += 16
                def _(sc, vS=vS):
                    sc.wait_ge(SEM["sV"], vS)
                    sc.activation(out=sq_t[:, :], in_=sq_t[:, :], func=AF.Square, accum_out=var_t[:, :]).then_inc(SEM["sA"], 1)
                SC(_); C["sA"] += 1
                aS = C["sA"]
                def _(sc, aS=aS):
                    sc.wait_ge(SEM["sA"], aS)
                    sc.activation(out=rs_t[:, :], in_=var_t[:, :], func=AF.Sqrt,
                                  scale=scD_t[:, :], bias=eps_t[:, :]).then_inc(SEM["sA"], 1)
                SC(_); C["sA"] += 1
                aS = C["sA"]
                def _(v, aS=aS, r0=r0, tile=tile):
                    v.wait_ge(SEM["sA"], aS)
                    if debug and tile == 1:
                        v.tensor_copy(out=m8v[:, 0:1], in_=mu_t[:, :])
                        v.tensor_copy(out=m8v[:, 1:2], in_=var_t[:, :])
                        v.tensor_copy(out=m8v[:, 2:3], in_=rs_t[:, :]).then_inc(SEM["sV"], 1)
                        C["sV"] += 1
                        v.wait_ge(SEM["sV"], C["sV"])
                    v.reciprocal(out=rs_t[:, :], in_=rs_t[:, :]).then_inc(SEM["sV"], 1)
                    C["sV"] += 1
                    v.wait_ge(SEM["sV"], C["sV"])
                    v.tensor_scalar(out=xres[:, :], in0=xres[:, :], scalar1=mu_t[:, :], scalar2=rs_t[:, :],
                                    op0=ALU.subtract, op1=ALU.mult).then_inc(SEM["sV"], 1)
                VE(_); C["sV"] += 1
                if debug and tile == 1:
                    def _(sy, vCur=C["sV"]):
                        sy.wait_ge(SEM["sV"], vCur)
                        sy.dma_start(out=dbg["vk"][0:128, 20:23], in_=m8v[:, 0:3]).then_inc(SEM["dQ"], 16)
                    SY(_); C["dQ"] += 16
                vS = C["sV"]
                def _(sy, vS=vS, r0=r0):
                    sy.wait_ge(SEM["sV"], vS)
                    sy.dma_start(out=ox1[r0:r0 + 128, :], in_=xres[:, :]).then_inc(SEM["dQ"], 16)
                SY(_); C["dQ"] += 16

                def _(v, vS=vS):
                    v.wait_ge(SEM["sV"], vS)
                    v.reduce_sum(out=mu_t[:, :], in_=xres[:, :], axis=AX.X).then_inc(SEM["sV"], 1)
                    C["sV"] += 1
                    v.wait_ge(SEM["sV"], C["sV"])
                    v.tensor_scalar(out=mu_t[:, :], in0=mu_t[:, :], scalar1=1.0 / D, scalar2=None, op0=ALU.mult).then_inc(SEM["sV"], 1)
                    C["sV"] += 1
                    v.wait_ge(SEM["sV"], C["sV"])
                    v.tensor_scalar(out=sq_t[:, :], in0=xres[:, :], scalar1=mu_t[:, :], scalar2=None,
                                    op0=ALU.subtract).then_inc(SEM["sV"], 1)
                VE(_); C["sV"] += 1
                vS = C["sV"]
                def _(sc, vS=vS):
                    sc.wait_ge(SEM["sV"], vS)
                    sc.activation(out=d16[0][:, 0:D], in_=sq_t[:, :], func=AF.Square, accum_out=var_t[:, :]).then_inc(SEM["sA"], 1)
                SC(_); C["sA"] += 1
                aS = C["sA"]
                def _(sc, aS=aS):
                    sc.wait_ge(SEM["sA"], aS)
                    sc.activation(out=rs_t[:, :], in_=var_t[:, :], func=AF.Sqrt, scale=scD_t[:, :],
                                  bias=eps_t[:, :]).then_inc(SEM["sA"], 1)
                SC(_); C["sA"] += 1
                aS = C["sA"]
                def _(v, aS=aS):
                    v.wait_ge(SEM["sA"], aS)
                    v.reciprocal(out=rs_t[:, :], in_=rs_t[:, :]).then_inc(SEM["sV"], 1)
                    C["sV"] += 1
                    v.wait_ge(SEM["sV"], C["sV"])
                    v.tensor_scalar(out=sq_t[:, :], in0=sq_t[:, :], scalar1=rs_t[:, :], scalar2=None, op0=ALU.mult).then_inc(SEM["sV"], 1)
                    C["sV"] += 1
                    v.wait_ge(SEM["sV"], C["sV"])
                    v.tensor_tensor(out=sq_t[:, :], in0=sq_t[:, :], in1=gr[:, :], op=ALU.mult).then_inc(SEM["sV"], 1)
                    C["sV"] += 1
                    v.wait_ge(SEM["sV"], C["sV"])
                    v.tensor_tensor(out=sq_t[:, :], in0=sq_t[:, :], in1=br[:, :], op=ALU.add).then_inc(SEM["sV"], 1)
                VE(_); C["sV"] += 1
                vS = C["sV"]
                def _(sy, vS=vS, r0=r0):
                    sy.wait_ge(SEM["sV"], vS)
                    sy.dma_start(out=oh[r0:r0 + 128, :], in_=sq_t[:, :]).then_inc(SEM["dS"], 16)
                SY(_); C["dS"] += 16
                def _(sc, vS=vS, dsc=C["dS"]):
                    sc.wait_ge(SEM["sV"], vS)
                    sc.wait_ge(SEM["dS"], dsc)
                    sc.activation(out=sq_t[:, :], in_=sq_t[:, :], func=AF.Abs).then_inc(SEM["sA"], 1)
                SC(_); C["sA"] += 1
                aS = C["sA"]
                def _(sy, aS=aS, r0=r0):
                    sy.wait_ge(SEM["sA"], aS)
                    sy.dma_start(out=ohabs[r0:r0 + 128, :], in_=sq_t[:, :]).then_inc(SEM["dQ"], 16)
                SY(_); C["dQ"] += 16
                def _(v, aS=aS, tile=tile):
                    v.wait_ge(SEM["sA"], aS)
                    for bn in range(NBIN):
                        v.tensor_scalar(out=d16[0][:, 0:D], in0=sq_t[:, :], scalar1=QCAP1 * bn / NBIN, scalar2=0.0,
                                        op0=ALU.is_ge, op1=ALU.add,
                                        accum_out=qcnt[:, tile * NBIN + bn:tile * NBIN + bn + 1])
                    v.engine_nop().then_inc(SEM["sV"], 1)
                VE(_); C["sV"] += 1

        def _(v):
            v.wait_ge(SEM["sV"], C["sV"])
            v.reduce_sum(out=qred[:, :], in_=qcnt[:, :].rearrange("p (t b) -> p b t", b=NBIN), axis=AX.X).then_inc(SEM["sV"], 1)
        VE(_); C["sV"] += 1
        vS = C["sV"]
        onescol = sb("onescol", [128, 1], F32)
        def _(v):
            v.memset(onescol[:, :], 1.0).then_inc(SEM["sV"], 1)
        VE(_); C["sV"] += 1
        vS = C["sV"]
        def _(te, vS=vS):
            te.wait_ge(SEM["sV"], vS)
            te.matmul(pV[0:1, 0:NBIN], lhsT=onescol[:, :], rhs=qred[:, :], start=True, stop=True).then_inc(SEM["sT"], 1)
        TE(_); C["sT"] += 1
        tS = C["sT"]
        def _(v, tS=tS):
            v.wait_ge(SEM["sT"], tS)
            v.tensor_copy(out=qfin[:, :], in_=pV[0:1, 0:NBIN]).then_inc(SEM["sV"], 1)
        VE(_); C["sV"] += 1
        vS = C["sV"]

        if debug:
            pass

        def _(sy, vS=vS):
            sy.wait_ge(SEM["sV"], vS)
            sy.dma_start(out=ocnt, in_=qfin[:, :]).then_inc(SEM["dQ"], 16)
            sy.wait_ge(SEM["dQ"], C["dQ"] + 16)
        SY(_); C["dQ"] += 16

    return nc


def host_inputs_LA(inp, c):
    x = np.ascontiguousarray(np.asarray(inp["L0_attn_x"][c], dtype=np.float32))
    xT = np.ascontiguousarray(x.T)
    Wq = np.ascontiguousarray(np.asarray(inp["L0_attn_Wq"], np.float32).reshape(8, 128, INNER))
    Wk = np.ascontiguousarray(np.asarray(inp["L0_attn_Wk"], np.float32).reshape(8, 128, INNER))
    Wv = np.ascontiguousarray(np.asarray(inp["L0_attn_Wv"], np.float32).reshape(8, 128, INNER))
    Wo = np.ascontiguousarray(np.asarray(inp["L0_attn_Wo"], np.float32).reshape(8, 128, D))
    bo = np.asarray(inp["L0_attn_bo"], np.float32).reshape(1, D)
    pre = np.asarray(inp["L0_attn_pre_proj"], np.float32)
    post = np.asarray(inp["L0_attn_post_proj"], np.float32)
    preM = np.zeros((128, 128), np.float32)
    postM = np.zeros((128, 128), np.float32)
    for q in range(8):
        preM[16 * q:16 * q + 16, 16 * q:16 * q + 16] = pre
        for k in range(16):
            for k2 in range(16):
                postM[16 * q + k, 8 * k2 + q] = post[k, k2]
    mem_k = np.asarray(inp["L0_attn_mem_k"], np.float32)
    mem_v = np.asarray(inp["L0_attn_mem_v"], np.float32)
    memkP = np.zeros((128, 8 * MEM), np.float32)
    for g in range(8):
        for hh in range(2):
            memkP[hh * 64:hh * 64 + 64, g * MEM:(g + 1) * MEM] = mem_k[2 * g + hh].T
    memvT = np.ascontiguousarray(mem_v.transpose(1, 0, 2).reshape(MEM, INNER))
    mask7 = np.zeros((128, 8), np.float32)
    for p in range(128):
        for cc in range(7):
            mask7[p, cc] = 1.0 if (p >> 4) > cc else 0.0
    lng = np.asarray(inp["L0_ffn_ln_g"], np.float32).reshape(1, D)
    lnb = np.asarray(inp["L0_ffn_ln_b"], np.float32).reshape(1, D)
    return {"x": x, "xT": xT, "Wq": Wq, "Wk": Wk, "Wv": Wv, "Wo": Wo, "bo": bo,
            "preM": preM, "postM": postM, "memkP": memkP, "memvT": memvT,
            "mask7": mask7, "lng": lng, "lnb": lnb}


NBR = 32


def build_round(R, CW, NB_in, khi):
    nc = bass.Bass("TRN2", target_bir_lowering=False, debug=False)
    hsrc = nc.dram_tensor("src", [R, CW], F32, kind="ExternalInput").ap()
    hcnt = nc.dram_tensor("cin", [8, NB_in], F32, kind="ExternalInput").ap()
    hbrk = nc.dram_tensor("bin", [1, 2], F32, kind="ExternalInput").ap()
    ocnt = nc.dram_tensor("cout", [1, NBR], F32, kind="ExternalOutput").ap()
    obrk = nc.dram_tensor("bout", [1, 2], F32, kind="ExternalOutput").ap()
    ntile = R // 128
    with ExitStack() as ctx:
        def sb(name, shape, dtype=F32):
            return ctx.enter_context(nc.sbuf_tensor(name, shape, dtype))
        stg = [sb(f"stg{i}", [128, CW], F32) for i in range(2)]
        ab = [sb(f"ab{i}", [128, CW], F32) for i in range(2)]
        scr = sb("scr", [128, CW], F32)
        ct = sb("ct", [8, NB_in])
        bk = sb("bk", [1, 2])
        csum = sb("csum", [1, NB_in])
        ind = sb("ind", [1, NB_in])
        lo_t = sb("lo_t", [1, 1])
        hi_t = sb("hi_t", [1, 1])
        st_t = sb("st_t", [1, 1])
        t1 = sb("t1", [1, 1])
        ones8 = sb("ones8", [8, 1])
        onesc = sb("onesc", [1, 128])
        onescol = sb("onescol", [128, 1])
        thr = sb("thr", [128, NBR])
        qc = sb("qc", [128, NBR])
        qr2 = sb("qr2", [128, NBR])
        qr = sb("qr", [1, NBR])
        pS = ctx.enter_context(nc.psum_tensor("pS", [128, 512], F32))
        SEM = {}
        for k in ("dQ", "sT", "sV", "sA"):
            SEM[k] = ctx.enter_context(nc.semaphore(k))
        C = {k: 0 for k in SEM}
        block = ctx.enter_context(nc.Block())
        SY, SC, VE, TE = block.sync, block.scalar, block.vector, block.tensor

        def vbrk(v):
            C["sV"] += 1
            v.wait_ge(SEM["sV"], C["sV"])

        def _(sy):
            sy.dma_start(out=ct[:, :], in_=hcnt).then_inc(SEM["dQ"], 16)
            sy.dma_start(out=bk[:, :], in_=hbrk).then_inc(SEM["dQ"], 16)
        SY(_); C["dQ"] += 32

        def _(v):
            v.wait_ge(SEM["dQ"], C["dQ"])
            v.memset(ones8[:, :], 1.0)
            v.memset(onesc[:, :], 1.0)
            v.memset(onescol[:, :], 1.0).then_inc(SEM["sV"], 1)
        VE(_); C["sV"] += 1
        def _(te):
            te.wait_ge(SEM["sV"], C["sV"])
            te.matmul(pS[0:1, 0:NB_in], lhsT=ones8[:, :], rhs=ct[:, :], start=True, stop=True).then_inc(SEM["sT"], 1)
        TE(_); C["sT"] += 1
        def _(v, khi=khi):
            v.wait_ge(SEM["sT"], C["sT"])
            v.tensor_copy(out=csum[:, :], in_=pS[0:1, 0:NB_in]).then_inc(SEM["sV"], 1)
            vbrk(v)
            v.tensor_scalar(out=ind[:, :], in0=csum[:, :], scalar1=float(khi) + 0.5, scalar2=None,
                            op0=ALU.is_ge).then_inc(SEM["sV"], 1)
            vbrk(v)
            v.reduce_sum(out=t1[:, :], in_=ind[:, :], axis=AX.X).then_inc(SEM["sV"], 1)
            vbrk(v)
            v.tensor_scalar(out=t1[:, :], in0=t1[:, :], scalar1=-2.0, scalar2=0.0, op0=ALU.add, op1=ALU.max).then_inc(SEM["sV"], 1)
            vbrk(v)
            v.tensor_tensor(out=lo_t[:, :], in0=t1[:, :], in1=bk[0:1, 1:2], op=ALU.mult).then_inc(SEM["sV"], 1)
            vbrk(v)
            v.tensor_tensor(out=lo_t[:, :], in0=lo_t[:, :], in1=bk[0:1, 0:1], op=ALU.add).then_inc(SEM["sV"], 1)
            vbrk(v)
            v.tensor_scalar(out=ind[:, :], in0=csum[:, :], scalar1=float(khi) - 0.5, scalar2=None,
                            op0=ALU.is_ge).then_inc(SEM["sV"], 1)
            vbrk(v)
            v.reduce_sum(out=t1[:, :], in_=ind[:, :], axis=AX.X).then_inc(SEM["sV"], 1)
            vbrk(v)
            v.tensor_scalar(out=t1[:, :], in0=t1[:, :], scalar1=1.0, scalar2=None, op0=ALU.add).then_inc(SEM["sV"], 1)
            vbrk(v)
            v.tensor_tensor(out=hi_t[:, :], in0=t1[:, :], in1=bk[0:1, 1:2], op=ALU.mult).then_inc(SEM["sV"], 1)
            vbrk(v)
            v.tensor_tensor(out=hi_t[:, :], in0=hi_t[:, :], in1=bk[0:1, 0:1], op=ALU.add).then_inc(SEM["sV"], 1)
            vbrk(v)
            v.tensor_tensor(out=st_t[:, :], in0=hi_t[:, :], in1=lo_t[:, :], op=ALU.subtract).then_inc(SEM["sV"], 1)
            vbrk(v)
            v.tensor_scalar(out=st_t[:, :], in0=st_t[:, :], scalar1=1.0 / NBR, scalar2=None, op0=ALU.mult).then_inc(SEM["sV"], 1)
            vbrk(v)
            for b in range(NBR):
                v.tensor_scalar(out=qr[0:1, b:b + 1], in0=st_t[:, :], scalar1=float(b), scalar2=None, op0=ALU.mult)
            v.engine_nop().then_inc(SEM["sV"], 1)
            vbrk(v)
            v.tensor_scalar(out=qr[:, :], in0=qr[:, :], scalar1=lo_t[0:1, 0:1], scalar2=None, op0=ALU.add).then_inc(SEM["sV"], 1)
        VE(_); C["sV"] += 1
        def _(te):
            te.wait_ge(SEM["sV"], C["sV"])
            te.matmul(pS[:, 0:NBR], lhsT=onesc[:, :], rhs=qr[:, :], start=True, stop=True).then_inc(SEM["sT"], 1)
        TE(_); C["sT"] += 1
        def _(v):
            v.wait_ge(SEM["sT"], C["sT"])
            v.tensor_copy(out=thr[:, :], in_=pS[:, 0:NBR]).then_inc(SEM["sV"], 1)
        VE(_); C["sV"] += 1
        thrS = C["sV"]

        for t in range(ntile):
            def _(sy, t=t):
                if t >= 2:
                    sy.wait_ge(SEM["sA"], C["sA"])
                sy.dma_start(out=stg[t % 2][:, :], in_=hsrc[128 * t:128 * (t + 1), :]).then_inc(SEM["dQ"], 16)
            SY(_); C["dQ"] += 16
            dq = C["dQ"]
            def _(sc, t=t, dq=dq):
                sc.wait_ge(SEM["dQ"], dq)
                sc.wait_ge(SEM["sV"], C["sV"])
                sc.activation(out=ab[t % 2][:, :], in_=stg[t % 2][:, :], func=AF.Abs).then_inc(SEM["sA"], 1)
            SC(_); C["sA"] += 1
            aS = C["sA"]
            def _(v, t=t, aS=aS, thrS=thrS):
                v.wait_ge(SEM["sA"], aS)
                v.wait_ge(SEM["sV"], C["sV"])
                for b in range(NBR):
                    v.tensor_scalar(out=scr[:, :], in0=ab[t % 2][:, :], scalar1=thr[:, b:b + 1], scalar2=0.0,
                                    op0=ALU.is_ge, op1=ALU.add,
                                    accum_out=(qc if t == 0 else qr2)[:, b:b + 1])
                v.engine_nop().then_inc(SEM["sV"], 1)
                vbrk(v)
                if t > 0:
                    v.tensor_tensor(out=qc[:, :], in0=qc[:, :], in1=qr2[:, :], op=ALU.add).then_inc(SEM["sV"], 1)
                    C["sV"] += 1
                    v.wait_ge(SEM["sV"], C["sV"])
                v.engine_nop().then_inc(SEM["sV"], 1)
            VE(_); C["sV"] += 1

        def _(te):
            te.wait_ge(SEM["sV"], C["sV"])
            te.matmul(pS[0:1, 0:NBR], lhsT=onescol[:, :], rhs=qc[:, :], start=True, stop=True).then_inc(SEM["sT"], 1)
        TE(_); C["sT"] += 1
        def _(v):
            v.wait_ge(SEM["sT"], C["sT"])
            v.tensor_copy(out=qr[:, :], in_=pS[0:1, 0:NBR])
            v.tensor_copy(out=bk[0:1, 0:1], in_=lo_t[:, :])
            v.tensor_copy(out=bk[0:1, 1:2], in_=st_t[:, :]).then_inc(SEM["sV"], 1)
        VE(_); C["sV"] += 1
        def _(sy):
            sy.wait_ge(SEM["sV"], C["sV"])
            sy.dma_start(out=ocnt, in_=qr[:, :]).then_inc(SEM["dQ"], 16)
            sy.dma_start(out=obrk, in_=bk[:, :]).then_inc(SEM["dQ"], 16)
            sy.wait_ge(SEM["dQ"], C["dQ"] + 32)
        SY(_); C["dQ"] += 32
    return nc


def build_extract(R, CW, NB_in, khi):
    nc = bass.Bass("TRN2", target_bir_lowering=False, debug=False)
    hsrc = nc.dram_tensor("src", [R, CW], F32, kind="ExternalInput").ap()
    hcnt = nc.dram_tensor("cin", [8, NB_in], F32, kind="ExternalInput").ap()
    hbrk = nc.dram_tensor("bin", [1, 2], F32, kind="ExternalInput").ap()
    ocand = nc.dram_tensor("cand", [128, 64], F32, kind="ExternalOutput").ap()
    ometa = nc.dram_tensor("meta", [1, 4], F32, kind="ExternalOutput").ap()
    ntile = R // 128
    SLOTS = 8 * KEXT
    with ExitStack() as ctx:
        def sb(name, shape, dtype=F32):
            return ctx.enter_context(nc.sbuf_tensor(name, shape, dtype))
        stg = [sb(f"stg{i}", [128, CW], F32) for i in range(2)]
        ab = [sb(f"ab{i}", [128, CW], F32) for i in range(2)]
        scr = sb("scr", [128, CW])
        stash = sb("stash", [128, ntile * SLOTS])
        ct = sb("ct", [8, NB_in])
        bk = sb("bk", [1, 2])
        csum = sb("csum", [1, NB_in])
        ind = sb("ind", [1, NB_in])
        lo_t = sb("lo_t", [1, 1]); hi_t = sb("hi_t", [1, 1]); t1 = sb("t1", [1, 1])
        lob = sb("lob", [128, 1]); hib = sb("hib", [128, 1])
        ones8 = sb("ones8", [8, 1]); onesc = sb("onesc", [1, 128]); onescol = sb("onescol", [128, 1])
        cand = sb("candt", [128, 64])
        chl = sb("chl", [128, 1])
        chacc = sb("chacc", [128, 1])
        meta = sb("metat", [1, 4])
        pS = ctx.enter_context(nc.psum_tensor("pS", [128, 128], F32))
        SEM = {}
        for k in ("dQ", "sT", "sV", "sA"):
            SEM[k] = ctx.enter_context(nc.semaphore(k))
        C = {k: 0 for k in SEM}
        block = ctx.enter_context(nc.Block())
        SY, SC, VE, TE = block.sync, block.scalar, block.vector, block.tensor

        def vbrk(v):
            C["sV"] += 1
            v.wait_ge(SEM["sV"], C["sV"])

        def _(sy):
            sy.dma_start(out=ct[:, :], in_=hcnt).then_inc(SEM["dQ"], 16)
            sy.dma_start(out=bk[:, :], in_=hbrk).then_inc(SEM["dQ"], 16)
        SY(_); C["dQ"] += 32
        def _(v):
            v.wait_ge(SEM["dQ"], C["dQ"])
            v.memset(ones8[:, :], 1.0)
            v.memset(onesc[:, :], 1.0)
            v.memset(onescol[:, :], 1.0)
            v.memset(stash[:, :], 0.0)
            v.memset(chacc[:, :], 0.0).then_inc(SEM["sV"], 1)
        VE(_); C["sV"] += 1
        def _(te):
            te.wait_ge(SEM["sV"], C["sV"])
            te.matmul(pS[0:1, 0:NB_in], lhsT=ones8[:, :], rhs=ct[:, :], start=True, stop=True).then_inc(SEM["sT"], 1)
        TE(_); C["sT"] += 1
        def _(v, khi=khi):
            v.wait_ge(SEM["sT"], C["sT"])
            v.tensor_copy(out=csum[:, :], in_=pS[0:1, 0:NB_in]).then_inc(SEM["sV"], 1)
            vbrk(v)
            v.tensor_scalar(out=ind[:, :], in0=csum[:, :], scalar1=float(khi) + 0.5, scalar2=None, op0=ALU.is_ge).then_inc(SEM["sV"], 1)
            vbrk(v)
            v.reduce_sum(out=t1[:, :], in_=ind[:, :], axis=AX.X).then_inc(SEM["sV"], 1)
            vbrk(v)
            v.tensor_scalar(out=t1[:, :], in0=t1[:, :], scalar1=-2.0, scalar2=0.0, op0=ALU.add, op1=ALU.max).then_inc(SEM["sV"], 1)
            vbrk(v)
            v.tensor_tensor(out=lo_t[:, :], in0=t1[:, :], in1=bk[0:1, 1:2], op=ALU.mult).then_inc(SEM["sV"], 1)
            vbrk(v)
            v.tensor_tensor(out=lo_t[:, :], in0=lo_t[:, :], in1=bk[0:1, 0:1], op=ALU.add).then_inc(SEM["sV"], 1)
            vbrk(v)
            v.tensor_scalar(out=ind[:, :], in0=csum[:, :], scalar1=float(khi) - 0.5, scalar2=None, op0=ALU.is_ge).then_inc(SEM["sV"], 1)
            vbrk(v)
            v.reduce_sum(out=t1[:, :], in_=ind[:, :], axis=AX.X).then_inc(SEM["sV"], 1)
            vbrk(v)
            v.tensor_scalar(out=t1[:, :], in0=t1[:, :], scalar1=1.0, scalar2=None, op0=ALU.add).then_inc(SEM["sV"], 1)
            vbrk(v)
            v.tensor_tensor(out=hi_t[:, :], in0=t1[:, :], in1=bk[0:1, 1:2], op=ALU.mult).then_inc(SEM["sV"], 1)
            vbrk(v)
            v.tensor_tensor(out=hi_t[:, :], in0=hi_t[:, :], in1=bk[0:1, 0:1], op=ALU.add).then_inc(SEM["sV"], 1)
            vbrk(v)
            v.engine_nop().then_inc(SEM["sV"], 1)
        VE(_); C["sV"] += 1
        def _(te):
            te.wait_ge(SEM["sV"], C["sV"])
            te.matmul(pS[:, 0:1], lhsT=onesc[:, :], rhs=lo_t[:, :], start=True, stop=True)
            te.matmul(pS[:, 1:2], lhsT=onesc[:, :], rhs=hi_t[:, :], start=True, stop=True).then_inc(SEM["sT"], 1)
        TE(_); C["sT"] += 1
        def _(v):
            v.wait_ge(SEM["sT"], C["sT"])
            v.tensor_copy(out=lob[:, :], in_=pS[:, 0:1])
            v.tensor_copy(out=hib[:, :], in_=pS[:, 1:2]).then_inc(SEM["sV"], 1)
        VE(_); C["sV"] += 1
        bcS = C["sV"]

        for t in range(ntile):
            def _(sy, t=t):
                if t >= 2:
                    sy.wait_ge(SEM["sA"], C["sA"])
                sy.dma_start(out=stg[t % 2][:, :], in_=hsrc[128 * t:128 * (t + 1), :]).then_inc(SEM["dQ"], 16)
            SY(_); C["dQ"] += 16
            dq = C["dQ"]
            def _(sc, t=t, dq=dq):
                sc.wait_ge(SEM["dQ"], dq)
                sc.wait_ge(SEM["sV"], C["sV"])
                sc.activation(out=ab[t % 2][:, :], in_=stg[t % 2][:, :], func=AF.Abs).then_inc(SEM["sA"], 1)
            SC(_); C["sA"] += 1
            aS = C["sA"]
            def _(v, t=t, aS=aS, bcS=bcS):
                v.wait_ge(SEM["sA"], aS)
                v.wait_ge(SEM["sV"], C["sV"])
                v.tensor_scalar(out=scr[:, :], in0=ab[t % 2][:, :], scalar1=hib[:, :], scalar2=0.0,
                                op0=ALU.is_ge, op1=ALU.add, accum_out=chl[:, :]).then_inc(SEM["sV"], 1)
                vbrk(v)
                v.tensor_tensor(out=chacc[:, :], in0=chacc[:, :], in1=chl[:, :], op=ALU.add).then_inc(SEM["sV"], 1)
                vbrk(v)
                v.scalar_tensor_tensor(out=scr[:, :], in0=ab[t % 2][:, :], scalar=hib[:, :],
                                       in1=ab[t % 2][:, :], op0=ALU.is_lt, op1=ALU.mult).then_inc(SEM["sV"], 1)
                vbrk(v)
                v.scalar_tensor_tensor(out=scr[:, :], in0=scr[:, :], scalar=lob[:, :],
                                       in1=scr[:, :], op0=ALU.is_ge, op1=ALU.mult).then_inc(SEM["sV"], 1)
                vbrk(v)
                for r in range(KEXT):
                    v.max(out=stash[:, t * SLOTS + 8 * r:t * SLOTS + 8 * r + 8], in_=scr[:, :]).then_inc(SEM["sV"], 1)
                    vbrk(v)
                    if r < KEXT - 1:
                        v.match_replace(out=scr[:, :], in_to_replace=stash[:, t * SLOTS + 8 * r:t * SLOTS + 8 * r + 8],
                                        in_values=scr[:, :], imm_value=0.0).then_inc(SEM["sV"], 1)
                        vbrk(v)
                v.engine_nop().then_inc(SEM["sV"], 1)
            VE(_); C["sV"] += 1

        def _(v):
            v.wait_ge(SEM["sV"], C["sV"])
            for r in range(8):
                v.max(out=cand[:, 8 * r:8 * r + 8], in_=stash[:, :]).then_inc(SEM["sV"], 1)
                vbrk(v)
                if r < 7:
                    v.match_replace(out=stash[:, :], in_to_replace=cand[:, 8 * r:8 * r + 8],
                                    in_values=stash[:, :], imm_value=0.0).then_inc(SEM["sV"], 1)
                    vbrk(v)
            v.engine_nop().then_inc(SEM["sV"], 1)
        VE(_); C["sV"] += 1

        def _(te):
            te.wait_ge(SEM["sV"], C["sV"])
            te.matmul(pS[0:1, 0:1], lhsT=onescol[:, :], rhs=chacc[:, :], start=True, stop=True).then_inc(SEM["sT"], 1)
        TE(_); C["sT"] += 1
        def _(v):
            v.wait_ge(SEM["sT"], C["sT"])
            v.tensor_copy(out=meta[0:1, 0:1], in_=pS[0:1, 0:1])
            v.tensor_copy(out=meta[0:1, 1:2], in_=lo_t[:, :])
            v.tensor_copy(out=meta[0:1, 2:3], in_=hi_t[:, :])
            v.memset(meta[0:1, 3:4], 0.0).then_inc(SEM["sV"], 1)
        VE(_); C["sV"] += 1
        def _(sy):
            sy.wait_ge(SEM["sV"], C["sV"])
            sy.dma_start(out=ocand, in_=cand[:, :]).then_inc(SEM["dQ"], 16)
            sy.dma_start(out=ometa, in_=meta[:, :]).then_inc(SEM["dQ"], 16)
            sy.wait_ge(SEM["dQ"], C["dQ"] + 32)
        SY(_); C["dQ"] += 32
    return nc


def emit_select(nc, block, SEM, C, sb, pS, cands_ap, metas_ap, khi, frac, amax_out):
    SY, SC, VE, TE = block.sync, block.scalar, block.vector, block.tensor
    cd = sb("sel_cd", [128, 512])
    mt = sb("sel_mt", [8, 4])
    o8 = sb("sel_o8", [8, 1])
    oc = sb("sel_oc", [1, 128])
    ocol = sb("sel_ocol", [128, 1])
    scr = sb("sel_scr", [128, 512])
    pc = sb("sel_pc", [128, 2])
    midb = sb("sel_midb", [128, 2])
    lo = sb("sel_lo", [1, 2]); w = sb("sel_w", [1, 2]); mid = sb("sel_mid", [1, 2])
    u = sb("sel_u", [1, 2]); sel = sb("sel_sel", [1, 2])
    cg = sb("sel_cg", [1, 2]); thrv = sb("sel_thrv", [1, 2]); chg = sb("sel_chg", [1, 1])
    d01 = sb("sel_d01", [1, 1])

    def vbrk(v):
        C["sV"] += 1
        v.wait_ge(SEM["sV"], C["sV"])

    def _(sy):
        for c in range(8):
            sy.dma_start(out=cd[:, 64 * c:64 * (c + 1)], in_=cands_ap[128 * c:128 * (c + 1), :]).then_inc(SEM["dQ"], 16)
        sy.dma_start(out=mt[:, :], in_=metas_ap).then_inc(SEM["dQ"], 16)
    SY(_); C["dQ"] += 9 * 16
    dq0 = C["dQ"]
    def _(v):
        v.wait_ge(SEM["dQ"], dq0)
        v.memset(o8[:, :], 1.0)
        v.memset(oc[:, :], 1.0)
        v.memset(ocol[:, :], 1.0)
        v.memset(lo[:, :], 0.0)
        v.memset(w[:, :], 16.0).then_inc(SEM["sV"], 1)
    VE(_); C["sV"] += 1
    def _(te):
        te.wait_ge(SEM["sV"], C["sV"])
        te.matmul(pS[0:1, 4:5], lhsT=o8[:, :], rhs=mt[:, 0:1], start=True, stop=True).then_inc(SEM["sT"], 1)
    TE(_); C["sT"] += 1
    def _(v, khi=khi):
        v.wait_ge(SEM["sT"], C["sT"])
        v.tensor_copy(out=chg[:, :], in_=pS[0:1, 4:5]).then_inc(SEM["sV"], 1)
        vbrk(v)
        v.tensor_scalar(out=thrv[0:1, 0:1], in0=chg[:, :], scalar1=-1.0, scalar2=float(khi) + 0.5,
                        op0=ALU.mult, op1=ALU.add)
        v.tensor_scalar(out=thrv[0:1, 1:2], in0=chg[:, :], scalar1=-1.0, scalar2=float(khi) - 0.5,
                        op0=ALU.mult, op1=ALU.add).then_inc(SEM["sV"], 1)
        vbrk(v)
        v.engine_nop().then_inc(SEM["sV"], 1)
    VE(_); C["sV"] += 1

    for it in range(NIT_SEL):
        def _(v):
            v.wait_ge(SEM["sV"], C["sV"])
            v.tensor_tensor(out=mid[:, :], in0=lo[:, :], in1=w[:, :], op=ALU.add).then_inc(SEM["sV"], 1)
        VE(_); C["sV"] += 1
        def _(te):
            te.wait_ge(SEM["sV"], C["sV"])
            te.matmul(pS[:, 0:2], lhsT=oc[:, :], rhs=mid[:, :], start=True, stop=True).then_inc(SEM["sT"], 1)
        TE(_); C["sT"] += 1
        def _(v):
            v.wait_ge(SEM["sT"], C["sT"])
            v.tensor_copy(out=midb[:, :], in_=pS[:, 0:2]).then_inc(SEM["sV"], 1)
            vbrk(v)
            v.tensor_scalar(out=scr[:, :], in0=cd[:, :], scalar1=midb[:, 0:1], scalar2=0.0,
                            op0=ALU.is_ge, op1=ALU.add, accum_out=pc[:, 0:1]).then_inc(SEM["sV"], 1)
            vbrk(v)
            v.tensor_scalar(out=scr[:, :], in0=cd[:, :], scalar1=midb[:, 1:2], scalar2=0.0,
                            op0=ALU.is_ge, op1=ALU.add, accum_out=pc[:, 1:2]).then_inc(SEM["sV"], 1)
        VE(_); C["sV"] += 1
        def _(te):
            te.wait_ge(SEM["sV"], C["sV"])
            te.matmul(pS[0:1, 2:4], lhsT=ocol[:, :], rhs=pc[:, :], start=True, stop=True).then_inc(SEM["sT"], 1)
        TE(_); C["sT"] += 1
        def _(v):
            v.wait_ge(SEM["sT"], C["sT"])
            v.tensor_copy(out=cg[:, :], in_=pS[0:1, 2:4]).then_inc(SEM["sV"], 1)
            vbrk(v)
            v.tensor_tensor(out=sel[:, :], in0=cg[:, :], in1=thrv[:, :], op=ALU.is_ge).then_inc(SEM["sV"], 1)
            vbrk(v)
            v.tensor_tensor(out=u[:, :], in0=sel[:, :], in1=w[:, :], op=ALU.mult).then_inc(SEM["sV"], 1)
            vbrk(v)
            v.tensor_tensor(out=lo[:, :], in0=lo[:, :], in1=u[:, :], op=ALU.add).then_inc(SEM["sV"], 1)
            vbrk(v)
            v.tensor_scalar(out=w[:, :], in0=w[:, :], scalar1=0.5, scalar2=None, op0=ALU.mult).then_inc(SEM["sV"], 1)
        VE(_); C["sV"] += 1

    def _(v, frac=frac):
        v.wait_ge(SEM["sV"], C["sV"])
        v.tensor_tensor(out=d01[:, :], in0=lo[0:1, 1:2], in1=lo[0:1, 0:1], op=ALU.subtract).then_inc(SEM["sV"], 1)
        vbrk(v)
        v.tensor_scalar(out=d01[:, :], in0=d01[:, :], scalar1=float(frac), scalar2=None, op0=ALU.mult).then_inc(SEM["sV"], 1)
        vbrk(v)
        v.tensor_tensor(out=d01[:, :], in0=d01[:, :], in1=lo[0:1, 0:1], op=ALU.add).then_inc(SEM["sV"], 1)
        vbrk(v)
        v.tensor_scalar(out=amax_out[:, :], in0=d01[:, :], scalar1=1e-5, scalar2=None, op0=ALU.max).then_inc(SEM["sV"], 1)
    VE(_); C["sV"] += 1
def build_LD():
    nc = bass.Bass("TRN2", target_bir_lowering=False, debug=False)
    hcand = nc.dram_tensor("cands", [8 * 128, 64], F32, kind="ExternalInput").ap()
    hmeta = nc.dram_tensor("metas", [8, 4], F32, kind="ExternalInput").ap()
    hh = nc.dram_tensor("hsig", [N, D], F32, kind="ExternalInput").ap()
    hw1 = nc.dram_tensor("w1T", [8, 128, FF], F32, kind="ExternalInput").ap()
    oh1 = nc.dram_tensor("h1", [FF, N], F32, kind="ExternalOutput").ap()
    ocnt = nc.dram_tensor("cnt2", [1, NBIN], F32, kind="ExternalOutput").ap()
    oam = nc.dram_tensor("amax1", [1, 2], F32, kind="ExternalOutput").ap()
    with ExitStack() as ctx:
        def sb(name, shape, dtype=F32):
            return ctx.enter_context(nc.sbuf_tensor(name, shape, dtype))
        SEM = {}
        for k in ("dQ", "dS", "sT", "sV", "sA"):
            SEM[k] = ctx.enter_context(nc.semaphore(k))
        C = {k: 0 for k in SEM}
        pS = ctx.enter_context(nc.psum_tensor("pS", [128, 128], F32))
        pM = ctx.enter_context(nc.psum_tensor("pM", [128, 512], F32))
        pM2 = ctx.enter_context(nc.psum_tensor("pM2", [128, 512], F32))
        block = ctx.enter_context(nc.Block())
        SY, SC, VE, TE = block.sync, block.scalar, block.vector, block.tensor
        amax = sb("amax", [1, 1])

        def vbrk(v):
            C["sV"] += 1
            v.wait_ge(SEM["sV"], C["sV"])

        emit_select(nc, block, SEM, C, sb, pS, hcand, hmeta, KHI1, FRAC1, amax)

        s1b = sb("s1b", [128, 1])
        gs1b = sb("gs1b", [128, 1])
        rg1b = sb("rg1b", [128, 1])
        t1 = sb("t1b", [1, 1]); t2 = sb("t2b", [1, 1])
        onesc = sb("onescL", [1, 128]); onescol = sb("onescolL", [128, 1])
        def _(v):
            v.wait_ge(SEM["sV"], C["sV"])
            v.memset(onesc[:, :], 1.0)
            v.memset(onescol[:, :], 1.0)
            v.reciprocal(out=t1[:, :], in_=amax[:, :]).then_inc(SEM["sV"], 1)
            vbrk(v)
            v.tensor_scalar(out=t1[:, :], in0=t1[:, :], scalar1=127.0, scalar2=None, op0=ALU.mult).then_inc(SEM["sV"], 1)
            vbrk(v)
            v.engine_nop().then_inc(SEM["sV"], 1)
        VE(_); C["sV"] += 1
        def _(te):
            te.wait_ge(SEM["sV"], C["sV"])
            te.matmul(pS[:, 6:7], lhsT=onesc[:, :], rhs=t1[:, :], start=True, stop=True).then_inc(SEM["sT"], 1)
        TE(_); C["sT"] += 1
        def _(v):
            v.wait_ge(SEM["sT"], C["sT"])
            v.tensor_copy(out=s1b[:, :], in_=pS[:, 6:7]).then_inc(SEM["sV"], 1)
        VE(_); C["sV"] += 1

        wstg = [sb(f"wstgL{i}", [128, FF], F32) for i in range(2)]
        w1s = [sb(f"w1s{d}", [128, FF], BF16) for d in range(8)]
        wi = sb("wiL", [128, FF], I32)
        gsum = sb("gsumL", [128, 8])
        g1 = sb("g1L", [1, 1])
        for d in range(8):
            def _(sy, d=d):
                if d >= 2:
                    sy.wait_ge(SEM["sA"], C["sA"])
                sy.dma_start(out=wstg[d % 2][:, :], in_=hw1[d]).then_inc(SEM["dQ"], 16)
            SY(_); C["dQ"] += 16
            dq = C["dQ"]
            def _(sc, d=d, dq=dq):
                sc.wait_ge(SEM["dQ"], dq)
                sc.activation(out=wstg[d % 2][:, :], in_=wstg[d % 2][:, :], func=AF.Abs,
                              accum_out=gsum[:, d:d + 1]).then_inc(SEM["sA"], 1)
            SC(_); C["sA"] += 1
        def _(v):
            v.wait_ge(SEM["sA"], C["sA"])
            v.reduce_sum(out=gsum[:, 0:1], in_=gsum[:, :], axis=AX.X).then_inc(SEM["sV"], 1)
            vbrk(v)
            v.engine_nop().then_inc(SEM["sV"], 1)
        VE(_); C["sV"] += 1
        def _(te):
            te.wait_ge(SEM["sV"], C["sV"])
            te.matmul(pS[0:1, 7:8], lhsT=onescol[:, :], rhs=gsum[:, 0:1], start=True, stop=True).then_inc(SEM["sT"], 1)
        TE(_); C["sT"] += 1
        def _(v):
            v.wait_ge(SEM["sT"], C["sT"])
            v.tensor_copy(out=g1[:, :], in_=pS[0:1, 7:8]).then_inc(SEM["sV"], 1)
            vbrk(v)
            v.tensor_scalar(out=g1[:, :], in0=g1[:, :], scalar1=1.0 / (FF * D), scalar2=1e-5,
                            op0=ALU.mult, op1=ALU.add).then_inc(SEM["sV"], 1)
            vbrk(v)
            v.tensor_tensor(out=t2[:, :], in0=g1[:, :], in1=amax[:, :], op=ALU.mult).then_inc(SEM["sV"], 1)
            vbrk(v)
            v.tensor_scalar(out=t2[:, :], in0=t2[:, :], scalar1=1.0 / 127.0, scalar2=None, op0=ALU.mult).then_inc(SEM["sV"], 1)
            vbrk(v)
            v.reciprocal(out=g1[:, :], in_=g1[:, :]).then_inc(SEM["sV"], 1)
            vbrk(v)
            v.engine_nop().then_inc(SEM["sV"], 1)
        VE(_); C["sV"] += 1
        def _(te):
            te.wait_ge(SEM["sV"], C["sV"])
            te.matmul(pS[:, 8:9], lhsT=onesc[:, :], rhs=g1[:, :], start=True, stop=True)
            te.matmul(pS[:, 9:10], lhsT=onesc[:, :], rhs=t2[:, :], start=True, stop=True).then_inc(SEM["sT"], 1)
        TE(_); C["sT"] += 1
        def _(v):
            v.wait_ge(SEM["sT"], C["sT"])
            v.tensor_copy(out=rg1b[:, :], in_=pS[:, 8:9])
            v.tensor_copy(out=gs1b[:, :], in_=pS[:, 9:10]).then_inc(SEM["sV"], 1)
        VE(_); C["sV"] += 1
        rgS = C["sV"]
        for d in range(8):
            def _(sy, d=d):
                sy.wait_ge(SEM["sV"], C["sV"])
                sy.dma_start(out=wstg[d % 2][:, :], in_=hw1[d]).then_inc(SEM["dQ"], 16)
            SY(_); C["dQ"] += 16
            dq = C["dQ"]
            def _(v, d=d, dq=dq, rgS=rgS):
                v.wait_ge(SEM["dQ"], dq)
                v.wait_ge(SEM["sV"], rgS)
                v.tensor_scalar(out=wstg[d % 2][:, :], in0=wstg[d % 2][:, :], scalar1=rg1b[:, :],
                                scalar2=None, op0=ALU.mult).then_inc(SEM["sV"], 1)
            VE(_); C["sV"] += 1
            vS = C["sV"]
            def _(sc, d=d, vS=vS):
                sc.wait_ge(SEM["sV"], vS)
                sc.wait_ge(SEM["sV"], C["sV"])
                sc.copy(out=wi[:, :], in_=wstg[d % 2][:, :]).then_inc(SEM["sA"], 1)
            SC(_); C["sA"] += 1
            aS = C["sA"]
            def _(v, d=d, aS=aS):
                v.wait_ge(SEM["sA"], aS)
                v.tensor_scalar(out=wi[:, :], in0=wi[:, :], scalar1=-1, scalar2=1,
                                op0=ALU.max, op1=ALU.min).then_inc(SEM["sV"], 1)
                vbrk(v)
                v.tensor_copy(out=w1s[d][:, :], in_=wi[:, :]).then_inc(SEM["sV"], 1)
            VE(_); C["sV"] += 1
        wS = C["sV"]

        hstg = [sb(f"hstgL{i}", [128, D], F32) for i in range(2)]
        ci = sb("ciL", [128, D], I32)
        cb = [sb(f"cbL{i}", [128, D], BF16) for i in range(2)]
        c1f = [sb(f"c1f{d}", [128, N], BF16) for d in range(8)]
        trq = {}
        for t in range(16):
            def _(sy, t=t):
                sy.wait_ge(SEM["sV"], C["sV"])
                sy.dma_start(out=hstg[t % 2][:, :], in_=hh[128 * t:128 * (t + 1), :]).then_inc(SEM["dS"], 16)
            SY(_); C["dS"] += 16
            dq = C["dS"]
            def _(v, t=t, dq=dq, wS=wS):
                v.wait_ge(SEM["dS"], dq)
                v.wait_ge(SEM["sV"], wS)
                v.tensor_scalar(out=hstg[t % 2][:, :], in0=hstg[t % 2][:, :], scalar1=s1b[:, :],
                                scalar2=None, op0=ALU.mult).then_inc(SEM["sV"], 1)
            VE(_); C["sV"] += 1
            vS = C["sV"]
            def _(sc, t=t, vS=vS):
                sc.wait_ge(SEM["sV"], vS)
                sc.copy(out=ci[:, :], in_=hstg[t % 2][:, :]).then_inc(SEM["sA"], 1)
            SC(_); C["sA"] += 1
            aS = C["sA"]
            def _(v, t=t, aS=aS):
                v.wait_ge(SEM["sA"], aS)
                if t >= 2:
                    v.wait_ge(SEM["dQ"], trq[t - 2])
                v.tensor_scalar(out=ci[:, :], in0=ci[:, :], scalar1=-128, scalar2=127,
                                op0=ALU.max, op1=ALU.min).then_inc(SEM["sV"], 1)
                vbrk(v)
                v.tensor_copy(out=cb[t % 2][:, :], in_=ci[:, :]).then_inc(SEM["sV"], 1)
            VE(_); C["sV"] += 1
            cbS = C["sV"]
            def _(sy, t=t, cbS=cbS):
                sy.wait_ge(SEM["sV"], cbS)
                for d in range(8):
                    sy.dma_start(out=c1f[d][:, 128 * t:128 * (t + 1)], in_=cb[t % 2][:, 128 * d:128 * (d + 1)],
                                 transpose=True).then_inc(SEM["dQ"], 16)
            SY(_); C["dQ"] += 128
            trq[t] = C["dQ"]
        trAll = C["dQ"]

        h1t = [sb(f"h1t{i}", [128, 512], F32) for i in range(2)]
        habs = sb("habsL", [128, 512], F32)
        scr = sb("scrq", [128, 512])
        qc = sb("qcL", [128, NBIN])
        qr2 = sb("qr2L", [128, NBIN])
        qr = sb("qrL", [1, NBIN])
        h1dma = {}
        geluA = {}
        for j in range(128):
            ffc, tc = j // 4, j % 4
            ps = [pM, pM2][j % 2]
            buf = j % 2
            def _(te, j=j, ffc=ffc, tc=tc, ps=ps, wS=wS, trAll=trAll):
                if j == 0:
                    te.wait_ge(SEM["dQ"], trAll)
                    te.wait_ge(SEM["sV"], wS)
                if j >= 2 and (j - 2) in geluA:
                    te.wait_ge(SEM["sA"], geluA[j - 2])
                for d in range(8):
                    mm = te.matmul(ps[:, :], lhsT=w1s[d][:, 128 * ffc:128 * (ffc + 1)],
                                   rhs=c1f[d][:, 512 * tc:512 * (tc + 1)],
                                   start=(d == 0), stop=(d == 7))
                mm.then_inc(SEM["sT"], 1)
            TE(_); C["sT"] += 1
            tS = C["sT"]
            def _(sc, j=j, ps=ps, buf=buf, tS=tS):
                sc.wait_ge(SEM["sT"], tS)
                sc.wait_ge(SEM["sV"], C["sV"])
                if j >= 2:
                    sc.wait_ge(SEM["dQ"], h1dma[j - 2])
                sc.activation(out=h1t[buf][:, :], in_=ps[:, :], func=AF.Gelu, scale=gs1b[:, :]).then_inc(SEM["sA"], 1)
            SC(_); C["sA"] += 1
            geluA[j] = C["sA"]
            aS = C["sA"]
            def _(sy, j=j, ffc=ffc, tc=tc, buf=buf, aS=aS):
                sy.wait_ge(SEM["sA"], aS)
                sy.dma_start(out=oh1[128 * ffc:128 * (ffc + 1), 512 * tc:512 * (tc + 1)],
                             in_=h1t[buf][:, :]).then_inc(SEM["dQ"], 16)
            SY(_); C["dQ"] += 16
            h1dma[j] = C["dQ"]
            def _(sc, buf=buf, aS=aS):
                sc.wait_ge(SEM["sA"], aS)
                sc.wait_ge(SEM["sV"], C["sV"])
                sc.activation(out=habs[:, :], in_=h1t[buf][:, :], func=AF.Abs).then_inc(SEM["sA"], 1)
            SC(_); C["sA"] += 1
            aS = C["sA"]
            def _(v, j=j, aS=aS):
                v.wait_ge(SEM["sA"], aS)
                for bn in range(NBIN):
                    v.tensor_scalar(out=scr[:, :], in0=habs[:, :], scalar1=QCAP2 * bn / NBIN, scalar2=0.0,
                                    op0=ALU.is_ge, op1=ALU.add,
                                    accum_out=(qc if j == 0 else qr2)[:, bn:bn + 1])
                v.engine_nop().then_inc(SEM["sV"], 1)
                vbrk(v)
                if j > 0:
                    v.tensor_tensor(out=qc[:, :], in0=qc[:, :], in1=qr2[:, :], op=ALU.add).then_inc(SEM["sV"], 1)
                    C["sV"] += 1
                    v.wait_ge(SEM["sV"], C["sV"])
                v.engine_nop().then_inc(SEM["sV"], 1)
            VE(_); C["sV"] += 1

        def _(te):
            te.wait_ge(SEM["sV"], C["sV"])
            te.matmul(pS[0:1, 10:10 + NBIN], lhsT=onescol[:, :], rhs=qc[:, :], start=True, stop=True).then_inc(SEM["sT"], 1)
        TE(_); C["sT"] += 1
        am2 = sb("am2", [1, 2])
        def _(v):
            v.wait_ge(SEM["sT"], C["sT"])
            v.tensor_copy(out=qr[:, :], in_=pS[0:1, 10:10 + NBIN])
            v.tensor_copy(out=am2[0:1, 0:1], in_=amax[:, :])
            v.tensor_copy(out=am2[0:1, 1:2], in_=amax[:, :]).then_inc(SEM["sV"], 1)
        VE(_); C["sV"] += 1
        def _(sy):
            sy.wait_ge(SEM["sV"], C["sV"])
            sy.dma_start(out=ocnt, in_=qr[:, :]).then_inc(SEM["dQ"], 16)
            sy.dma_start(out=oam, in_=am2[:, :]).then_inc(SEM["dQ"], 16)
            sy.wait_ge(SEM["dQ"], C["dQ"] + 32)
        SY(_); C["dQ"] += 32
    return nc


def build_LF():
    nc = bass.Bass("TRN2", target_bir_lowering=False, debug=False)
    hcand = nc.dram_tensor("cands", [8 * 128, 64], F32, kind="ExternalInput").ap()
    hmeta = nc.dram_tensor("metas", [8, 4], F32, kind="ExternalInput").ap()
    hh1 = nc.dram_tensor("h1", [FF, N], F32, kind="ExternalInput").ap()
    hx1 = nc.dram_tensor("x1", [N, D], F32, kind="ExternalInput").ap()
    hw2 = nc.dram_tensor("w2T", [32, 128, D], F32, kind="ExternalInput").ap()
    oout = nc.dram_tensor("out", [N, D], F32, kind="ExternalOutput").ap()
    oam = nc.dram_tensor("amax2", [1, 2], F32, kind="ExternalOutput").ap()
    with ExitStack() as ctx:
        def sb(name, shape, dtype=F32):
            return ctx.enter_context(nc.sbuf_tensor(name, shape, dtype))
        SEM = {}
        for k in ("dQ", "dS", "sT", "sV", "sA"):
            SEM[k] = ctx.enter_context(nc.semaphore(k))
        C = {k: 0 for k in SEM}
        pS = ctx.enter_context(nc.psum_tensor("pS", [128, 128], F32))
        pMa = ctx.enter_context(nc.psum_tensor("pMa", [128, 512], F32))
        pMb = ctx.enter_context(nc.psum_tensor("pMb", [128, 512], F32))
        block = ctx.enter_context(nc.Block())
        SY, SC, VE, TE = block.sync, block.scalar, block.vector, block.tensor
        amax = sb("amax", [1, 1])

        def vbrk(v):
            C["sV"] += 1
            v.wait_ge(SEM["sV"], C["sV"])

        emit_select(nc, block, SEM, C, sb, pS, hcand, hmeta, KHI2, FRAC2, amax)

        s2b = sb("s2b", [128, 1])
        gs2 = sb("gs2", [1, 1])
        gs2b = sb("gs2b", [128, 1])
        rg2b = sb("rg2b", [128, 1])
        t1 = sb("t1F", [1, 1]); t2 = sb("t2F", [1, 1])
        onesc = sb("onescF", [1, 128]); onescol = sb("onescolF", [128, 1])
        def _(v):
            v.wait_ge(SEM["sV"], C["sV"])
            v.memset(onesc[:, :], 1.0)
            v.memset(onescol[:, :], 1.0)
            v.reciprocal(out=t1[:, :], in_=amax[:, :]).then_inc(SEM["sV"], 1)
            vbrk(v)
            v.tensor_scalar(out=t1[:, :], in0=t1[:, :], scalar1=127.0, scalar2=None, op0=ALU.mult).then_inc(SEM["sV"], 1)
            vbrk(v)
            v.engine_nop().then_inc(SEM["sV"], 1)
        VE(_); C["sV"] += 1
        def _(te):
            te.wait_ge(SEM["sV"], C["sV"])
            te.matmul(pS[:, 6:7], lhsT=onesc[:, :], rhs=t1[:, :], start=True, stop=True).then_inc(SEM["sT"], 1)
        TE(_); C["sT"] += 1
        def _(v):
            v.wait_ge(SEM["sT"], C["sT"])
            v.tensor_copy(out=s2b[:, :], in_=pS[:, 6:7]).then_inc(SEM["sV"], 1)
        VE(_); C["sV"] += 1

        wstg = [sb(f"wstgF{i}", [128, D], F32) for i in range(2)]
        w2s = [sb(f"w2s{f}", [128, D], BF16) for f in range(32)]
        wi = sb("wiF", [128, D], I32)
        gsum = sb("gsumF", [128, 32])
        g2 = sb("g2F", [1, 1])
        for f in range(32):
            def _(sy, f=f):
                if f >= 2:
                    sy.wait_ge(SEM["sA"], C["sA"])
                sy.dma_start(out=wstg[f % 2][:, :], in_=hw2[f]).then_inc(SEM["dQ"], 16)
            SY(_); C["dQ"] += 16
            dq = C["dQ"]
            def _(sc, f=f, dq=dq):
                sc.wait_ge(SEM["dQ"], dq)
                sc.activation(out=wstg[f % 2][:, :], in_=wstg[f % 2][:, :], func=AF.Abs,
                              accum_out=gsum[:, f:f + 1]).then_inc(SEM["sA"], 1)
            SC(_); C["sA"] += 1
        def _(v):
            v.wait_ge(SEM["sA"], C["sA"])
            v.reduce_sum(out=gsum[:, 0:1], in_=gsum[:, :], axis=AX.X).then_inc(SEM["sV"], 1)
            vbrk(v)
            v.engine_nop().then_inc(SEM["sV"], 1)
        VE(_); C["sV"] += 1
        def _(te):
            te.wait_ge(SEM["sV"], C["sV"])
            te.matmul(pS[0:1, 7:8], lhsT=onescol[:, :], rhs=gsum[:, 0:1], start=True, stop=True).then_inc(SEM["sT"], 1)
        TE(_); C["sT"] += 1
        def _(v):
            v.wait_ge(SEM["sT"], C["sT"])
            v.tensor_copy(out=g2[:, :], in_=pS[0:1, 7:8]).then_inc(SEM["sV"], 1)
            vbrk(v)
            v.tensor_scalar(out=g2[:, :], in0=g2[:, :], scalar1=1.0 / (FF * D), scalar2=1e-5,
                            op0=ALU.mult, op1=ALU.add).then_inc(SEM["sV"], 1)
            vbrk(v)
            v.tensor_tensor(out=t2[:, :], in0=g2[:, :], in1=amax[:, :], op=ALU.mult).then_inc(SEM["sV"], 1)
            vbrk(v)
            v.tensor_scalar(out=gs2[:, :], in0=t2[:, :], scalar1=1.0 / 127.0, scalar2=None, op0=ALU.mult).then_inc(SEM["sV"], 1)
            vbrk(v)
            v.reciprocal(out=g2[:, :], in_=g2[:, :]).then_inc(SEM["sV"], 1)
            vbrk(v)
            v.engine_nop().then_inc(SEM["sV"], 1)
        VE(_); C["sV"] += 1
        def _(te):
            te.wait_ge(SEM["sV"], C["sV"])
            te.matmul(pS[:, 8:9], lhsT=onesc[:, :], rhs=g2[:, :], start=True, stop=True)
            te.matmul(pS[:, 9:10], lhsT=onesc[:, :], rhs=gs2[:, :], start=True, stop=True).then_inc(SEM["sT"], 1)
        TE(_); C["sT"] += 1
        def _(v):
            v.wait_ge(SEM["sT"], C["sT"])
            v.tensor_copy(out=rg2b[:, :], in_=pS[:, 8:9])
            v.tensor_copy(out=gs2b[:, :], in_=pS[:, 9:10]).then_inc(SEM["sV"], 1)
        VE(_); C["sV"] += 1
        rgS = C["sV"]
        for f in range(32):
            def _(sy, f=f):
                sy.wait_ge(SEM["sV"], C["sV"])
                sy.dma_start(out=wstg[f % 2][:, :], in_=hw2[f]).then_inc(SEM["dQ"], 16)
            SY(_); C["dQ"] += 16
            dq = C["dQ"]
            def _(v, f=f, dq=dq, rgS=rgS):
                v.wait_ge(SEM["dQ"], dq)
                v.wait_ge(SEM["sV"], rgS)
                v.tensor_scalar(out=wstg[f % 2][:, :], in0=wstg[f % 2][:, :], scalar1=rg2b[:, :],
                                scalar2=None, op0=ALU.mult).then_inc(SEM["sV"], 1)
            VE(_); C["sV"] += 1
            vS = C["sV"]
            def _(sc, f=f, vS=vS):
                sc.wait_ge(SEM["sV"], vS)
                sc.copy(out=wi[:, :], in_=wstg[f % 2][:, :]).then_inc(SEM["sA"], 1)
            SC(_); C["sA"] += 1
            aS = C["sA"]
            def _(v, f=f, aS=aS):
                v.wait_ge(SEM["sA"], aS)
                v.tensor_scalar(out=wi[:, :], in0=wi[:, :], scalar1=-1, scalar2=1,
                                op0=ALU.max, op1=ALU.min).then_inc(SEM["sV"], 1)
                vbrk(v)
                v.tensor_copy(out=w2s[f][:, :], in_=wi[:, :]).then_inc(SEM["sV"], 1)
            VE(_); C["sV"] += 1
        wS = C["sV"]

        h1c = [sb(f"h1c{i}", [128, 128], F32) for i in range(2)]
        ci = sb("ciF", [128, 128], I32)
        c2 = [sb(f"c2_{f}", [128, 128], BF16) for f in range(32)]
        xr = sb("xrF", [128, D], F32)
        sq = sb("sqF", [128, D], F32)
        ot = sb("otF", [128, D], F32)
        mu = sb("muF", [128, 1]); var = sb("varF", [128, 1]); rs = sb("rsF", [128, 1])
        eps = sb("epsF", [128, 1]); scD = sb("scDF", [128, 1])
        def _(v):
            v.memset(eps[:, :], 1e-5)
            v.memset(scD[:, :], 1.0 / D).then_inc(SEM["sV"], 1)
        VE(_); C["sV"] += 1
        outdma = {}
        for tc in range(16):
            for f in range(32):
                buf = f % 2
                def _(sy, tc=tc, f=f, buf=buf):
                    sy.wait_ge(SEM["sV"], C["sV"])
                    sy.dma_start(out=h1c[buf][:, :],
                                 in_=hh1[128 * f:128 * (f + 1), 128 * tc:128 * (tc + 1)]).then_inc(SEM["dS"], 16)
                SY(_); C["dS"] += 16
                dq = C["dS"]
                def _(v, f=f, buf=buf, dq=dq, wS=wS):
                    v.wait_ge(SEM["dS"], dq)
                    v.wait_ge(SEM["sV"], wS)
                    v.tensor_scalar(out=h1c[buf][:, :], in0=h1c[buf][:, :], scalar1=s2b[:, :],
                                    scalar2=None, op0=ALU.mult).then_inc(SEM["sV"], 1)
                VE(_); C["sV"] += 1
                vS = C["sV"]
                def _(sc, buf=buf, vS=vS):
                    sc.wait_ge(SEM["sV"], vS)
                    sc.copy(out=ci[:, :], in_=h1c[buf][:, :]).then_inc(SEM["sA"], 1)
                SC(_); C["sA"] += 1
                aS = C["sA"]
                def _(v, f=f, aS=aS):
                    v.wait_ge(SEM["sA"], aS)
                    v.tensor_scalar(out=ci[:, :], in0=ci[:, :], scalar1=-128, scalar2=127,
                                    op0=ALU.max, op1=ALU.min).then_inc(SEM["sV"], 1)
                    vbrk(v)
                    v.tensor_copy(out=c2[f][:, :], in_=ci[:, :]).then_inc(SEM["sV"], 1)
                VE(_); C["sV"] += 1
            cS = C["sV"]
            def _(te, tc=tc, cS=cS):
                te.wait_ge(SEM["sV"], cS)
                for f in range(32):
                    te.matmul(pMa[:, :], lhsT=c2[f][:, :], rhs=w2s[f][:, 0:512],
                              start=(f == 0), stop=(f == 31))
                for f in range(32):
                    mm = te.matmul(pMb[:, :], lhsT=c2[f][:, :], rhs=w2s[f][:, 512:1024],
                                   start=(f == 0), stop=(f == 31))
                mm.then_inc(SEM["sT"], 1)
            TE(_); C["sT"] += 1
            tS = C["sT"]
            def _(sy, tc=tc):
                sy.wait_ge(SEM["sV"], C["sV"])
                sy.dma_start(out=xr[:, :], in_=hx1[128 * tc:128 * (tc + 1), :]).then_inc(SEM["dQ"], 16)
            SY(_); C["dQ"] += 16
            xS = C["dQ"]
            def _(v, tS=tS, xS=xS, tc=tc):
                v.wait_ge(SEM["sT"], tS)
                v.wait_ge(SEM["dQ"], xS)
                v.tensor_scalar(out=ot[:, 0:512], in0=pMa[:, :], scalar1=gs2b[:, :], scalar2=None, op0=ALU.mult)
                v.tensor_scalar(out=ot[:, 512:1024], in0=pMb[:, :], scalar1=gs2b[:, :], scalar2=None, op0=ALU.mult).then_inc(SEM["sV"], 1)
                vbrk(v)
                v.tensor_tensor(out=ot[:, :], in0=ot[:, :], in1=xr[:, :], op=ALU.add).then_inc(SEM["sV"], 1)
                vbrk(v)
                v.reduce_sum(out=mu[:, :], in_=ot[:, :], axis=AX.X).then_inc(SEM["sV"], 1)
                vbrk(v)
                v.tensor_scalar(out=mu[:, :], in0=mu[:, :], scalar1=1.0 / D, scalar2=None, op0=ALU.mult).then_inc(SEM["sV"], 1)
                vbrk(v)
                v.tensor_scalar(out=sq[:, :], in0=ot[:, :], scalar1=mu[:, :], scalar2=None,
                                op0=ALU.subtract).then_inc(SEM["sV"], 1)
            VE(_); C["sV"] += 1
            vS = C["sV"]
            def _(sc, vS=vS):
                sc.wait_ge(SEM["sV"], vS)
                sc.activation(out=sq[:, :], in_=sq[:, :], func=AF.Square, accum_out=var[:, :]).then_inc(SEM["sA"], 1)
            SC(_); C["sA"] += 1
            aS = C["sA"]
            def _(sc, aS=aS):
                sc.wait_ge(SEM["sA"], aS)
                sc.activation(out=rs[:, :], in_=var[:, :], func=AF.Sqrt, scale=scD[:, :], bias=eps[:, :]).then_inc(SEM["sA"], 1)
            SC(_); C["sA"] += 1
            aS = C["sA"]
            def _(v, aS=aS):
                v.wait_ge(SEM["sA"], aS)
                v.reciprocal(out=rs[:, :], in_=rs[:, :]).then_inc(SEM["sV"], 1)
                vbrk(v)
                v.tensor_scalar(out=ot[:, :], in0=ot[:, :], scalar1=mu[:, :], scalar2=rs[:, :],
                                op0=ALU.subtract, op1=ALU.mult).then_inc(SEM["sV"], 1)
            VE(_); C["sV"] += 1
            vS = C["sV"]
            def _(sy, tc=tc, vS=vS):
                sy.wait_ge(SEM["sV"], vS)
                sy.dma_start(out=oout[128 * tc:128 * (tc + 1), :], in_=ot[:, :]).then_inc(SEM["dQ"], 16)
            SY(_); C["dQ"] += 16
            outdma[tc] = C["dQ"]

        am2 = sb("am2F", [1, 2])
        def _(v):
            v.wait_ge(SEM["sV"], C["sV"])
            v.tensor_copy(out=am2[0:1, 0:1], in_=amax[:, :])
            v.tensor_copy(out=am2[0:1, 1:2], in_=amax[:, :]).then_inc(SEM["sV"], 1)
        VE(_); C["sV"] += 1
        def _(sy):
            sy.wait_ge(SEM["sV"], C["sV"])
            sy.dma_start(out=oam, in_=am2[:, :]).then_inc(SEM["dQ"], 16)
            sy.wait_ge(SEM["dQ"], C["dQ"] + 16)
        SY(_); C["dQ"] += 16
    return nc


_NCS = {}


def _get(name, builder):
    if name not in _NCS:
        _NCS[name] = builder()
    return _NCS[name]


def kernel(**inputs):
    cores = list(range(B))
    nc = _get("LA", build_LA)
    in_maps = [host_inputs_LA(inputs, c) for c in range(B)]
    res = run_bass_kernel_spmd(nc, in_maps, core_ids=cores)
    x1 = [np.asarray(res.results[c]["x1"]) for c in range(B)]
    hsig = [np.asarray(res.results[c]["hsig"]) for c in range(B)]
    habs = [np.asarray(res.results[c]["habs"]) for c in range(B)]
    cnts = np.stack([np.asarray(res.results[c]["cnt1"])[0] for c in range(B)])
    brk = np.array([[0.0, QCAP1 / NBIN]], np.float32)

    for rnd, nbin_in in ((2, NBIN), (3, NBR)):
        nc = _get(f"q1r{rnd}", lambda nbin_in=nbin_in: build_round(N, D, nbin_in, KHI1))
        res = run_bass_kernel_spmd(nc, [{"src": habs[c], "cin": cnts.astype(np.float32),
                                         "bin": brk.astype(np.float32)} for c in range(B)], core_ids=cores)
        cnts = np.stack([np.asarray(res.results[c]["cout"])[0] for c in range(B)])
        brk = np.asarray(res.results[0]["bout"])

    nc = _get("q1x", lambda: build_extract(N, D, NBR, KHI1))
    res = run_bass_kernel_spmd(nc, [{"src": habs[c], "cin": cnts.astype(np.float32),
                                     "bin": brk.astype(np.float32)} for c in range(B)], core_ids=cores)
    cands = np.concatenate([np.asarray(res.results[c]["cand"]) for c in range(B)], axis=0)
    metas = np.stack([np.asarray(res.results[c]["meta"])[0] for c in range(B)])

    w1T = np.ascontiguousarray(np.asarray(inputs["L0_ffn_w1"], np.float32).T.reshape(8, 128, FF))
    nc = _get("LD", build_LD)
    res = run_bass_kernel_spmd(nc, [{"cands": cands, "metas": metas, "hsig": hsig[c], "w1T": w1T}
                                    for c in range(B)], core_ids=cores)
    h1 = [np.asarray(res.results[c]["h1"]) for c in range(B)]
    cnts = np.stack([np.asarray(res.results[c]["cnt2"])[0] for c in range(B)])
    brk = np.array([[0.0, QCAP2 / NBIN]], np.float32)

    for rnd, nbin_in in ((2, NBIN), (3, NBR)):
        nc = _get(f"q2r{rnd}", lambda nbin_in=nbin_in: build_round(FF, N, nbin_in, KHI2))
        res = run_bass_kernel_spmd(nc, [{"src": h1[c], "cin": cnts.astype(np.float32),
                                         "bin": brk.astype(np.float32)} for c in range(B)], core_ids=cores)
        cnts = np.stack([np.asarray(res.results[c]["cout"])[0] for c in range(B)])
        brk = np.asarray(res.results[0]["bout"])

    nc = _get("q2x", lambda: build_extract(FF, N, NBR, KHI2))
    res = run_bass_kernel_spmd(nc, [{"src": h1[c], "cin": cnts.astype(np.float32),
                                     "bin": brk.astype(np.float32)} for c in range(B)], core_ids=cores)
    cands = np.concatenate([np.asarray(res.results[c]["cand"]) for c in range(B)], axis=0)
    metas = np.stack([np.asarray(res.results[c]["meta"])[0] for c in range(B)])

    w2T = np.ascontiguousarray(np.asarray(inputs["L0_ffn_w2"], np.float32).T.reshape(32, 128, D))
    nc = _get("LF", build_LF)
    res = run_bass_kernel_spmd(nc, [{"cands": cands, "metas": metas, "h1": h1[c], "x1": x1[c], "w2T": w2T}
                                    for c in range(B)], core_ids=cores)
    out = np.stack([np.asarray(res.results[c]["out"]) for c in range(B)])
    return out


# version 04 (final)
# speedup vs baseline: 1.1501x; 1.1501x over previous
import math
import numpy as np
from contextlib import ExitStack

import concourse.bass as bass
import concourse.mybir as mybir
from concourse.bass_utils import run_bass_kernel_spmd

F32 = mybir.dt.float32
BF16 = mybir.dt.bfloat16
I32 = mybir.dt.int32
AF = mybir.ActivationFunctionType
ALU = mybir.AluOpType
AX = mybir.AxisListType

B, N, D, H, DH = 8, 2048, 1024, 16, 64
MEM, J = 16, 2064
INNER = H * DH
FF = 4096
NQS = 8
CHQ = 32
NCH = N // CHQ
SPC = CHQ // NQS
SHIFT = 16.0
TOPK = 64
NIT_BF, NIT_F32 = 8, 9
ULP16 = 0.0625
BIS_LO, BIS_HI = 8.0, 26.0
VK_INACT = 1.0
WMAX = J
SLW = 17 * 128
BTW = 17 * 16 * 32

N1, N2 = B * N * D, B * N * FF
P1, P2 = 0.995 * (N1 - 1), 0.995 * (N2 - 1)
KHI1 = N1 - 1 - math.floor(P1)
FRAC1 = P1 - math.floor(P1)
KHI2 = N2 - 1 - math.floor(P2)
FRAC2 = P2 - math.floor(P2)
QCAP1, QCAP2 = 8.0, 8.0
NIT_SEL = 64
KEXT = 3
NBIN = 16


def cdiv(a, b):
    return (a + b - 1) // b


def slice_W(c, s):
    return CHQ * c + NQS * s + 24


def build_LA(debug=False):
    nc = bass.Bass("TRN2", target_bir_lowering=False, debug=False)
    hx = nc.dram_tensor("x", [N, D], F32, kind="ExternalInput").ap()
    hxT = nc.dram_tensor("xT", [D, N], F32, kind="ExternalInput").ap()
    hWq = nc.dram_tensor("Wq", [8, 128, INNER], F32, kind="ExternalInput").ap()
    hWk = nc.dram_tensor("Wk", [8, 128, INNER], F32, kind="ExternalInput").ap()
    hWv = nc.dram_tensor("Wv", [8, 128, INNER], F32, kind="ExternalInput").ap()
    hWo = nc.dram_tensor("Wo", [8, 128, D], F32, kind="ExternalInput").ap()
    hbo = nc.dram_tensor("bo", [1, D], F32, kind="ExternalInput").ap()
    hpre = nc.dram_tensor("preM", [128, 128], F32, kind="ExternalInput").ap()
    hpost = nc.dram_tensor("postM", [128, 128], F32, kind="ExternalInput").ap()
    hmemk = nc.dram_tensor("memkP", [128, 8 * MEM], F32, kind="ExternalInput").ap()
    hmemv = nc.dram_tensor("memvT", [MEM, INNER], F32, kind="ExternalInput").ap()
    hmsk7 = nc.dram_tensor("mask7", [128, 8], F32, kind="ExternalInput").ap()
    hg = nc.dram_tensor("lng", [1, D], F32, kind="ExternalInput").ap()
    hb = nc.dram_tensor("lnb", [1, D], F32, kind="ExternalInput").ap()
    heye = nc.dram_tensor("eye", [128, 128], F32, kind="ExternalInput").ap()
    ox1 = nc.dram_tensor("x1", [N, D], F32, kind="ExternalOutput").ap()
    ohabs = nc.dram_tensor("habs", [N, D], F32, kind="ExternalOutput").ap()
    oh = nc.dram_tensor("hsig", [N, D], F32, kind="ExternalOutput").ap()
    ocnt = nc.dram_tensor("cnt1", [1, NBIN], F32, kind="ExternalOutput").ap()
    dbg = {}
    if debug:
        dbg["qp"] = nc.dram_tensor("dbg_qp", [128, N], F32, kind="ExternalOutput").ap()
        dbg["kp"] = nc.dram_tensor("dbg_kp", [128, J], F32, kind="ExternalOutput").ap()
        dbg["vt"] = nc.dram_tensor("dbg_vt", [128, INNER], F32, kind="ExternalOutput").ap()
        dbg["ds"] = nc.dram_tensor("dbg_ds", [128, WMAX], F32, kind="ExternalOutput").ap()
        dbg["vk"] = nc.dram_tensor("dbg_vk", [128, 24], F32, kind="ExternalOutput").ap()
        dbg["at"] = nc.dram_tensor("dbg_at", [128, WMAX], F32, kind="ExternalOutput").ap()
        dbg["y"] = nc.dram_tensor("dbg_y", [128, D], F32, kind="ExternalOutput").ap()
        dbg["bt"] = nc.dram_tensor("dbg_bt", [128, BTW], F32, kind="ExternalOutput").ap()

    ctx = ExitStack()
    with ctx:
        def sb(name, shape, dtype=F32, st=ctx):
            return st.enter_context(nc.sbuf_tensor(name, shape, dtype))
        def pt(name, shape, dtype=F32):
            return ctx.enter_context(nc.psum_tensor(name, shape, dtype))

        QP = [sb(f"QP{g}", [128, N], BF16) for g in range(8)]
        KP = [sb(f"KP{g}", [128, J], BF16) for g in range(8)]
        VT = [sb(f"VT{t}", [128, INNER], BF16) for t in range(16)] + [sb("VT16", [16, INNER], BF16)]
        WB = [sb(f"WB{d}", [128, INNER], BF16) for d in range(8)]
        preM = sb("preMs", [128, 128], BF16)
        postM = sb("postMs", [128, 128], BF16)
        bo_t = sb("bo_t", [1, D], F32)
        ones1 = sb("ones1", [1, 128], F32)
        mask7 = sb("mask7s", [128, 8], F32)
        gr = sb("gr", [128, D], F32)
        br = sb("br", [128, D], F32)
        shift_t = sb("shift_t", [128, 1], F32)
        eps_t = sb("eps_t", [128, 1], F32)
        sc8_t = sb("sc8_t", [128, 1], F32)
        scD_t = sb("scD_t", [128, 1], F32)
        idx8 = sb("idx8", [128, 8], F32)
        wk32 = sb("wk32", [128, WMAX], F32)
        idnb = sb("idnb", [128, 128], BF16)

        pA = pt("pA", [128, 512], F32)
        pB = pt("pB", [128, 512], F32)
        pC = pt("pC", [128, 512], F32)
        pD = pt("pD", [128, 512], F32)
        pW = pt("pW", [128, D], F32)
        pV = pt("pV", [128, 128], F32)
        pT = pt("pT", [128, 512], BF16)

        SEM = {}
        for k in ("dQ", "dS", "dG", "sT", "sV", "sA", "sP"):
            SEM[k] = ctx.enter_context(nc.semaphore(k))
        C = {k: 0 for k in SEM}
        block = ctx.enter_context(nc.Block())
        SY, SC, VE, TE, GP = block.sync, block.scalar, block.vector, block.tensor, block.gpsimd

        ph0 = ExitStack()
        xfb = [sb(f"xfb{d}", [128, N], BF16, st=ph0) for d in range(8)]
        xstg = [sb(f"xstg{i}", [128, N], F32, st=ph0) for i in range(2)]
        vtmp = sb("vtmp", [128, INNER], BF16, st=ph0)

        def _(sy):
            sy.dma_start(out=wk32[:, 0:128], in_=hpre).then_inc(SEM["dQ"], 16)
            sy.dma_start(out=wk32[:, 128:256], in_=hpost).then_inc(SEM["dQ"], 16)
            sy.dma_start(out=wk32[:, 256:256 + 8 * MEM], in_=hmemk).then_inc(SEM["dQ"], 16)
            sy.dma_start(out=xstg[0][0:MEM, 0:INNER], in_=hmemv).then_inc(SEM["dQ"], 16)
            sy.dma_start(out=bo_t[:, :], in_=hbo).then_inc(SEM["dQ"], 16)
            sy.dma_start(out=mask7[:, :], in_=hmsk7).then_inc(SEM["dQ"], 16)
            sy.dma_start(out=xstg[1][0:1, 0:D], in_=hg).then_inc(SEM["dQ"], 16)
            sy.dma_start(out=wk32[:, 384:512], in_=heye).then_inc(SEM["dQ"], 16)
        SY(_); C["dQ"] += 8 * 16
        def _(sy):
            sy.dma_start(out=xstg[1][0:1, D:2 * D], in_=hb).then_inc(SEM["dQ"], 16)
        SY(_); C["dQ"] += 16

        def _(v):
            v.wait_ge(SEM["dQ"], C["dQ"])
            v.tensor_copy(out=preM[:, :], in_=wk32[:, 0:128])
            v.tensor_copy(out=postM[:, :], in_=wk32[:, 128:256])
            for g in range(8):
                v.tensor_copy(out=KP[g][:, 0:MEM], in_=wk32[:, 256 + g * MEM:256 + (g + 1) * MEM])
            v.tensor_copy(out=VT[0][0:MEM, :], in_=xstg[0][0:MEM, 0:INNER])
            v.tensor_copy(out=idnb[:, :], in_=wk32[:, 384:512])
            v.memset(ones1[:, :], 1.0)
            v.memset(shift_t[:, :], SHIFT)
            v.memset(eps_t[:, :], 1e-5)
            v.memset(sc8_t[:, :], 0.125)
            v.memset(scD_t[:, :], 1.0 / D)
            for i in range(8):
                v.memset(idx8[:, i:i + 1], float(i))
            v.engine_nop().then_inc(SEM["sV"], 1)
        VE(_); C["sV"] += 1

        def _(te):
            te.wait_ge(SEM["sV"], C["sV"])
            te.matmul(pW[:, 0:512], lhsT=ones1[:, :], rhs=xstg[1][0:1, 0:512], start=True, stop=True)
            te.matmul(pW[:, 512:1024], lhsT=ones1[:, :], rhs=xstg[1][0:1, 512:D], start=True, stop=True).then_inc(SEM["sT"], 1)
        TE(_); C["sT"] += 1
        def _(v):
            v.wait_ge(SEM["sT"], C["sT"])
            v.tensor_copy(out=gr[:, :], in_=pW[:, :]).then_inc(SEM["sV"], 1)
        VE(_); C["sV"] += 1
        def _(te):
            te.wait_ge(SEM["sV"], C["sV"])
            te.matmul(pW[:, 0:512], lhsT=ones1[:, :], rhs=xstg[1][0:1, D:D + 512], start=True, stop=True)
            te.matmul(pW[:, 512:1024], lhsT=ones1[:, :], rhs=xstg[1][0:1, D + 512:2 * D], start=True, stop=True).then_inc(SEM["sT"], 1)
        TE(_); C["sT"] += 1
        def _(v):
            v.wait_ge(SEM["sT"], C["sT"])
            v.tensor_copy(out=br[:, :], in_=pW[:, :]).then_inc(SEM["sV"], 1)
        VE(_); C["sV"] += 1

        for d in range(8):
            def _(sy, d=d):
                sy.wait_ge(SEM["sV"], C["sV"])
                sy.wait_ge(SEM["sT"], C["sT"])
                sy.dma_start(out=xstg[d % 2][:, :], in_=hxT[128 * d:128 * (d + 1), :]).then_inc(SEM["dQ"], 16)
            SY(_); C["dQ"] += 16
            dq = C["dQ"]
            def _(v, d=d, dq=dq):
                v.wait_ge(SEM["dQ"], dq)
                v.tensor_copy(out=xfb[d][:, :], in_=xstg[d % 2][:, :]).then_inc(SEM["sV"], 1)
            VE(_); C["sV"] += 1

        for proj, hW in (("q", hWq), ("k", hWk), ("v", hWv)):
            for d in range(8):
                def _(sc, d=d, hW=hW):
                    sc.wait_ge(SEM["sV"], C["sV"])
                    sc.dma_start(out=xstg[d % 2][:, 0:INNER], in_=hW[d]).then_inc(SEM["dS"], 16)
                SC(_); C["dS"] += 16
                dsc = C["dS"]
                def _(v, d=d, dsc=dsc):
                    v.wait_ge(SEM["dS"], dsc)
                    v.tensor_copy(out=WB[d][:, :], in_=xstg[d % 2][:, 0:INNER]).then_inc(SEM["sV"], 1)
                VE(_); C["sV"] += 1
            vAll = C["sV"]
            if proj in ("q", "k"):
                for g in range(8):
                    for qc in range(4):
                        ps = [pA, pB][(g * 4 + qc) % 2]
                        def _(te, g=g, qc=qc, ps=ps, vAll=vAll):
                            te.wait_ge(SEM["sV"], vAll)
                            te.wait_ge(SEM["sA"], C["sA"])
                            for d in range(8):
                                mm = te.matmul(ps[:, :], lhsT=WB[d][:, 128 * g:128 * (g + 1)],
                                               rhs=xfb[d][:, 512 * qc:512 * (qc + 1)],
                                               start=(d == 0), stop=(d == 7))
                            mm.then_inc(SEM["sT"], 1)
                        TE(_); C["sT"] += 1
                        tS = C["sT"]
                        if proj == "q":
                            def _(sc, g=g, qc=qc, ps=ps, tS=tS):
                                sc.wait_ge(SEM["sT"], tS)
                                sc.activation(out=QP[g][:, 512 * qc:512 * (qc + 1)], in_=ps[:, :],
                                              func=AF.Copy, scale=0.125).then_inc(SEM["sA"], 1)
                            SC(_); C["sA"] += 1
                        else:
                            def _(sc, g=g, qc=qc, ps=ps, tS=tS):
                                sc.wait_ge(SEM["sT"], tS)
                                sc.activation(out=KP[g][:, MEM + 512 * qc:MEM + 512 * (qc + 1)], in_=ps[:, :],
                                              func=AF.Copy).then_inc(SEM["sA"], 1)
                            SC(_); C["sA"] += 1
            else:
                for t in range(17):
                    tok0 = max(0, 128 * t - MEM)
                    tok1 = min(N, 128 * t + 128 - MEM)
                    M = tok1 - tok0
                    def _(te, t=t, tok0=tok0, M=M, vAll=vAll):
                        te.wait_ge(SEM["sV"], vAll)
                        te.wait_ge(SEM["sA"], C["sA"])
                        for half in range(2):
                            for d in range(8):
                                mm = te.matmul(pW[0:M, 512 * half:512 * (half + 1)],
                                               lhsT=xfb[d][:, tok0:tok0 + M],
                                               rhs=WB[d][:, 512 * half:512 * (half + 1)],
                                               start=(d == 0), stop=(d == 7))
                        mm.then_inc(SEM["sT"], 1)
                    TE(_); C["sT"] += 1
                    tS = C["sT"]
                    if t == 0:
                        def _(sc, tS=tS):
                            sc.wait_ge(SEM["sT"], tS)
                            sc.activation(out=xfb[0].bitcast(BF16)[0:112, 0:INNER] if False else vtmp[0:112, :],
                                          in_=pW[0:112, :], func=AF.Copy).then_inc(SEM["sA"], 1)
                        SC(_); C["sA"] += 1
                        aS0 = C["sA"]
                        def _(sy, aS0=aS0):
                            sy.wait_ge(SEM["sA"], aS0)
                            sy.dma_start(out=VT[0][MEM:128, :], in_=vtmp[0:112, :]).then_inc(SEM["dQ"], 16)
                        SY(_); C["dQ"] += 16
                    else:
                        def _(sc, t=t, M=M, tS=tS):
                            sc.wait_ge(SEM["sT"], tS)
                            sc.activation(out=VT[t][0:M, :], in_=pW[0:M, :], func=AF.Copy).then_inc(SEM["sA"], 1)
                        SC(_); C["sA"] += 1

        for d in range(8):
            def _(sc, d=d):
                sc.wait_ge(SEM["sV"], C["sV"])
                sc.dma_start(out=xstg[d % 2][:, 0:D], in_=hWo[d]).then_inc(SEM["dS"], 16)
            SC(_); C["dS"] += 16
            dsc = C["dS"]
            def _(v, d=d, dsc=dsc):
                v.wait_ge(SEM["dS"], dsc)
                v.wait_ge(SEM["sT"], C["sT"])
                v.tensor_copy(out=WB[d][:, :D], in_=xstg[d % 2][:, 0:D]).then_inc(SEM["sV"], 1)
            VE(_); C["sV"] += 1

        if debug:
            def _(sy):
                sy.wait_ge(SEM["sA"], C["sA"])
                sy.wait_ge(SEM["sV"], C["sV"])
            SY(_)
            def _(v, aCur=C["sA"]):
                v.wait_ge(SEM["sA"], aCur)
                v.tensor_copy(out=xstg[0][:, :], in_=QP[0][:, :]).then_inc(SEM["sV"], 1)
            VE(_); C["sV"] += 1
            def _(sy):
                sy.wait_ge(SEM["sV"], C["sV"])
                sy.dma_start(out=dbg["qp"], in_=xstg[0][:, :]).then_inc(SEM["dQ"], 16)
            SY(_); C["dQ"] += 16
            def _(v, aCur=C["sA"]):
                v.wait_ge(SEM["sA"], aCur)
                v.tensor_copy(out=wk32[:, 0:J], in_=KP[0][:, :]).then_inc(SEM["sV"], 1)
            VE(_); C["sV"] += 1
            def _(sy):
                sy.wait_ge(SEM["sV"], C["sV"])
                sy.dma_start(out=dbg["kp"], in_=wk32[:, 0:J]).then_inc(SEM["dQ"], 16)
            SY(_); C["dQ"] += 16
            def _(v, aCur=C["sA"]):
                v.wait_ge(SEM["sA"], aCur)
                v.tensor_copy(out=xstg[0][:, 0:INNER], in_=VT[1][:, :]).then_inc(SEM["sV"], 1)
            VE(_); C["sV"] += 1
            def _(sy):
                sy.wait_ge(SEM["sV"], C["sV"])
                sy.dma_start(out=dbg["vt"], in_=xstg[0][:, 0:INNER]).then_inc(SEM["dQ"], 16)
            SY(_); C["dQ"] += 16

        def _(sy):
            sy.wait_ge(SEM["sA"], C["sA"])
            sy.wait_ge(SEM["sV"], C["sV"])
            sy.wait_ge(SEM["sT"], C["sT"])
            sy.wait_ge(SEM["dQ"], C["dQ"])
        SY(_)
        ph0.close()

        qkbT = [sb(f"qkb{t}", [128, WMAX], BF16) for t in range(4)]
        qkb = [(qkbT[g // 2], 64 * (g % 2)) for g in range(8)]
        slb = sb("slb", [128, SPC * SLW], BF16)
        dsb = [sb(f"ds{i}", [128, WMAX], F32) for i in range(1)]
        d16 = [sb(f"d16_{i}", [128, WMAX], BF16) for i in range(1)]
        bt = sb("bt", [128, BTW], BF16)
        ctx_t = [sb(f"ctx{p}", [128, 128], BF16) for p in range(8)]
        qt = [sb(f"qt{i}", [128, 64], BF16) for i in range(2)]
        xres = sb("xres", [128, D], F32)
        sq_t = wk32[:, 1024:2048]
        qcnt = sb("qcnt", [128, 16 * NBIN], F32)
        qred = sb("qred", [128, NBIN], F32)
        qfin = sb("qfin", [1, NBIN], F32)
        st_lo = sb("st_lo", [128, 2], F32)
        st_w = sb("st_w", [128, 2], F32)
        st_mid = sb("st_mid", [128, 2], F32)
        st_cnt = sb("st_cnt", [128, 2], F32)
        st_u = sb("st_u", [128, 2], F32)
        st_hi = sb("st_hi", [128, 2], F32)
        chi_t = sb("chi_t", [128, 2], F32)
        vk_t = sb("vk_t", [128, 2], F32)
        nvk = sb("nvk", [128, 2], F32)
        zz = sb("zz", [128, 2], F32)
        zr = sb("zr", [128, 2], F32)
        m8v = sb("m8v", [128, 8], F32)
        sel8 = sb("sel8", [128, 8], F32)
        mu_t = sb("mu_t", [128, 1], F32)
        var_t = sb("var_t", [128, 1], F32)
        rs_t = sb("rs_t", [128, 1], F32)

        def _(v):
            v.memset(slb[:, :], 0.0)
            v.memset(qcnt[:, :], 0.0)
            for i in range(2):
                v.memset(qt[i][:, :], 0.0)
            v.engine_nop().then_inc(SEM["sV"], 1)
        VE(_); C["sV"] += 1

        evQT = {}
        giT = [0]
        for c in range(NCH):
            Wc = CHQ * c + 48
            q0 = CHQ * c
            njc = cdiv(Wc, 512)
            for g in range(8):
                def _(v, g=g, aCur=C["sA"]):
                    v.wait_ge(SEM["sT"], C["sT"])
                    v.wait_ge(SEM["sA"], aCur)
                    v.tensor_copy(out=qt[g % 2][0:64, 0:32], in_=QP[g][0:64, q0:q0 + 32])
                    v.tensor_copy(out=qt[g % 2][64:128, 32:64], in_=QP[g][64:128, q0:q0 + 32]).then_inc(SEM["sV"], 1)
                VE(_); C["sV"] += 1
                vS = C["sV"]
                for jc in range(njc):
                    jn = min(512, Wc - 512 * jc)
                    ps = [pA, pB][jc % 2]
                    def _(te, g=g, jc=jc, jn=jn, ps=ps, vS=vS):
                        te.wait_ge(SEM["sV"], vS)
                        te.wait_ge(SEM["sA"], C["sA"])
                        te.matmul(ps[0:64, 0:jn], lhsT=qt[g % 2][:, :], rhs=KP[g][:, 512 * jc:512 * jc + jn],
                                  start=True, stop=True).then_inc(SEM["sT"], 1)
                    TE(_); C["sT"] += 1
                    tS = C["sT"]
                    def _(sc, g=g, jc=jc, jn=jn, ps=ps, tS=tS, dg=C["dG"]):
                        sc.wait_ge(SEM["sT"], tS)
                        sc.wait_ge(SEM["dG"], dg)
                        qt_, ro_ = qkb[g]
                        sc.activation(out=qt_[ro_:ro_ + 64, 512 * jc:512 * jc + jn], in_=ps[0:64, 0:jn],
                                      func=AF.Copy).then_inc(SEM["sA"], 1)
                    SC(_); C["sA"] += 1
            qkA = C["sA"]

            for p2 in range(4):
                ss = [p2]
                Ws = [slice_W(c, s) for s in ss]
                active = [CHQ * c + NQS * s >= 48 for s in ss]
                for i, s in enumerate(ss):
                    def _(gp, i=i, s=s, qkA=qkA, tq=C["sT"]):
                        gp.wait_ge(SEM["sA"], qkA)
                        gp.wait_ge(SEM["sV"], C["sV"])
                        gp.wait_ge(SEM["sT"], tq)
                        for h in range(H):
                            qt_, ro_ = qkb[h // 2]
                            src = qt_[ro_ + 32 * (h % 2) + NQS * s:ro_ + 32 * (h % 2) + NQS * s + NQS, 0:Ws[i]]
                            gp.dma_start(out=slb[h:h + 113:16, SLW * s:SLW * s + Ws[i]], in_=src).then_inc(SEM["dG"], 16)
                    GP(_); C["dG"] += 16 * H
                    gS = C["dG"]
                    nj = cdiv(Ws[i], 512)
                    for jc in range(nj):
                        jn = min(512, Ws[i] - 512 * jc)
                        ps = [pC, pD][jc % 2]
                        def _(te, i=i, s=s, jc=jc, jn=jn, ps=ps, gS=gS):
                            te.wait_ge(SEM["dG"], gS)
                            te.wait_ge(SEM["sA"], C["sA"])
                            te.matmul(ps[:, 0:jn], lhsT=preM[:, :],
                                      rhs=slb[:, SLW * s + 512 * jc:SLW * s + 512 * jc + jn],
                                      start=True, stop=True).then_inc(SEM["sT"], 1)
                        TE(_); C["sT"] += 1
                        tS = C["sT"]
                        def _(sc, i=i, jc=jc, jn=jn, ps=ps, tS=tS):
                            sc.wait_ge(SEM["sT"], tS)
                            sc.wait_ge(SEM["sV"], C["sV"])
                            sc.activation(out=dsb[i][:, 512 * jc:512 * jc + jn], in_=ps[:, 0:jn],
                                          func=AF.Copy, bias=SHIFT)
                            sc.activation(out=d16[i][:, 512 * jc:512 * jc + jn], in_=ps[:, 0:jn],
                                          func=AF.Copy, bias=SHIFT).then_inc(SEM["sA"], 1)
                        SC(_); C["sA"] += 1
                preA = C["sA"]

                def _(v, Ws=Ws, preA=preA):
                    v.wait_ge(SEM["sA"], preA)
                    for i in range(len(ss)):
                        v.tensor_tensor(out=dsb[i][:, Ws[i] - 7:Ws[i]], in0=dsb[i][:, Ws[i] - 7:Ws[i]],
                                        in1=mask7[:, 0:7], op=ALU.mult)
                        v.tensor_tensor(out=d16[i][:, Ws[i] - 7:Ws[i]], in0=d16[i][:, Ws[i] - 7:Ws[i]],
                                        in1=mask7[:, 0:7], op=ALU.mult)
                    v.memset(st_lo[:, :], BIS_LO)
                    v.memset(st_w[:, :], (BIS_HI - BIS_LO) / 2)
                    v.memset(st_mid[:, :], (BIS_HI + BIS_LO) / 2).then_inc(SEM["sV"], 1)
                    C["sV"] += 1
                    v.wait_ge(SEM["sV"], C["sV"])
                    v.engine_nop().then_inc(SEM["sV"], 1)
                VE(_); C["sV"] += 1

                if debug and c == 16 and p2 == 0:
                    def _(sy, vCur=C["sV"]):
                        sy.wait_ge(SEM["sV"], vCur)
                        sy.dma_start(out=dbg["ds"], in_=dsb[0][:, :]).then_inc(SEM["dQ"], 16)
                    SY(_); C["dQ"] += 16
                if any(active):
                    for it in range(NIT_BF + NIT_F32):
                        kind = "bf" if it < NIT_BF else "f32"
                        def _(v, it=it, kind=kind, Ws=Ws, active=active, p2=p2):
                            for i in range(len(ss)):
                                if not active[i]:
                                    continue
                                src = d16[i] if kind == "bf" else dsb[i]
                                s = ss[i]
                                v.tensor_scalar(out=slb[:, SLW * s:SLW * s + Ws[i]],
                                                in0=src[:, 0:Ws[i]],
                                                scalar1=st_mid[:, i:i + 1], scalar2=0.0,
                                                op0=ALU.is_ge, op1=ALU.add,
                                                accum_out=st_cnt[:, i:i + 1]).then_inc(SEM["sV"], 1)
                            C["sV"] += 1
                            v.wait_ge(SEM["sV"], C["sV"])
                            v.scalar_tensor_tensor(out=st_u[:, :], in0=st_cnt[:, :], scalar=float(TOPK),
                                                   in1=st_w[:, :], op0=ALU.is_ge, op1=ALU.mult).then_inc(SEM["sV"], 1)
                            C["sV"] += 1
                            v.wait_ge(SEM["sV"], C["sV"])
                            v.tensor_tensor(out=st_lo[:, :], in0=st_lo[:, :], in1=st_u[:, :], op=ALU.add)
                            v.tensor_scalar(out=st_w[:, :], in0=st_w[:, :], scalar1=0.5, scalar2=None, op0=ALU.mult).then_inc(SEM["sV"], 1)
                            C["sV"] += 1
                            v.wait_ge(SEM["sV"], C["sV"])
                            v.tensor_tensor(out=st_mid[:, :], in0=st_lo[:, :], in1=st_w[:, :], op=ALU.add)
                            if it == NIT_BF - 1:
                                v.tensor_scalar(out=st_lo[:, :], in0=st_lo[:, :], scalar1=ULP16, scalar2=None, op0=ALU.subtract)
                                v.tensor_scalar(out=st_w[:, :], in0=st_w[:, :], scalar1=ULP16, scalar2=None, op0=ALU.add).then_inc(SEM["sV"], 1)
                                C["sV"] += 1
                                v.wait_ge(SEM["sV"], C["sV"])
                                v.tensor_tensor(out=st_mid[:, :], in0=st_lo[:, :], in1=st_w[:, :], op=ALU.add)
                            v.engine_nop().then_inc(SEM["sV"], 1)
                        VE(_); C["sV"] += 1
                        if debug and c == 16 and p2 == 0 and it in (0, 4, 8, 12, 16):
                            col = 4 + 4 * ((0, 4, 8, 12, 16).index(it))
                            def _(v):
                                v.tensor_copy(out=m8v[:, 0:1], in_=st_mid[:, 0:1])
                                v.tensor_copy(out=m8v[:, 1:2], in_=st_cnt[:, 0:1])
                                v.tensor_copy(out=m8v[:, 2:3], in_=st_lo[:, 0:1])
                                v.tensor_copy(out=m8v[:, 3:4], in_=st_w[:, 0:1]).then_inc(SEM["sV"], 1)
                            VE(_); C["sV"] += 1
                            def _(sy, vCur=C["sV"], col=col):
                                sy.wait_ge(SEM["sV"], vCur)
                                sy.dma_start(out=dbg["vk"][0:128, col:col + 4], in_=m8v[:, 0:4]).then_inc(SEM["dQ"], 16)
                            SY(_); C["dQ"] += 16

                def _(v, Ws=Ws, active=active, ss=ss, aCur=C["sA"]):
                    v.wait_ge(SEM["sA"], aCur)
                    v.tensor_tensor(out=st_hi[:, :], in0=st_w[:, :], in1=st_w[:, :], op=ALU.add).then_inc(SEM["sV"], 1)
                    C["sV"] += 1
                    v.wait_ge(SEM["sV"], C["sV"])
                    v.tensor_tensor(out=st_hi[:, :], in0=st_hi[:, :], in1=st_lo[:, :], op=ALU.add).then_inc(SEM["sV"], 1)
                    C["sV"] += 1
                    v.wait_ge(SEM["sV"], C["sV"])
                    for i in range(len(ss)):
                        s = ss[i]
                        if not active[i]:
                            v.memset(vk_t[:, i:i + 1], VK_INACT)
                            continue
                        v.tensor_scalar(out=slb[:, SLW * s:SLW * s + Ws[i]], in0=dsb[i][:, 0:Ws[i]],
                                        scalar1=st_hi[:, i:i + 1], scalar2=0.0,
                                        op0=ALU.is_ge, op1=ALU.add, accum_out=chi_t[:, i:i + 1]).then_inc(SEM["sV"], 1)
                        C["sV"] += 1
                        v.wait_ge(SEM["sV"], C["sV"])
                        v.scalar_tensor_tensor(out=wk32[:, 0:Ws[i]], in0=dsb[i][:, 0:Ws[i]],
                                               scalar=st_hi[:, i:i + 1], in1=dsb[i][:, 0:Ws[i]],
                                               op0=ALU.is_lt, op1=ALU.mult)
                        v.max(out=m8v[:, :], in_=wk32[:, 0:Ws[i]]).then_inc(SEM["sV"], 1)
                        C["sV"] += 1
                        v.wait_ge(SEM["sV"], C["sV"])
                        v.tensor_scalar(out=chi_t[:, i:i + 1], in0=chi_t[:, i:i + 1], scalar1=-1.0,
                                        scalar2=float(TOPK - 1), op0=ALU.mult, op1=ALU.add).then_inc(SEM["sV"], 1)
                        C["sV"] += 1
                        v.wait_ge(SEM["sV"], C["sV"])
                        v.tensor_scalar(out=chi_t[:, i:i + 1], in0=chi_t[:, i:i + 1], scalar1=0.0,
                                        scalar2=7.0, op0=ALU.max, op1=ALU.min).then_inc(SEM["sV"], 1)
                        C["sV"] += 1
                        v.wait_ge(SEM["sV"], C["sV"])
                        v.tensor_scalar(out=sel8[:, :], in0=idx8[:, :], scalar1=chi_t[:, i:i + 1],
                                        scalar2=None, op0=ALU.is_equal).then_inc(SEM["sV"], 1)
                        C["sV"] += 1
                        v.wait_ge(SEM["sV"], C["sV"])
                        v.tensor_tensor(out=sel8[:, :], in0=sel8[:, :], in1=m8v[:, :], op=ALU.mult).then_inc(SEM["sV"], 1)
                        C["sV"] += 1
                        v.wait_ge(SEM["sV"], C["sV"])
                        v.reduce_sum(out=vk_t[:, i:i + 1], in_=sel8[:, :], axis=AX.X)
                    v.engine_nop().then_inc(SEM["sV"], 1)
                    C["sV"] += 1
                    v.wait_ge(SEM["sV"], C["sV"])
                    v.tensor_scalar(out=nvk[:, :], in0=vk_t[:, :], scalar1=-1.0, scalar2=None, op0=ALU.mult)
                    v.engine_nop().then_inc(SEM["sV"], 1)
                VE(_); C["sV"] += 1
                vkS = C["sV"]
                if debug and c == 16 and p2 == 0:
                    def _(v, vkS=vkS):
                        v.wait_ge(SEM["sV"], vkS)
                        v.tensor_copy(out=sel8[:, 0:1], in_=vk_t[:, 0:1])
                        v.tensor_copy(out=sel8[:, 1:2], in_=st_hi[:, 0:1])
                        v.tensor_copy(out=sel8[:, 2:3], in_=chi_t[:, 0:1])
                        v.tensor_copy(out=sel8[:, 3:4], in_=m8v[:, 7:8]).then_inc(SEM["sV"], 1)
                    VE(_); C["sV"] += 1
                    def _(sy, vCur=C["sV"]):
                        sy.wait_ge(SEM["sV"], vCur)
                        sy.dma_start(out=dbg["vk"][0:128, 0:4], in_=sel8[:, 0:4]).then_inc(SEM["dQ"], 16)
                    SY(_); C["dQ"] += 16

                for i, s in enumerate(ss):
                    def _(sc, i=i, s=s, Ws=Ws, vkS=vkS):
                        sc.wait_ge(SEM["sV"], vkS)
                        sc.activation(out=d16[i][:, 0:Ws[i]], in_=dsb[i][:, 0:Ws[i]], func=AF.Exp,
                                      bias=nvk[:, i:i + 1], scale=1.0).then_inc(SEM["sA"], 1)
                    SC(_); C["sA"] += 1
                expA = C["sA"]
                def _(v, Ws=Ws, ss=ss, expA=expA):
                    v.wait_ge(SEM["sA"], expA)
                    for i, s in enumerate(ss):
                        v.scalar_tensor_tensor(out=slb[:, SLW * s:SLW * s + Ws[i]], in0=dsb[i][:, 0:Ws[i]],
                                               scalar=vk_t[:, i:i + 1], in1=d16[i][:, 0:Ws[i]],
                                               op0=ALU.is_ge, op1=ALU.mult, accum_out=zz[:, i:i + 1]).then_inc(SEM["sV"], 1)
                    C["sV"] += 1
                    v.wait_ge(SEM["sV"], C["sV"])
                    v.reciprocal(out=zr[:, :], in_=zz[:, :]).then_inc(SEM["sV"], 1)
                    C["sV"] += 1
                    v.wait_ge(SEM["sV"], C["sV"])
                    for i, s in enumerate(ss):
                        v.tensor_scalar(out=slb[:, SLW * s:SLW * s + Ws[i]],
                                        in0=slb[:, SLW * s:SLW * s + Ws[i]],
                                        scalar1=zr[:, i:i + 1], scalar2=None, op0=ALU.mult)
                    v.engine_nop().then_inc(SEM["sV"], 1)
                VE(_); C["sV"] += 1
                nS = C["sV"]
                if debug and c == 16 and p2 == 0:
                    def _(v, nS=nS):
                        v.wait_ge(SEM["sV"], nS)
                        v.tensor_copy(out=wk32[:, 0:SLW] if False else wk32[:, 0:WMAX], in_=slb[:, 0:WMAX]).then_inc(SEM["sV"], 1)
                    VE(_); C["sV"] += 1
                    def _(sy, vCur=C["sV"]):
                        sy.wait_ge(SEM["sV"], vCur)
                        sy.dma_start(out=dbg["at"], in_=wk32[:, 0:WMAX]).then_inc(SEM["dQ"], 16)
                    SY(_); C["dQ"] += 16

                for i, s in enumerate(ss):
                    nj = cdiv(Ws[i], 512)
                    for jc in range(nj):
                        jn = min(512, Ws[i] - 512 * jc)
                        ps = [pC, pD][jc % 2]
                        def _(te, s=s, jc=jc, jn=jn, ps=ps, nS=nS):
                            te.wait_ge(SEM["sV"], nS)
                            te.wait_ge(SEM["sA"], C["sA"])
                            te.matmul(ps[:, 0:jn], lhsT=postM[:, :],
                                      rhs=slb[:, SLW * s + 512 * jc:SLW * s + 512 * jc + jn],
                                      start=True, stop=True).then_inc(SEM["sT"], 1)
                        TE(_); C["sT"] += 1
                        tS = C["sT"]
                        def _(sc, s=s, jc=jc, jn=jn, ps=ps, tS=tS):
                            sc.wait_ge(SEM["sT"], tS)
                            sc.activation(out=slb[:, SLW * s + 512 * jc:SLW * s + 512 * jc + jn],
                                          in_=ps[:, 0:jn], func=AF.Copy).then_inc(SEM["sA"], 1)
                        SC(_); C["sA"] += 1
                pmA = C["sA"]

                for i, s in enumerate(ss):
                    njt = cdiv(Wc, 128)
                    for jt0 in range(0, njt, 2):
                        gi = giT[0]
                        gN = min(2, njt - jt0)
                        half = (gi % 2) * 256
                        pw = evQT.get(gi - 2)
                        def _(te, s=s, jt0=jt0, gN=gN, half=half, pmA=pmA, pw=pw):
                            te.wait_ge(SEM["sA"], pmA)
                            if pw is not None:
                                te.wait_ge(SEM["sP"], pw)
                            for u in range(gN):
                                mm = te.transpose(pT[:, half + 128 * u:half + 128 * (u + 1)],
                                                  in_=slb[:, SLW * s + 128 * (jt0 + u):SLW * s + 128 * (jt0 + u + 1)],
                                                  identity=idnb[:, :])
                            mm.then_inc(SEM["sT"], 1)
                        TE(_); C["sT"] += 1
                        tS = C["sT"]
                        def _(sc, s=s, jt0=jt0, gN=gN, half=half, tS=tS):
                            sc.wait_ge(SEM["sT"], tS)
                            btv = bt[:, :].rearrange("p (jt s q) -> p jt s q", s=4, q=128)
                            ptv = pT[:, :].rearrange("p (g q) -> p g q", q=128)
                            sc.activation(out=btv[:, jt0:jt0 + gN, s, :], in_=ptv[:, half // 128:half // 128 + gN, :],
                                          func=AF.Copy).then_inc(SEM["sP"], 1)
                        SC(_); C["sP"] += 1
                        evQT[gi] = C["sP"]
                        giT[0] = gi + 1

            trP = C["sP"]
            njt = cdiv(Wc, 128)
            for k in range(H):
                def _(te, k=k, njt=njt, trP=trP, c=c):
                    te.wait_ge(SEM["sP"], trP)
                    te.wait_ge(SEM["sA"], C["sA"])
                    btv = bt[:, :].rearrange("p (jt s q) -> p jt s q", s=4, q=128)
                    for jt in range(njt):
                        kk = min(128, Wc - 128 * jt)
                        mm = te.matmul(pV[0:64, 0:32], lhsT=VT[jt][0:kk, 64 * k:64 * (k + 1)],
                                       rhs=btv[0:kk, jt, :, 8 * k:8 * k + 8],
                                       start=(jt == 0), stop=(jt == njt - 1))
                    mm.then_inc(SEM["sT"], 1)
                TE(_); C["sT"] += 1
                tS = C["sT"]
                def _(sc, k=k, c=c, tS=tS):
                    sc.wait_ge(SEM["sT"], tS)
                    sc.activation(out=ctx_t[k // 2][64 * (k % 2):64 * (k % 2) + 64, 32 * (c % 4):32 * (c % 4) + 32],
                                  in_=pV[0:64, 0:32], func=AF.Copy).then_inc(SEM["sA"], 1)
                SC(_); C["sA"] += 1

            if (c + 1) % 4 == 0:
                tile = c // 4
                r0 = 128 * tile
                ctxA = C["sA"]
                def _(te, ctxA=ctxA):
                    te.wait_ge(SEM["sA"], ctxA)
                    for half in range(2):
                        for p in range(8):
                            te.matmul(pW[:, 512 * half:512 * (half + 1)], lhsT=ctx_t[p][:, :],
                                      rhs=WB[p][:, 512 * half:512 * (half + 1)], start=(p == 0), stop=False)
                        mm = te.matmul(pW[:, 512 * half:512 * (half + 1)], lhsT=ones1[:, :],
                                       rhs=bo_t[:, 512 * half:512 * (half + 1)], start=False, stop=True)
                    mm.then_inc(SEM["sT"], 1)
                TE(_); C["sT"] += 1
                tS = C["sT"]
                def _(sy, r0=r0):
                    sy.wait_ge(SEM["sV"], C["sV"])
                    sy.dma_start(out=xres[:, :], in_=hx[r0:r0 + 128, :]).then_inc(SEM["dQ"], 16)
                SY(_); C["dQ"] += 16
                xS = C["dQ"]
                def _(v, tS=tS, xS=xS):
                    v.wait_ge(SEM["sT"], tS)
                    v.wait_ge(SEM["dQ"], xS)
                    v.tensor_tensor(out=xres[:, :], in0=pW[:, :], in1=xres[:, :], op=ALU.add)
                    v.reduce_sum(out=mu_t[:, :], in_=xres[:, :], axis=AX.X).then_inc(SEM["sV"], 1)
                    C["sV"] += 1
                    v.wait_ge(SEM["sV"], C["sV"])
                    v.tensor_scalar(out=mu_t[:, :], in0=mu_t[:, :], scalar1=1.0 / D, scalar2=None, op0=ALU.mult).then_inc(SEM["sV"], 1)
                    C["sV"] += 1
                    v.wait_ge(SEM["sV"], C["sV"])
                    v.tensor_scalar(out=sq_t[:, :], in0=xres[:, :], scalar1=mu_t[:, :], scalar2=None,
                                    op0=ALU.subtract).then_inc(SEM["sV"], 1)
                VE(_); C["sV"] += 1
                vS = C["sV"]
                if debug and tile == 0:
                    def _(sy, vS=vS):
                        sy.wait_ge(SEM["sV"], vS)
                        sy.dma_start(out=dbg["y"], in_=xres[:, :]).then_inc(SEM["dQ"], 16)
                    SY(_); C["dQ"] += 16
                    def _(v):
                        v.tensor_copy(out=wk32[:, 0:2048], in_=bt[:, 0:2048]).then_inc(SEM["sV"], 1)
                    VE(_); C["sV"] += 1
                    def _(sy, vCur=C["sV"]):
                        sy.wait_ge(SEM["sV"], vCur)
                        sy.dma_start(out=dbg["bt"][0:128, 0:2048], in_=wk32[:, 0:2048]).then_inc(SEM["dQ"], 16)
                    SY(_); C["dQ"] += 16
                def _(sc, vS=vS):
                    sc.wait_ge(SEM["sV"], vS)
                    sc.activation(out=sq_t[:, :], in_=sq_t[:, :], func=AF.Square, accum_out=var_t[:, :]).then_inc(SEM["sA"], 1)
                SC(_); C["sA"] += 1
                aS = C["sA"]
                def _(sc, aS=aS):
                    sc.wait_ge(SEM["sA"], aS)
                    sc.activation(out=rs_t[:, :], in_=var_t[:, :], func=AF.Sqrt,
                                  scale=scD_t[:, :], bias=eps_t[:, :]).then_inc(SEM["sA"], 1)
                SC(_); C["sA"] += 1
                aS = C["sA"]
                def _(v, aS=aS, r0=r0, tile=tile):
                    v.wait_ge(SEM["sA"], aS)
                    if debug and tile == 1:
                        v.tensor_copy(out=m8v[:, 0:1], in_=mu_t[:, :])
                        v.tensor_copy(out=m8v[:, 1:2], in_=var_t[:, :])
                        v.tensor_copy(out=m8v[:, 2:3], in_=rs_t[:, :]).then_inc(SEM["sV"], 1)
                        C["sV"] += 1
                        v.wait_ge(SEM["sV"], C["sV"])
                    v.reciprocal(out=rs_t[:, :], in_=rs_t[:, :]).then_inc(SEM["sV"], 1)
                    C["sV"] += 1
                    v.wait_ge(SEM["sV"], C["sV"])
                    v.tensor_scalar(out=xres[:, :], in0=xres[:, :], scalar1=mu_t[:, :], scalar2=rs_t[:, :],
                                    op0=ALU.subtract, op1=ALU.mult).then_inc(SEM["sV"], 1)
                VE(_); C["sV"] += 1
                if debug and tile == 1:
                    def _(sy, vCur=C["sV"]):
                        sy.wait_ge(SEM["sV"], vCur)
                        sy.dma_start(out=dbg["vk"][0:128, 20:23], in_=m8v[:, 0:3]).then_inc(SEM["dQ"], 16)
                    SY(_); C["dQ"] += 16
                vS = C["sV"]
                def _(sy, vS=vS, r0=r0):
                    sy.wait_ge(SEM["sV"], vS)
                    sy.dma_start(out=ox1[r0:r0 + 128, :], in_=xres[:, :]).then_inc(SEM["dQ"], 16)
                SY(_); C["dQ"] += 16

                def _(v, vS=vS):
                    v.wait_ge(SEM["sV"], vS)
                    v.reduce_sum(out=mu_t[:, :], in_=xres[:, :], axis=AX.X).then_inc(SEM["sV"], 1)
                    C["sV"] += 1
                    v.wait_ge(SEM["sV"], C["sV"])
                    v.tensor_scalar(out=mu_t[:, :], in0=mu_t[:, :], scalar1=1.0 / D, scalar2=None, op0=ALU.mult).then_inc(SEM["sV"], 1)
                    C["sV"] += 1
                    v.wait_ge(SEM["sV"], C["sV"])
                    v.tensor_scalar(out=sq_t[:, :], in0=xres[:, :], scalar1=mu_t[:, :], scalar2=None,
                                    op0=ALU.subtract).then_inc(SEM["sV"], 1)
                VE(_); C["sV"] += 1
                vS = C["sV"]
                def _(sc, vS=vS):
                    sc.wait_ge(SEM["sV"], vS)
                    sc.activation(out=d16[0][:, 0:D], in_=sq_t[:, :], func=AF.Square, accum_out=var_t[:, :]).then_inc(SEM["sA"], 1)
                SC(_); C["sA"] += 1
                aS = C["sA"]
                def _(sc, aS=aS):
                    sc.wait_ge(SEM["sA"], aS)
                    sc.activation(out=rs_t[:, :], in_=var_t[:, :], func=AF.Sqrt, scale=scD_t[:, :],
                                  bias=eps_t[:, :]).then_inc(SEM["sA"], 1)
                SC(_); C["sA"] += 1
                aS = C["sA"]
                def _(v, aS=aS):
                    v.wait_ge(SEM["sA"], aS)
                    v.reciprocal(out=rs_t[:, :], in_=rs_t[:, :]).then_inc(SEM["sV"], 1)
                    C["sV"] += 1
                    v.wait_ge(SEM["sV"], C["sV"])
                    v.tensor_scalar(out=sq_t[:, :], in0=sq_t[:, :], scalar1=rs_t[:, :], scalar2=None, op0=ALU.mult).then_inc(SEM["sV"], 1)
                    C["sV"] += 1
                    v.wait_ge(SEM["sV"], C["sV"])
                    v.tensor_tensor(out=sq_t[:, :], in0=sq_t[:, :], in1=gr[:, :], op=ALU.mult).then_inc(SEM["sV"], 1)
                    C["sV"] += 1
                    v.wait_ge(SEM["sV"], C["sV"])
                    v.tensor_tensor(out=sq_t[:, :], in0=sq_t[:, :], in1=br[:, :], op=ALU.add).then_inc(SEM["sV"], 1)
                VE(_); C["sV"] += 1
                vS = C["sV"]
                def _(sy, vS=vS, r0=r0):
                    sy.wait_ge(SEM["sV"], vS)
                    sy.dma_start(out=oh[r0:r0 + 128, :], in_=sq_t[:, :]).then_inc(SEM["dS"], 16)
                SY(_); C["dS"] += 16
                def _(sc, vS=vS, dsc=C["dS"]):
                    sc.wait_ge(SEM["sV"], vS)
                    sc.wait_ge(SEM["dS"], dsc)
                    sc.activation(out=sq_t[:, :], in_=sq_t[:, :], func=AF.Abs).then_inc(SEM["sA"], 1)
                SC(_); C["sA"] += 1
                aS = C["sA"]
                def _(sy, aS=aS, r0=r0):
                    sy.wait_ge(SEM["sA"], aS)
                    sy.dma_start(out=ohabs[r0:r0 + 128, :], in_=sq_t[:, :]).then_inc(SEM["dQ"], 16)
                SY(_); C["dQ"] += 16
                def _(v, aS=aS, tile=tile):
                    v.wait_ge(SEM["sA"], aS)
                    for bn in range(NBIN):
                        op = v.tensor_scalar(out=(d16[0][:, 0:D] if bn % 2 == 0 else wk32[:, 0:D]),
                                             in0=sq_t[:, :], scalar1=QCAP1 * bn / NBIN, scalar2=0.0,
                                             op0=ALU.is_ge, op1=ALU.add,
                                             accum_out=qcnt[:, tile * NBIN + bn:tile * NBIN + bn + 1])
                        if bn % 2 == 1:
                            op.then_inc(SEM["sV"], 1)
                            C["sV"] += 1
                            v.wait_ge(SEM["sV"], C["sV"])
                    v.engine_nop().then_inc(SEM["sV"], 1)
                VE(_); C["sV"] += 1

        def _(v):
            v.wait_ge(SEM["sV"], C["sV"])
            v.reduce_sum(out=qred[:, :], in_=qcnt[:, :].rearrange("p (t b) -> p b t", b=NBIN), axis=AX.X).then_inc(SEM["sV"], 1)
        VE(_); C["sV"] += 1
        vS = C["sV"]
        onescol = sb("onescol", [128, 1], F32)
        def _(v):
            v.memset(onescol[:, :], 1.0).then_inc(SEM["sV"], 1)
        VE(_); C["sV"] += 1
        vS = C["sV"]
        def _(te, vS=vS):
            te.wait_ge(SEM["sV"], vS)
            te.matmul(pV[0:1, 0:NBIN], lhsT=onescol[:, :], rhs=qred[:, :], start=True, stop=True).then_inc(SEM["sT"], 1)
        TE(_); C["sT"] += 1
        tS = C["sT"]
        def _(v, tS=tS):
            v.wait_ge(SEM["sT"], tS)
            v.tensor_copy(out=qfin[:, :], in_=pV[0:1, 0:NBIN]).then_inc(SEM["sV"], 1)
        VE(_); C["sV"] += 1
        vS = C["sV"]

        if debug:
            pass

        def _(sy, vS=vS):
            sy.wait_ge(SEM["sV"], vS)
            sy.dma_start(out=ocnt, in_=qfin[:, :]).then_inc(SEM["dQ"], 16)
            sy.wait_ge(SEM["dQ"], C["dQ"] + 16)
        SY(_); C["dQ"] += 16

    return nc


def host_inputs_LA(inp, c):
    x = np.ascontiguousarray(np.asarray(inp["L0_attn_x"][c], dtype=np.float32))
    xT = np.ascontiguousarray(x.T)
    Wq = np.ascontiguousarray(np.asarray(inp["L0_attn_Wq"], np.float32).reshape(8, 128, INNER))
    Wk = np.ascontiguousarray(np.asarray(inp["L0_attn_Wk"], np.float32).reshape(8, 128, INNER))
    Wv = np.ascontiguousarray(np.asarray(inp["L0_attn_Wv"], np.float32).reshape(8, 128, INNER))
    Wo = np.ascontiguousarray(np.asarray(inp["L0_attn_Wo"], np.float32).reshape(8, 128, D))
    bo = np.asarray(inp["L0_attn_bo"], np.float32).reshape(1, D)
    pre = np.asarray(inp["L0_attn_pre_proj"], np.float32)
    post = np.asarray(inp["L0_attn_post_proj"], np.float32)
    preM = np.zeros((128, 128), np.float32)
    postM = np.zeros((128, 128), np.float32)
    for q in range(8):
        preM[16 * q:16 * q + 16, 16 * q:16 * q + 16] = pre
        for k in range(16):
            for k2 in range(16):
                postM[16 * q + k, 8 * k2 + q] = post[k, k2]
    mem_k = np.asarray(inp["L0_attn_mem_k"], np.float32)
    mem_v = np.asarray(inp["L0_attn_mem_v"], np.float32)
    memkP = np.zeros((128, 8 * MEM), np.float32)
    for g in range(8):
        for hh in range(2):
            memkP[hh * 64:hh * 64 + 64, g * MEM:(g + 1) * MEM] = mem_k[2 * g + hh].T
    memvT = np.ascontiguousarray(mem_v.transpose(1, 0, 2).reshape(MEM, INNER))
    mask7 = np.zeros((128, 8), np.float32)
    for p in range(128):
        for cc in range(7):
            mask7[p, cc] = 1.0 if (p >> 4) > cc else 0.0
    lng = np.asarray(inp["L0_ffn_ln_g"], np.float32).reshape(1, D)
    lnb = np.asarray(inp["L0_ffn_ln_b"], np.float32).reshape(1, D)
    return {"x": x, "xT": xT, "Wq": Wq, "Wk": Wk, "Wv": Wv, "Wo": Wo, "bo": bo,
            "preM": preM, "postM": postM, "memkP": memkP, "memvT": memvT,
            "mask7": mask7, "lng": lng, "lnb": lnb, "eye": np.eye(128, dtype=np.float32)}


NBR = 32


def build_round(R, CW, NB_in, khi):
    nc = bass.Bass("TRN2", target_bir_lowering=False, debug=False)
    hsrc = nc.dram_tensor("src", [R, CW], F32, kind="ExternalInput").ap()
    hcnt = nc.dram_tensor("cin", [8, NB_in], F32, kind="ExternalInput").ap()
    hbrk = nc.dram_tensor("bin", [1, 2], F32, kind="ExternalInput").ap()
    ocnt = nc.dram_tensor("cout", [1, NBR], F32, kind="ExternalOutput").ap()
    obrk = nc.dram_tensor("bout", [1, 2], F32, kind="ExternalOutput").ap()
    ntile = R // 128
    with ExitStack() as ctx:
        def sb(name, shape, dtype=F32):
            return ctx.enter_context(nc.sbuf_tensor(name, shape, dtype))
        stg = [sb(f"stg{i}", [128, CW], F32) for i in range(2)]
        ab = [sb(f"ab{i}", [128, CW], F32) for i in range(2)]
        scrN = [sb(f"scr{i}", [128, CW], F32) for i in range(4)]
        ct = sb("ct", [8, NB_in])
        bk = sb("bk", [1, 2])
        csum = sb("csum", [1, NB_in])
        ind = sb("ind", [1, NB_in])
        lo_t = sb("lo_t", [1, 1])
        hi_t = sb("hi_t", [1, 1])
        st_t = sb("st_t", [1, 1])
        t1 = sb("t1", [1, 1])
        ones8 = sb("ones8", [8, 1])
        onesc = sb("onesc", [1, 128])
        onescol = sb("onescol", [128, 1])
        thr = sb("thr", [128, NBR])
        qc = sb("qc", [128, NBR])
        qr2 = sb("qr2", [128, NBR])
        qr = sb("qr", [1, NBR])
        pS = ctx.enter_context(nc.psum_tensor("pS", [128, 512], F32))
        SEM = {}
        for k in ("dQ", "sT", "sV", "sA"):
            SEM[k] = ctx.enter_context(nc.semaphore(k))
        C = {k: 0 for k in SEM}
        block = ctx.enter_context(nc.Block())
        SY, SC, VE, TE = block.sync, block.scalar, block.vector, block.tensor

        def vbrk(v):
            C["sV"] += 1
            v.wait_ge(SEM["sV"], C["sV"])

        def _(sy):
            sy.dma_start(out=ct[:, :], in_=hcnt).then_inc(SEM["dQ"], 16)
            sy.dma_start(out=bk[:, :], in_=hbrk).then_inc(SEM["dQ"], 16)
        SY(_); C["dQ"] += 32

        def _(v):
            v.wait_ge(SEM["dQ"], C["dQ"])
            v.memset(ones8[:, :], 1.0)
            v.memset(onesc[:, :], 1.0)
            v.memset(onescol[:, :], 1.0).then_inc(SEM["sV"], 1)
        VE(_); C["sV"] += 1
        def _(te):
            te.wait_ge(SEM["sV"], C["sV"])
            te.matmul(pS[0:1, 0:NB_in], lhsT=ones8[:, :], rhs=ct[:, :], start=True, stop=True).then_inc(SEM["sT"], 1)
        TE(_); C["sT"] += 1
        def _(v, khi=khi):
            v.wait_ge(SEM["sT"], C["sT"])
            v.tensor_copy(out=csum[:, :], in_=pS[0:1, 0:NB_in]).then_inc(SEM["sV"], 1)
            vbrk(v)
            v.tensor_scalar(out=ind[:, :], in0=csum[:, :], scalar1=float(khi) + 0.5, scalar2=None,
                            op0=ALU.is_ge).then_inc(SEM["sV"], 1)
            vbrk(v)
            v.reduce_sum(out=t1[:, :], in_=ind[:, :], axis=AX.X).then_inc(SEM["sV"], 1)
            vbrk(v)
            v.tensor_scalar(out=t1[:, :], in0=t1[:, :], scalar1=-2.0, scalar2=0.0, op0=ALU.add, op1=ALU.max).then_inc(SEM["sV"], 1)
            vbrk(v)
            v.tensor_tensor(out=lo_t[:, :], in0=t1[:, :], in1=bk[0:1, 1:2], op=ALU.mult).then_inc(SEM["sV"], 1)
            vbrk(v)
            v.tensor_tensor(out=lo_t[:, :], in0=lo_t[:, :], in1=bk[0:1, 0:1], op=ALU.add).then_inc(SEM["sV"], 1)
            vbrk(v)
            v.tensor_scalar(out=ind[:, :], in0=csum[:, :], scalar1=float(khi) - 0.5, scalar2=None,
                            op0=ALU.is_ge).then_inc(SEM["sV"], 1)
            vbrk(v)
            v.reduce_sum(out=t1[:, :], in_=ind[:, :], axis=AX.X).then_inc(SEM["sV"], 1)
            vbrk(v)
            v.tensor_scalar(out=t1[:, :], in0=t1[:, :], scalar1=1.0, scalar2=None, op0=ALU.add).then_inc(SEM["sV"], 1)
            vbrk(v)
            v.tensor_tensor(out=hi_t[:, :], in0=t1[:, :], in1=bk[0:1, 1:2], op=ALU.mult).then_inc(SEM["sV"], 1)
            vbrk(v)
            v.tensor_tensor(out=hi_t[:, :], in0=hi_t[:, :], in1=bk[0:1, 0:1], op=ALU.add).then_inc(SEM["sV"], 1)
            vbrk(v)
            v.tensor_tensor(out=st_t[:, :], in0=hi_t[:, :], in1=lo_t[:, :], op=ALU.subtract).then_inc(SEM["sV"], 1)
            vbrk(v)
            v.tensor_scalar(out=st_t[:, :], in0=st_t[:, :], scalar1=1.0 / NBR, scalar2=None, op0=ALU.mult).then_inc(SEM["sV"], 1)
            vbrk(v)
            for b in range(NBR):
                v.tensor_scalar(out=qr[0:1, b:b + 1], in0=st_t[:, :], scalar1=float(b), scalar2=None, op0=ALU.mult)
            v.engine_nop().then_inc(SEM["sV"], 1)
            vbrk(v)
            v.tensor_scalar(out=qr[:, :], in0=qr[:, :], scalar1=lo_t[0:1, 0:1], scalar2=None, op0=ALU.add).then_inc(SEM["sV"], 1)
        VE(_); C["sV"] += 1
        def _(te):
            te.wait_ge(SEM["sV"], C["sV"])
            te.matmul(pS[:, 0:NBR], lhsT=onesc[:, :], rhs=qr[:, :], start=True, stop=True).then_inc(SEM["sT"], 1)
        TE(_); C["sT"] += 1
        def _(v):
            v.wait_ge(SEM["sT"], C["sT"])
            v.tensor_copy(out=thr[:, :], in_=pS[:, 0:NBR]).then_inc(SEM["sV"], 1)
        VE(_); C["sV"] += 1
        thrS = C["sV"]

        for t in range(ntile):
            def _(sy, t=t):
                if t >= 2:
                    sy.wait_ge(SEM["sA"], C["sA"])
                sy.dma_start(out=stg[t % 2][:, :], in_=hsrc[128 * t:128 * (t + 1), :]).then_inc(SEM["dQ"], 16)
            SY(_); C["dQ"] += 16
            dq = C["dQ"]
            def _(sc, t=t, dq=dq):
                sc.wait_ge(SEM["dQ"], dq)
                sc.wait_ge(SEM["sV"], C["sV"])
                sc.activation(out=ab[t % 2][:, :], in_=stg[t % 2][:, :], func=AF.Abs).then_inc(SEM["sA"], 1)
            SC(_); C["sA"] += 1
            aS = C["sA"]
            def _(v, t=t, aS=aS, thrS=thrS):
                v.wait_ge(SEM["sA"], aS)
                v.wait_ge(SEM["sV"], C["sV"])
                for b in range(NBR):
                    op = v.tensor_scalar(out=scrN[b % 4][:, :], in0=ab[t % 2][:, :],
                                         scalar1=thr[:, b:b + 1], scalar2=0.0,
                                         op0=ALU.is_ge, op1=ALU.add,
                                         accum_out=(qc if t == 0 else qr2)[:, b:b + 1])
                    if b % 4 == 3:
                        op.then_inc(SEM["sV"], 1)
                        vbrk(v)
                v.engine_nop().then_inc(SEM["sV"], 1)
                vbrk(v)
                if t > 0:
                    v.tensor_tensor(out=qc[:, :], in0=qc[:, :], in1=qr2[:, :], op=ALU.add).then_inc(SEM["sV"], 1)
                    C["sV"] += 1
                    v.wait_ge(SEM["sV"], C["sV"])
                v.engine_nop().then_inc(SEM["sV"], 1)
            VE(_); C["sV"] += 1

        def _(te):
            te.wait_ge(SEM["sV"], C["sV"])
            te.matmul(pS[0:1, 0:NBR], lhsT=onescol[:, :], rhs=qc[:, :], start=True, stop=True).then_inc(SEM["sT"], 1)
        TE(_); C["sT"] += 1
        def _(v):
            v.wait_ge(SEM["sT"], C["sT"])
            v.tensor_copy(out=qr[:, :], in_=pS[0:1, 0:NBR])
            v.tensor_copy(out=bk[0:1, 0:1], in_=lo_t[:, :])
            v.tensor_copy(out=bk[0:1, 1:2], in_=st_t[:, :]).then_inc(SEM["sV"], 1)
        VE(_); C["sV"] += 1
        def _(sy):
            sy.wait_ge(SEM["sV"], C["sV"])
            sy.dma_start(out=ocnt, in_=qr[:, :]).then_inc(SEM["dQ"], 16)
            sy.dma_start(out=obrk, in_=bk[:, :]).then_inc(SEM["dQ"], 16)
            sy.wait_ge(SEM["dQ"], C["dQ"] + 32)
        SY(_); C["dQ"] += 32
    return nc


def build_extract(R, CW, NB_in, khi, kext=KEXT, candw=64):
    nc = bass.Bass("TRN2", target_bir_lowering=False, debug=False)
    hsrc = nc.dram_tensor("src", [R, CW], F32, kind="ExternalInput").ap()
    hcnt = nc.dram_tensor("cin", [8, NB_in], F32, kind="ExternalInput").ap()
    hbrk = nc.dram_tensor("bin", [1, 2], F32, kind="ExternalInput").ap()
    ocand = nc.dram_tensor("cand", [128, candw], F32, kind="ExternalOutput").ap()
    ometa = nc.dram_tensor("meta", [1, 4], F32, kind="ExternalOutput").ap()
    ntile = R // 128
    SLOTS = 8 * kext
    with ExitStack() as ctx:
        def sb(name, shape, dtype=F32):
            return ctx.enter_context(nc.sbuf_tensor(name, shape, dtype))
        stg = [sb(f"stg{i}", [128, CW], F32) for i in range(2)]
        ab = [sb(f"ab{i}", [128, CW], F32) for i in range(2)]
        scr = sb("scr", [128, CW])
        stash = sb("stash", [128, ntile * SLOTS])
        ct = sb("ct", [8, NB_in])
        bk = sb("bk", [1, 2])
        csum = sb("csum", [1, NB_in])
        ind = sb("ind", [1, NB_in])
        lo_t = sb("lo_t", [1, 1]); hi_t = sb("hi_t", [1, 1]); t1 = sb("t1", [1, 1])
        lob = sb("lob", [128, 1]); hib = sb("hib", [128, 1])
        ones8 = sb("ones8", [8, 1]); onesc = sb("onesc", [1, 128]); onescol = sb("onescol", [128, 1])
        cand = sb("candt", [128, candw])
        chl = sb("chl", [128, 1])
        chacc = sb("chacc", [128, 1])
        meta = sb("metat", [1, 4])
        pS = ctx.enter_context(nc.psum_tensor("pS", [128, 128], F32))
        SEM = {}
        for k in ("dQ", "sT", "sV", "sA"):
            SEM[k] = ctx.enter_context(nc.semaphore(k))
        C = {k: 0 for k in SEM}
        block = ctx.enter_context(nc.Block())
        SY, SC, VE, TE = block.sync, block.scalar, block.vector, block.tensor

        def vbrk(v):
            C["sV"] += 1
            v.wait_ge(SEM["sV"], C["sV"])

        def _(sy):
            sy.dma_start(out=ct[:, :], in_=hcnt).then_inc(SEM["dQ"], 16)
            sy.dma_start(out=bk[:, :], in_=hbrk).then_inc(SEM["dQ"], 16)
        SY(_); C["dQ"] += 32
        def _(v):
            v.wait_ge(SEM["dQ"], C["dQ"])
            v.memset(ones8[:, :], 1.0)
            v.memset(onesc[:, :], 1.0)
            v.memset(onescol[:, :], 1.0)
            v.memset(stash[:, :], 0.0)
            v.memset(chacc[:, :], 0.0).then_inc(SEM["sV"], 1)
        VE(_); C["sV"] += 1
        def _(te):
            te.wait_ge(SEM["sV"], C["sV"])
            te.matmul(pS[0:1, 0:NB_in], lhsT=ones8[:, :], rhs=ct[:, :], start=True, stop=True).then_inc(SEM["sT"], 1)
        TE(_); C["sT"] += 1
        def _(v, khi=khi):
            v.wait_ge(SEM["sT"], C["sT"])
            v.tensor_copy(out=csum[:, :], in_=pS[0:1, 0:NB_in]).then_inc(SEM["sV"], 1)
            vbrk(v)
            v.tensor_scalar(out=ind[:, :], in0=csum[:, :], scalar1=float(khi) + 0.5, scalar2=None, op0=ALU.is_ge).then_inc(SEM["sV"], 1)
            vbrk(v)
            v.reduce_sum(out=t1[:, :], in_=ind[:, :], axis=AX.X).then_inc(SEM["sV"], 1)
            vbrk(v)
            v.tensor_scalar(out=t1[:, :], in0=t1[:, :], scalar1=-2.0, scalar2=0.0, op0=ALU.add, op1=ALU.max).then_inc(SEM["sV"], 1)
            vbrk(v)
            v.tensor_tensor(out=lo_t[:, :], in0=t1[:, :], in1=bk[0:1, 1:2], op=ALU.mult).then_inc(SEM["sV"], 1)
            vbrk(v)
            v.tensor_tensor(out=lo_t[:, :], in0=lo_t[:, :], in1=bk[0:1, 0:1], op=ALU.add).then_inc(SEM["sV"], 1)
            vbrk(v)
            v.tensor_scalar(out=ind[:, :], in0=csum[:, :], scalar1=float(khi) - 0.5, scalar2=None, op0=ALU.is_ge).then_inc(SEM["sV"], 1)
            vbrk(v)
            v.reduce_sum(out=t1[:, :], in_=ind[:, :], axis=AX.X).then_inc(SEM["sV"], 1)
            vbrk(v)
            v.tensor_scalar(out=t1[:, :], in0=t1[:, :], scalar1=1.0, scalar2=None, op0=ALU.add).then_inc(SEM["sV"], 1)
            vbrk(v)
            v.tensor_tensor(out=hi_t[:, :], in0=t1[:, :], in1=bk[0:1, 1:2], op=ALU.mult).then_inc(SEM["sV"], 1)
            vbrk(v)
            v.tensor_tensor(out=hi_t[:, :], in0=hi_t[:, :], in1=bk[0:1, 0:1], op=ALU.add).then_inc(SEM["sV"], 1)
            vbrk(v)
            v.engine_nop().then_inc(SEM["sV"], 1)
        VE(_); C["sV"] += 1
        def _(te):
            te.wait_ge(SEM["sV"], C["sV"])
            te.matmul(pS[:, 0:1], lhsT=onesc[:, :], rhs=lo_t[:, :], start=True, stop=True)
            te.matmul(pS[:, 1:2], lhsT=onesc[:, :], rhs=hi_t[:, :], start=True, stop=True).then_inc(SEM["sT"], 1)
        TE(_); C["sT"] += 1
        def _(v):
            v.wait_ge(SEM["sT"], C["sT"])
            v.tensor_copy(out=lob[:, :], in_=pS[:, 0:1])
            v.tensor_copy(out=hib[:, :], in_=pS[:, 1:2]).then_inc(SEM["sV"], 1)
        VE(_); C["sV"] += 1
        bcS = C["sV"]

        for t in range(ntile):
            def _(sy, t=t):
                if t >= 2:
                    sy.wait_ge(SEM["sA"], C["sA"])
                sy.dma_start(out=stg[t % 2][:, :], in_=hsrc[128 * t:128 * (t + 1), :]).then_inc(SEM["dQ"], 16)
            SY(_); C["dQ"] += 16
            dq = C["dQ"]
            def _(sc, t=t, dq=dq):
                sc.wait_ge(SEM["dQ"], dq)
                sc.wait_ge(SEM["sV"], C["sV"])
                sc.activation(out=ab[t % 2][:, :], in_=stg[t % 2][:, :], func=AF.Abs).then_inc(SEM["sA"], 1)
            SC(_); C["sA"] += 1
            aS = C["sA"]
            def _(v, t=t, aS=aS, bcS=bcS):
                v.wait_ge(SEM["sA"], aS)
                v.wait_ge(SEM["sV"], C["sV"])
                v.tensor_scalar(out=scr[:, :], in0=ab[t % 2][:, :], scalar1=hib[:, :], scalar2=0.0,
                                op0=ALU.is_ge, op1=ALU.add, accum_out=chl[:, :]).then_inc(SEM["sV"], 1)
                vbrk(v)
                v.tensor_tensor(out=chacc[:, :], in0=chacc[:, :], in1=chl[:, :], op=ALU.add).then_inc(SEM["sV"], 1)
                vbrk(v)
                v.scalar_tensor_tensor(out=scr[:, :], in0=ab[t % 2][:, :], scalar=hib[:, :],
                                       in1=ab[t % 2][:, :], op0=ALU.is_lt, op1=ALU.mult).then_inc(SEM["sV"], 1)
                vbrk(v)
                v.scalar_tensor_tensor(out=scr[:, :], in0=scr[:, :], scalar=lob[:, :],
                                       in1=scr[:, :], op0=ALU.is_ge, op1=ALU.mult).then_inc(SEM["sV"], 1)
                vbrk(v)
                for r in range(kext):
                    v.max(out=stash[:, t * SLOTS + 8 * r:t * SLOTS + 8 * r + 8], in_=scr[:, :]).then_inc(SEM["sV"], 1)
                    vbrk(v)
                    if r < kext - 1:
                        v.match_replace(out=scr[:, :], in_to_replace=stash[:, t * SLOTS + 8 * r:t * SLOTS + 8 * r + 8],
                                        in_values=scr[:, :], imm_value=0.0).then_inc(SEM["sV"], 1)
                        vbrk(v)
                v.engine_nop().then_inc(SEM["sV"], 1)
            VE(_); C["sV"] += 1

        def _(v):
            v.wait_ge(SEM["sV"], C["sV"])
            for r in range(candw // 8):
                v.max(out=cand[:, 8 * r:8 * r + 8], in_=stash[:, :]).then_inc(SEM["sV"], 1)
                vbrk(v)
                if r < candw // 8 - 1:
                    v.match_replace(out=stash[:, :], in_to_replace=cand[:, 8 * r:8 * r + 8],
                                    in_values=stash[:, :], imm_value=0.0).then_inc(SEM["sV"], 1)
                    vbrk(v)
            v.engine_nop().then_inc(SEM["sV"], 1)
        VE(_); C["sV"] += 1

        def _(te):
            te.wait_ge(SEM["sV"], C["sV"])
            te.matmul(pS[0:1, 0:1], lhsT=onescol[:, :], rhs=chacc[:, :], start=True, stop=True).then_inc(SEM["sT"], 1)
        TE(_); C["sT"] += 1
        def _(v):
            v.wait_ge(SEM["sT"], C["sT"])
            v.tensor_copy(out=meta[0:1, 0:1], in_=pS[0:1, 0:1])
            v.tensor_copy(out=meta[0:1, 1:2], in_=lo_t[:, :])
            v.tensor_copy(out=meta[0:1, 2:3], in_=hi_t[:, :])
            v.memset(meta[0:1, 3:4], 0.0).then_inc(SEM["sV"], 1)
        VE(_); C["sV"] += 1
        def _(sy):
            sy.wait_ge(SEM["sV"], C["sV"])
            sy.dma_start(out=ocand, in_=cand[:, :]).then_inc(SEM["dQ"], 16)
            sy.dma_start(out=ometa, in_=meta[:, :]).then_inc(SEM["dQ"], 16)
            sy.wait_ge(SEM["dQ"], C["dQ"] + 32)
        SY(_); C["dQ"] += 32
    return nc


def emit_select(nc, block, SEM, C, sb, pS, cands_ap, metas_ap, khi, frac, amax_out, candw=64, dq="dQ"):
    SY, SC, VE, TE = block.sync, block.scalar, block.vector, block.tensor
    cd = sb("sel_cd", [128, 8 * candw])
    mt = sb("sel_mt", [8, 4])
    o8 = sb("sel_o8", [8, 1])
    oc = sb("sel_oc", [1, 128])
    ocol = sb("sel_ocol", [128, 1])
    scr = sb("sel_scr", [128, 8 * candw])
    pc = sb("sel_pc", [128, 2])
    midb = sb("sel_midb", [128, 2])
    lo = sb("sel_lo", [1, 2]); w = sb("sel_w", [1, 2]); mid = sb("sel_mid", [1, 2])
    u = sb("sel_u", [1, 2]); sel = sb("sel_sel", [1, 2])
    cg = sb("sel_cg", [1, 2]); thrv = sb("sel_thrv", [1, 2]); chg = sb("sel_chg", [1, 1])
    d01 = sb("sel_d01", [1, 1])

    def vbrk(v):
        C["sV"] += 1
        v.wait_ge(SEM["sV"], C["sV"])

    def _(sy):
        for c in range(8):
            sy.dma_start(out=cd[:, candw * c:candw * (c + 1)], in_=cands_ap[128 * c:128 * (c + 1), :]).then_inc(SEM[dq], 16)
        sy.dma_start(out=mt[:, :], in_=metas_ap).then_inc(SEM[dq], 16)
    SY(_); C[dq] += 9 * 16
    dq0 = C[dq]
    def _(v):
        v.wait_ge(SEM[dq], dq0)
        v.memset(o8[:, :], 1.0)
        v.memset(oc[:, :], 1.0)
        v.memset(ocol[:, :], 1.0)
        v.memset(lo[:, :], 0.0)
        v.memset(w[:, :], 16.0).then_inc(SEM["sV"], 1)
    VE(_); C["sV"] += 1
    def _(te):
        te.wait_ge(SEM["sV"], C["sV"])
        te.matmul(pS[0:1, 4:5], lhsT=o8[:, :], rhs=mt[:, 0:1], start=True, stop=True).then_inc(SEM["sT"], 1)
    TE(_); C["sT"] += 1
    def _(v, khi=khi):
        v.wait_ge(SEM["sT"], C["sT"])
        v.tensor_copy(out=chg[:, :], in_=pS[0:1, 4:5]).then_inc(SEM["sV"], 1)
        vbrk(v)
        v.tensor_scalar(out=thrv[0:1, 0:1], in0=chg[:, :], scalar1=-1.0, scalar2=float(khi) + 0.5,
                        op0=ALU.mult, op1=ALU.add)
        v.tensor_scalar(out=thrv[0:1, 1:2], in0=chg[:, :], scalar1=-1.0, scalar2=float(khi) - 0.5,
                        op0=ALU.mult, op1=ALU.add).then_inc(SEM["sV"], 1)
        vbrk(v)
        v.engine_nop().then_inc(SEM["sV"], 1)
    VE(_); C["sV"] += 1

    for it in range(NIT_SEL):
        def _(v):
            v.wait_ge(SEM["sV"], C["sV"])
            v.tensor_tensor(out=mid[:, :], in0=lo[:, :], in1=w[:, :], op=ALU.add).then_inc(SEM["sV"], 1)
        VE(_); C["sV"] += 1
        def _(te):
            te.wait_ge(SEM["sV"], C["sV"])
            te.matmul(pS[:, 0:2], lhsT=oc[:, :], rhs=mid[:, :], start=True, stop=True).then_inc(SEM["sT"], 1)
        TE(_); C["sT"] += 1
        def _(v):
            v.wait_ge(SEM["sT"], C["sT"])
            v.tensor_copy(out=midb[:, :], in_=pS[:, 0:2]).then_inc(SEM["sV"], 1)
            vbrk(v)
            v.tensor_scalar(out=scr[:, :], in0=cd[:, :], scalar1=midb[:, 0:1], scalar2=0.0,
                            op0=ALU.is_ge, op1=ALU.add, accum_out=pc[:, 0:1]).then_inc(SEM["sV"], 1)
            vbrk(v)
            v.tensor_scalar(out=scr[:, :], in0=cd[:, :], scalar1=midb[:, 1:2], scalar2=0.0,
                            op0=ALU.is_ge, op1=ALU.add, accum_out=pc[:, 1:2]).then_inc(SEM["sV"], 1)
        VE(_); C["sV"] += 1
        def _(te):
            te.wait_ge(SEM["sV"], C["sV"])
            te.matmul(pS[0:1, 2:4], lhsT=ocol[:, :], rhs=pc[:, :], start=True, stop=True).then_inc(SEM["sT"], 1)
        TE(_); C["sT"] += 1
        def _(v):
            v.wait_ge(SEM["sT"], C["sT"])
            v.tensor_copy(out=cg[:, :], in_=pS[0:1, 2:4]).then_inc(SEM["sV"], 1)
            vbrk(v)
            v.tensor_tensor(out=sel[:, :], in0=cg[:, :], in1=thrv[:, :], op=ALU.is_ge).then_inc(SEM["sV"], 1)
            vbrk(v)
            v.tensor_tensor(out=u[:, :], in0=sel[:, :], in1=w[:, :], op=ALU.mult).then_inc(SEM["sV"], 1)
            vbrk(v)
            v.tensor_tensor(out=lo[:, :], in0=lo[:, :], in1=u[:, :], op=ALU.add).then_inc(SEM["sV"], 1)
            vbrk(v)
            v.tensor_scalar(out=w[:, :], in0=w[:, :], scalar1=0.5, scalar2=None, op0=ALU.mult).then_inc(SEM["sV"], 1)
        VE(_); C["sV"] += 1

    def _(v, frac=frac):
        v.wait_ge(SEM["sV"], C["sV"])
        v.tensor_tensor(out=d01[:, :], in0=lo[0:1, 1:2], in1=lo[0:1, 0:1], op=ALU.subtract).then_inc(SEM["sV"], 1)
        vbrk(v)
        v.tensor_scalar(out=d01[:, :], in0=d01[:, :], scalar1=float(frac), scalar2=None, op0=ALU.mult).then_inc(SEM["sV"], 1)
        vbrk(v)
        v.tensor_tensor(out=d01[:, :], in0=d01[:, :], in1=lo[0:1, 0:1], op=ALU.add).then_inc(SEM["sV"], 1)
        vbrk(v)
        v.tensor_scalar(out=amax_out[:, :], in0=d01[:, :], scalar1=1e-5, scalar2=None, op0=ALU.max).then_inc(SEM["sV"], 1)
    VE(_); C["sV"] += 1
CANDW1, CANDW2 = 128, 256


def build_LD():
    nc = bass.Bass("TRN2", target_bir_lowering=False, debug=False)
    hcand = nc.dram_tensor("cands", [8 * 128, CANDW1], F32, kind="ExternalInput").ap()
    hmeta = nc.dram_tensor("metas", [8, 4], F32, kind="ExternalInput").ap()
    hh = nc.dram_tensor("hsig", [N, D], F32, kind="ExternalInput").ap()
    hw1 = nc.dram_tensor("w1T", [8, 128, FF], F32, kind="ExternalInput").ap()
    heye = nc.dram_tensor("eye", [128, 128], F32, kind="ExternalInput").ap()
    oh1 = nc.dram_tensor("h1", [FF, N], F32, kind="ExternalOutput").ap()
    ocnt = nc.dram_tensor("cnt2", [1, NBIN], F32, kind="ExternalOutput").ap()
    oam = nc.dram_tensor("amax1", [1, 2], F32, kind="ExternalOutput").ap()
    with ExitStack() as ctx:
        def sb(name, shape, dtype=F32):
            return ctx.enter_context(nc.sbuf_tensor(name, shape, dtype))
        SEM = {}
        for k in ("dQ", "dS", "sT", "sV", "sA", "sP"):
            SEM[k] = ctx.enter_context(nc.semaphore(k))
        C = {k: 0 for k in SEM}
        pS = ctx.enter_context(nc.psum_tensor("pS", [128, 128], F32))
        pM = ctx.enter_context(nc.psum_tensor("pM", [128, 512], F32))
        pM2 = ctx.enter_context(nc.psum_tensor("pM2", [128, 512], F32))
        pT = ctx.enter_context(nc.psum_tensor("pT", [128, 512], BF16))
        block = ctx.enter_context(nc.Block())
        SY, SC, VE, TE = block.sync, block.scalar, block.vector, block.tensor
        GP = block.gpsimd
        amax = sb("amax", [1, 1])
        idnb = sb("idnbL", [128, 128], BF16)
        eyest = sb("eyestL", [128, 128], F32)
        def _(sy):
            sy.dma_start(out=eyest[:, :], in_=heye).then_inc(SEM["dS"], 16)
        SY(_); C["dS"] += 16

        def vbrk(v):
            C["sV"] += 1
            v.wait_ge(SEM["sV"], C["sV"])

        emit_select(nc, block, SEM, C, sb, pS, hcand, hmeta, KHI1, FRAC1, amax, candw=CANDW1, dq="dS")

        def _(v):
            v.wait_ge(SEM["dS"], 160)
            v.tensor_copy(out=idnb[:, :], in_=eyest[:, :]).then_inc(SEM["sV"], 1)
        VE(_); C["sV"] += 1

        s1b = sb("s1b", [128, 1])
        gs1b = sb("gs1b", [128, 1])
        rg1b = sb("rg1b", [128, 1])
        t1 = sb("t1b", [1, 1]); t2 = sb("t2b", [1, 1])
        onesc = sb("onescL", [1, 128]); onescol = sb("onescolL", [128, 1])
        def _(v):
            v.wait_ge(SEM["sV"], C["sV"])
            v.memset(onesc[:, :], 1.0)
            v.memset(onescol[:, :], 1.0)
            v.reciprocal(out=t1[:, :], in_=amax[:, :]).then_inc(SEM["sV"], 1)
            vbrk(v)
            v.tensor_scalar(out=t1[:, :], in0=t1[:, :], scalar1=127.0, scalar2=None, op0=ALU.mult).then_inc(SEM["sV"], 1)
            vbrk(v)
            v.engine_nop().then_inc(SEM["sV"], 1)
        VE(_); C["sV"] += 1
        def _(te):
            te.wait_ge(SEM["sV"], C["sV"])
            te.matmul(pS[:, 6:7], lhsT=onesc[:, :], rhs=t1[:, :], start=True, stop=True).then_inc(SEM["sT"], 1)
        TE(_); C["sT"] += 1
        def _(v):
            v.wait_ge(SEM["sT"], C["sT"])
            v.tensor_copy(out=s1b[:, :], in_=pS[:, 6:7]).then_inc(SEM["sV"], 1)
        VE(_); C["sV"] += 1

        wstg = [sb(f"wstgL{i}", [128, FF], F32) for i in range(2)]
        w1s = [sb(f"w1s{d}", [128, FF], BF16) for d in range(8)]
        wi = sb("wiL", [128, FF], I32)
        gsum = sb("gsumL", [128, 8])
        g1 = sb("g1L", [1, 1])
        for d in range(8):
            def _(sy, d=d):
                if d >= 1:
                    sy.wait_ge(SEM["sA"], C["sA"])
                sy.dma_start(out=wstg[d % 2][:, :], in_=hw1[d]).then_inc(SEM["dQ"], 16)
            SY(_); C["dQ"] += 16
            dq = C["dQ"]
            def _(sc, d=d, dq=dq):
                sc.wait_ge(SEM["dQ"], dq)
                sc.activation(out=wstg[d % 2][:, :], in_=wstg[d % 2][:, :], func=AF.Abs,
                              accum_out=gsum[:, d:d + 1]).then_inc(SEM["sA"], 1)
            SC(_); C["sA"] += 1
        def _(v):
            v.wait_ge(SEM["sA"], C["sA"])
            v.reduce_sum(out=gsum[:, 0:1], in_=gsum[:, :], axis=AX.X).then_inc(SEM["sV"], 1)
            vbrk(v)
            v.engine_nop().then_inc(SEM["sV"], 1)
        VE(_); C["sV"] += 1
        def _(te):
            te.wait_ge(SEM["sV"], C["sV"])
            te.matmul(pS[0:1, 7:8], lhsT=onescol[:, :], rhs=gsum[:, 0:1], start=True, stop=True).then_inc(SEM["sT"], 1)
        TE(_); C["sT"] += 1
        def _(v):
            v.wait_ge(SEM["sT"], C["sT"])
            v.tensor_copy(out=g1[:, :], in_=pS[0:1, 7:8]).then_inc(SEM["sV"], 1)
            vbrk(v)
            v.tensor_scalar(out=g1[:, :], in0=g1[:, :], scalar1=1.0 / (FF * D), scalar2=1e-5,
                            op0=ALU.mult, op1=ALU.add).then_inc(SEM["sV"], 1)
            vbrk(v)
            v.tensor_tensor(out=t2[:, :], in0=g1[:, :], in1=amax[:, :], op=ALU.mult).then_inc(SEM["sV"], 1)
            vbrk(v)
            v.tensor_scalar(out=t2[:, :], in0=t2[:, :], scalar1=1.0 / 127.0, scalar2=None, op0=ALU.mult).then_inc(SEM["sV"], 1)
            vbrk(v)
            v.reciprocal(out=g1[:, :], in_=g1[:, :]).then_inc(SEM["sV"], 1)
            vbrk(v)
            v.engine_nop().then_inc(SEM["sV"], 1)
        VE(_); C["sV"] += 1
        def _(te):
            te.wait_ge(SEM["sV"], C["sV"])
            te.matmul(pS[:, 8:9], lhsT=onesc[:, :], rhs=g1[:, :], start=True, stop=True)
            te.matmul(pS[:, 9:10], lhsT=onesc[:, :], rhs=t2[:, :], start=True, stop=True).then_inc(SEM["sT"], 1)
        TE(_); C["sT"] += 1
        def _(v):
            v.wait_ge(SEM["sT"], C["sT"])
            v.tensor_copy(out=rg1b[:, :], in_=pS[:, 8:9])
            v.tensor_copy(out=gs1b[:, :], in_=pS[:, 9:10]).then_inc(SEM["sV"], 1)
        VE(_); C["sV"] += 1
        rgS = C["sV"]
        for d in range(8):
            def _(sy, d=d):
                sy.wait_ge(SEM["sV"], C["sV"])
                sy.dma_start(out=wstg[d % 2][:, :], in_=hw1[d]).then_inc(SEM["dQ"], 16)
            SY(_); C["dQ"] += 16
            dq = C["dQ"]
            def _(v, d=d, dq=dq, rgS=rgS):
                v.wait_ge(SEM["dQ"], dq)
                v.wait_ge(SEM["sV"], rgS)
                v.tensor_scalar(out=wstg[d % 2][:, :], in0=wstg[d % 2][:, :], scalar1=rg1b[:, :],
                                scalar2=None, op0=ALU.mult).then_inc(SEM["sV"], 1)
            VE(_); C["sV"] += 1
            vS = C["sV"]
            def _(sc, d=d, vS=vS):
                sc.wait_ge(SEM["sV"], vS)
                sc.copy(out=wi[:, :], in_=wstg[d % 2][:, :]).then_inc(SEM["sA"], 1)
            SC(_); C["sA"] += 1
            aS = C["sA"]
            def _(v, d=d, aS=aS):
                v.wait_ge(SEM["sA"], aS)
                v.tensor_scalar(out=wi[:, :], in0=wi[:, :], scalar1=-1, scalar2=1,
                                op0=ALU.max, op1=ALU.min).then_inc(SEM["sV"], 1)
                vbrk(v)
                v.tensor_copy(out=w1s[d][:, :], in_=wi[:, :]).then_inc(SEM["sV"], 1)
            VE(_); C["sV"] += 1
        wS = C["sV"]

        hstg = [sb(f"hstgL{i}", [128, D], F32) for i in range(2)]
        ci = sb("ciL", [128, D], I32)
        cb = [sb(f"cbL{i}", [128, D], BF16) for i in range(2)]
        c1f = [sb(f"c1f{d}", [128, N], BF16) for d in range(8)]
        trq = {}
        evQT = {}
        giT = [0]
        for t in range(16):
            def _(sy, t=t):
                sy.wait_ge(SEM["sV"], C["sV"])
                sy.dma_start(out=hstg[t % 2][:, :], in_=hh[128 * t:128 * (t + 1), :]).then_inc(SEM["dS"], 16)
            SY(_); C["dS"] += 16
            dq = C["dS"]
            def _(v, t=t, dq=dq, wS=wS):
                v.wait_ge(SEM["dS"], dq)
                v.wait_ge(SEM["sV"], wS)
                v.tensor_scalar(out=hstg[t % 2][:, :], in0=hstg[t % 2][:, :], scalar1=s1b[:, :],
                                scalar2=None, op0=ALU.mult).then_inc(SEM["sV"], 1)
            VE(_); C["sV"] += 1
            vS = C["sV"]
            def _(sc, t=t, vS=vS):
                sc.wait_ge(SEM["sV"], vS)
                sc.copy(out=ci[:, :], in_=hstg[t % 2][:, :]).then_inc(SEM["sA"], 1)
            SC(_); C["sA"] += 1
            aS = C["sA"]
            def _(v, t=t, aS=aS):
                v.wait_ge(SEM["sA"], aS)
                if t >= 2:
                    v.wait_ge(SEM["sT"], trq[t - 2])
                v.tensor_scalar(out=ci[:, :], in0=ci[:, :], scalar1=-128, scalar2=127,
                                op0=ALU.max, op1=ALU.min).then_inc(SEM["sV"], 1)
                vbrk(v)
                v.tensor_copy(out=cb[t % 2][:, :], in_=ci[:, :]).then_inc(SEM["sV"], 1)
            VE(_); C["sV"] += 1
            cbS = C["sV"]
            for d0 in range(0, 8, 2):
                gi = giT[0]
                half = (gi % 2) * 256
                pw = evQT.get(gi - 2)
                def _(te, t=t, d0=d0, half=half, cbS=cbS, pw=pw):
                    te.wait_ge(SEM["sV"], cbS)
                    if pw is not None:
                        te.wait_ge(SEM["sP"], pw)
                    for u in range(2):
                        mm = te.transpose(pT[:, half + 128 * u:half + 128 * (u + 1)],
                                          in_=cb[t % 2][:, 128 * (d0 + u):128 * (d0 + u + 1)],
                                          identity=idnb[:, :])
                    mm.then_inc(SEM["sT"], 1)
                TE(_); C["sT"] += 1
                tS = C["sT"]
                def _(sc, t=t, d0=d0, half=half, tS=tS):
                    sc.wait_ge(SEM["sT"], tS)
                    sc.activation(out=c1f[d0][:, 128 * t:128 * (t + 1)],
                                  in_=pT[:, half:half + 128], func=AF.Copy)
                    sc.activation(out=c1f[d0 + 1][:, 128 * t:128 * (t + 1)],
                                  in_=pT[:, half + 128:half + 256], func=AF.Copy).then_inc(SEM["sP"], 1)
                SC(_); C["sP"] += 1
                evQT[gi] = C["sP"]
                giT[0] = gi + 1
            trq[t] = C["sT"]
        trAll = C["sP"]

        h1t = [sb(f"h1t{i}", [128, 512], F32) for i in range(2)]
        habs = sb("habsL", [128, 512], F32)
        scrN = [sb(f"scrq{i}", [128, 512]) for i in range(4)]
        qc = sb("qcL", [128, NBIN])
        qr2 = sb("qr2L", [128, NBIN])
        qr = sb("qrL", [1, NBIN])
        h1dma = {}
        geluA = {}
        for j in range(128):
            ffc, tc = j // 4, j % 4
            ps = [pM, pM2][j % 2]
            buf = j % 2
            def _(te, j=j, ffc=ffc, tc=tc, ps=ps, wS=wS, trAll=trAll):
                if j == 0:
                    te.wait_ge(SEM["sP"], trAll)
                    te.wait_ge(SEM["sV"], wS)
                if j >= 2 and (j - 2) in geluA:
                    te.wait_ge(SEM["sA"], geluA[j - 2])
                for d in range(8):
                    mm = te.matmul(ps[:, :], lhsT=w1s[d][:, 128 * ffc:128 * (ffc + 1)],
                                   rhs=c1f[d][:, 512 * tc:512 * (tc + 1)],
                                   start=(d == 0), stop=(d == 7))
                mm.then_inc(SEM["sT"], 1)
            TE(_); C["sT"] += 1
            tS = C["sT"]
            def _(sc, j=j, ps=ps, buf=buf, tS=tS):
                sc.wait_ge(SEM["sT"], tS)
                sc.wait_ge(SEM["sV"], C["sV"])
                if j >= 1:
                    sc.wait_ge(SEM["dQ"], h1dma[j - 1])
                sc.activation(out=h1t[buf][:, :], in_=ps[:, :], func=AF.Gelu, scale=gs1b[:, :]).then_inc(SEM["sA"], 1)
            SC(_); C["sA"] += 1
            geluA[j] = C["sA"]
            aS = C["sA"]
            def _(sy, j=j, ffc=ffc, tc=tc, buf=buf, aS=aS):
                sy.wait_ge(SEM["sA"], aS)
                sy.dma_start(out=oh1[128 * ffc:128 * (ffc + 1), 512 * tc:512 * (tc + 1)],
                             in_=h1t[buf][:, :]).then_inc(SEM["dQ"], 16)
            SY(_); C["dQ"] += 16
            h1dma[j] = C["dQ"]
            def _(sc, buf=buf, aS=aS):
                sc.wait_ge(SEM["sA"], aS)
                sc.wait_ge(SEM["sV"], C["sV"])
                sc.activation(out=habs[:, :], in_=h1t[buf][:, :], func=AF.Abs).then_inc(SEM["sA"], 1)
            SC(_); C["sA"] += 1
            aS = C["sA"]
            def _(v, j=j, aS=aS):
                v.wait_ge(SEM["sA"], aS)
                for bn in range(NBIN):
                    op = v.tensor_scalar(out=scrN[bn % 4][:, :], in0=habs[:, :],
                                         scalar1=QCAP2 * bn / NBIN, scalar2=0.0,
                                         op0=ALU.is_ge, op1=ALU.add,
                                         accum_out=(qc if j == 0 else qr2)[:, bn:bn + 1])
                    if bn % 4 == 3:
                        op.then_inc(SEM["sV"], 1)
                        vbrk(v)
                v.engine_nop().then_inc(SEM["sV"], 1)
                vbrk(v)
                if j > 0:
                    v.tensor_tensor(out=qc[:, :], in0=qc[:, :], in1=qr2[:, :], op=ALU.add).then_inc(SEM["sV"], 1)
                    C["sV"] += 1
                    v.wait_ge(SEM["sV"], C["sV"])
                v.engine_nop().then_inc(SEM["sV"], 1)
            VE(_); C["sV"] += 1

        def _(te):
            te.wait_ge(SEM["sV"], C["sV"])
            te.matmul(pS[0:1, 10:10 + NBIN], lhsT=onescol[:, :], rhs=qc[:, :], start=True, stop=True).then_inc(SEM["sT"], 1)
        TE(_); C["sT"] += 1
        am2 = sb("am2", [1, 2])
        def _(v):
            v.wait_ge(SEM["sT"], C["sT"])
            v.tensor_copy(out=qr[:, :], in_=pS[0:1, 10:10 + NBIN])
            v.tensor_copy(out=am2[0:1, 0:1], in_=amax[:, :])
            v.tensor_copy(out=am2[0:1, 1:2], in_=amax[:, :]).then_inc(SEM["sV"], 1)
        VE(_); C["sV"] += 1
        def _(sy):
            sy.wait_ge(SEM["sV"], C["sV"])
            sy.dma_start(out=ocnt, in_=qr[:, :]).then_inc(SEM["dQ"], 16)
            sy.dma_start(out=oam, in_=am2[:, :]).then_inc(SEM["dQ"], 16)
            sy.wait_ge(SEM["dQ"], C["dQ"] + 32)
        SY(_); C["dQ"] += 32
    return nc


def build_LF():
    nc = bass.Bass("TRN2", target_bir_lowering=False, debug=False)
    hcand = nc.dram_tensor("cands", [8 * 128, CANDW2], F32, kind="ExternalInput").ap()
    hmeta = nc.dram_tensor("metas", [8, 4], F32, kind="ExternalInput").ap()
    hh1 = nc.dram_tensor("h1", [FF, N], F32, kind="ExternalInput").ap()
    hx1 = nc.dram_tensor("x1", [N, D], F32, kind="ExternalInput").ap()
    hw2 = nc.dram_tensor("w2T", [32, 128, D], F32, kind="ExternalInput").ap()
    oout = nc.dram_tensor("out", [N, D], F32, kind="ExternalOutput").ap()
    oam = nc.dram_tensor("amax2", [1, 2], F32, kind="ExternalOutput").ap()
    with ExitStack() as ctx:
        def sb(name, shape, dtype=F32):
            return ctx.enter_context(nc.sbuf_tensor(name, shape, dtype))
        SEM = {}
        for k in ("dQ", "dS", "sT", "sV", "sA"):
            SEM[k] = ctx.enter_context(nc.semaphore(k))
        C = {k: 0 for k in SEM}
        pS = ctx.enter_context(nc.psum_tensor("pS", [128, 128], F32))
        pMa = ctx.enter_context(nc.psum_tensor("pMa", [128, 512], F32))
        pMb = ctx.enter_context(nc.psum_tensor("pMb", [128, 512], F32))
        block = ctx.enter_context(nc.Block())
        SY, SC, VE, TE = block.sync, block.scalar, block.vector, block.tensor
        amax = sb("amax", [1, 1])

        def vbrk(v):
            C["sV"] += 1
            v.wait_ge(SEM["sV"], C["sV"])

        emit_select(nc, block, SEM, C, sb, pS, hcand, hmeta, KHI2, FRAC2, amax, candw=CANDW2, dq="dS")

        s2b = sb("s2b", [128, 1])
        gs2 = sb("gs2", [1, 1])
        gs2b = sb("gs2b", [128, 1])
        rg2b = sb("rg2b", [128, 1])
        t1 = sb("t1F", [1, 1]); t2 = sb("t2F", [1, 1])
        onesc = sb("onescF", [1, 128]); onescol = sb("onescolF", [128, 1])
        def _(v):
            v.wait_ge(SEM["sV"], C["sV"])
            v.memset(onesc[:, :], 1.0)
            v.memset(onescol[:, :], 1.0)
            v.reciprocal(out=t1[:, :], in_=amax[:, :]).then_inc(SEM["sV"], 1)
            vbrk(v)
            v.tensor_scalar(out=t1[:, :], in0=t1[:, :], scalar1=127.0, scalar2=None, op0=ALU.mult).then_inc(SEM["sV"], 1)
            vbrk(v)
            v.engine_nop().then_inc(SEM["sV"], 1)
        VE(_); C["sV"] += 1
        def _(te):
            te.wait_ge(SEM["sV"], C["sV"])
            te.matmul(pS[:, 6:7], lhsT=onesc[:, :], rhs=t1[:, :], start=True, stop=True).then_inc(SEM["sT"], 1)
        TE(_); C["sT"] += 1
        def _(v):
            v.wait_ge(SEM["sT"], C["sT"])
            v.tensor_copy(out=s2b[:, :], in_=pS[:, 6:7]).then_inc(SEM["sV"], 1)
        VE(_); C["sV"] += 1

        wstg = [sb(f"wstgF{i}", [128, D], F32) for i in range(2)]
        w2s = [sb(f"w2s{f}", [128, D], BF16) for f in range(32)]
        wi = sb("wiF", [128, D], I32)
        gsum = sb("gsumF", [128, 32])
        g2 = sb("g2F", [1, 1])
        for f in range(32):
            def _(sy, f=f):
                if f >= 1:
                    sy.wait_ge(SEM["sA"], C["sA"])
                sy.dma_start(out=wstg[f % 2][:, :], in_=hw2[f]).then_inc(SEM["dQ"], 16)
            SY(_); C["dQ"] += 16
            dq = C["dQ"]
            def _(sc, f=f, dq=dq):
                sc.wait_ge(SEM["dQ"], dq)
                sc.activation(out=wstg[f % 2][:, :], in_=wstg[f % 2][:, :], func=AF.Abs,
                              accum_out=gsum[:, f:f + 1]).then_inc(SEM["sA"], 1)
            SC(_); C["sA"] += 1
        def _(v):
            v.wait_ge(SEM["sA"], C["sA"])
            v.reduce_sum(out=gsum[:, 0:1], in_=gsum[:, :], axis=AX.X).then_inc(SEM["sV"], 1)
            vbrk(v)
            v.engine_nop().then_inc(SEM["sV"], 1)
        VE(_); C["sV"] += 1
        def _(te):
            te.wait_ge(SEM["sV"], C["sV"])
            te.matmul(pS[0:1, 7:8], lhsT=onescol[:, :], rhs=gsum[:, 0:1], start=True, stop=True).then_inc(SEM["sT"], 1)
        TE(_); C["sT"] += 1
        def _(v):
            v.wait_ge(SEM["sT"], C["sT"])
            v.tensor_copy(out=g2[:, :], in_=pS[0:1, 7:8]).then_inc(SEM["sV"], 1)
            vbrk(v)
            v.tensor_scalar(out=g2[:, :], in0=g2[:, :], scalar1=1.0 / (FF * D), scalar2=1e-5,
                            op0=ALU.mult, op1=ALU.add).then_inc(SEM["sV"], 1)
            vbrk(v)
            v.tensor_tensor(out=t2[:, :], in0=g2[:, :], in1=amax[:, :], op=ALU.mult).then_inc(SEM["sV"], 1)
            vbrk(v)
            v.tensor_scalar(out=gs2[:, :], in0=t2[:, :], scalar1=1.0 / 127.0, scalar2=None, op0=ALU.mult).then_inc(SEM["sV"], 1)
            vbrk(v)
            v.reciprocal(out=g2[:, :], in_=g2[:, :]).then_inc(SEM["sV"], 1)
            vbrk(v)
            v.engine_nop().then_inc(SEM["sV"], 1)
        VE(_); C["sV"] += 1
        def _(te):
            te.wait_ge(SEM["sV"], C["sV"])
            te.matmul(pS[:, 8:9], lhsT=onesc[:, :], rhs=g2[:, :], start=True, stop=True)
            te.matmul(pS[:, 9:10], lhsT=onesc[:, :], rhs=gs2[:, :], start=True, stop=True).then_inc(SEM["sT"], 1)
        TE(_); C["sT"] += 1
        def _(v):
            v.wait_ge(SEM["sT"], C["sT"])
            v.tensor_copy(out=rg2b[:, :], in_=pS[:, 8:9])
            v.tensor_copy(out=gs2b[:, :], in_=pS[:, 9:10]).then_inc(SEM["sV"], 1)
        VE(_); C["sV"] += 1
        rgS = C["sV"]
        for f in range(32):
            def _(sy, f=f):
                sy.wait_ge(SEM["sV"], C["sV"])
                sy.dma_start(out=wstg[f % 2][:, :], in_=hw2[f]).then_inc(SEM["dQ"], 16)
            SY(_); C["dQ"] += 16
            dq = C["dQ"]
            def _(v, f=f, dq=dq, rgS=rgS):
                v.wait_ge(SEM["dQ"], dq)
                v.wait_ge(SEM["sV"], rgS)
                v.tensor_scalar(out=wstg[f % 2][:, :], in0=wstg[f % 2][:, :], scalar1=rg2b[:, :],
                                scalar2=None, op0=ALU.mult).then_inc(SEM["sV"], 1)
            VE(_); C["sV"] += 1
            def _(v, f=f):
                v.wait_ge(SEM["sV"], C["sV"])
                v.tensor_copy(out=wi[:, :], in_=wstg[f % 2][:, :]).then_inc(SEM["sV"], 1)
                vbrk(v)
                v.tensor_scalar(out=wi[:, :], in0=wi[:, :], scalar1=-1, scalar2=1,
                                op0=ALU.max, op1=ALU.min).then_inc(SEM["sV"], 1)
                vbrk(v)
                v.tensor_copy(out=w2s[f][:, :], in_=wi[:, :]).then_inc(SEM["sV"], 1)
            VE(_); C["sV"] += 1
        wS = C["sV"]

        h1c = [sb(f"h1c{i}", [128, 512], F32) for i in range(2)]
        ci = sb("ciF", [128, 512], I32)
        c2g = [sb(f"c2g{g}", [128, 512], BF16) for g in range(8)]
        xr = sb("xrF", [128, D], F32)
        sq = sb("sqF", [128, D], F32)
        ot = sb("otF", [128, D], F32)
        mu = sb("muF", [128, 1]); var = sb("varF", [128, 1]); rs = sb("rsF", [128, 1])
        eps = sb("epsF", [128, 1]); scD = sb("scDF", [128, 1])
        def _(v):
            v.memset(eps[:, :], 1e-5)
            v.memset(scD[:, :], 1.0 / D).then_inc(SEM["sV"], 1)
        VE(_); C["sV"] += 1
        outdma = {}
        for tc in range(16):
            for g in range(8):
                buf = g % 2
                def _(sy, tc=tc, g=g, buf=buf):
                    sy.wait_ge(SEM["sV"], C["sV"])
                    for i in range(4):
                        f = 4 * g + i
                        sy.dma_start(out=h1c[buf][:, 128 * i:128 * (i + 1)],
                                     in_=hh1[128 * f:128 * (f + 1), 128 * tc:128 * (tc + 1)]).then_inc(SEM["dS"], 16)
                SY(_); C["dS"] += 64
                dq = C["dS"]
                def _(v, g=g, buf=buf, dq=dq, wS=wS):
                    v.wait_ge(SEM["dS"], dq)
                    v.wait_ge(SEM["sV"], wS)
                    v.tensor_scalar(out=h1c[buf][:, :], in0=h1c[buf][:, :], scalar1=s2b[:, :],
                                    scalar2=None, op0=ALU.mult).then_inc(SEM["sV"], 1)
                    vbrk(v)
                    v.tensor_copy(out=ci[:, :], in_=h1c[buf][:, :]).then_inc(SEM["sV"], 1)
                    vbrk(v)
                    v.tensor_scalar(out=ci[:, :], in0=ci[:, :], scalar1=-128, scalar2=127,
                                    op0=ALU.max, op1=ALU.min).then_inc(SEM["sV"], 1)
                    vbrk(v)
                    v.tensor_copy(out=c2g[g][:, :], in_=ci[:, :]).then_inc(SEM["sV"], 1)
                VE(_); C["sV"] += 1
            cS = C["sV"]
            def _(te, tc=tc, cS=cS):
                te.wait_ge(SEM["sV"], cS)
                for f in range(32):
                    te.matmul(pMa[:, :], lhsT=c2g[f // 4][:, 128 * (f % 4):128 * (f % 4 + 1)], rhs=w2s[f][:, 0:512],
                              start=(f == 0), stop=(f == 31))
                for f in range(32):
                    mm = te.matmul(pMb[:, :], lhsT=c2g[f // 4][:, 128 * (f % 4):128 * (f % 4 + 1)], rhs=w2s[f][:, 512:1024],
                                   start=(f == 0), stop=(f == 31))
                mm.then_inc(SEM["sT"], 1)
            TE(_); C["sT"] += 1
            tS = C["sT"]
            def _(sy, tc=tc):
                sy.wait_ge(SEM["sV"], C["sV"])
                sy.dma_start(out=xr[:, :], in_=hx1[128 * tc:128 * (tc + 1), :]).then_inc(SEM["dQ"], 16)
            SY(_); C["dQ"] += 16
            xS = C["dQ"]
            def _(v, tS=tS, xS=xS, tc=tc):
                v.wait_ge(SEM["sT"], tS)
                v.wait_ge(SEM["dQ"], xS)
                v.tensor_scalar(out=ot[:, 0:512], in0=pMa[:, :], scalar1=gs2b[:, :], scalar2=None, op0=ALU.mult)
                v.tensor_scalar(out=ot[:, 512:1024], in0=pMb[:, :], scalar1=gs2b[:, :], scalar2=None, op0=ALU.mult).then_inc(SEM["sV"], 1)
                vbrk(v)
                v.tensor_tensor(out=ot[:, :], in0=ot[:, :], in1=xr[:, :], op=ALU.add).then_inc(SEM["sV"], 1)
                vbrk(v)
                v.reduce_sum(out=mu[:, :], in_=ot[:, :], axis=AX.X).then_inc(SEM["sV"], 1)
                vbrk(v)
                v.tensor_scalar(out=mu[:, :], in0=mu[:, :], scalar1=1.0 / D, scalar2=None, op0=ALU.mult).then_inc(SEM["sV"], 1)
                vbrk(v)
                v.tensor_scalar(out=sq[:, :], in0=ot[:, :], scalar1=mu[:, :], scalar2=None,
                                op0=ALU.subtract).then_inc(SEM["sV"], 1)
            VE(_); C["sV"] += 1
            vS = C["sV"]
            def _(sc, vS=vS):
                sc.wait_ge(SEM["sV"], vS)
                sc.activation(out=sq[:, :], in_=sq[:, :], func=AF.Square, accum_out=var[:, :]).then_inc(SEM["sA"], 1)
            SC(_); C["sA"] += 1
            aS = C["sA"]
            def _(sc, aS=aS):
                sc.wait_ge(SEM["sA"], aS)
                sc.activation(out=rs[:, :], in_=var[:, :], func=AF.Sqrt, scale=scD[:, :], bias=eps[:, :]).then_inc(SEM["sA"], 1)
            SC(_); C["sA"] += 1
            aS = C["sA"]
            def _(v, aS=aS):
                v.wait_ge(SEM["sA"], aS)
                v.reciprocal(out=rs[:, :], in_=rs[:, :]).then_inc(SEM["sV"], 1)
                vbrk(v)
                v.tensor_scalar(out=ot[:, :], in0=ot[:, :], scalar1=mu[:, :], scalar2=rs[:, :],
                                op0=ALU.subtract, op1=ALU.mult).then_inc(SEM["sV"], 1)
            VE(_); C["sV"] += 1
            vS = C["sV"]
            def _(sy, tc=tc, vS=vS):
                sy.wait_ge(SEM["sV"], vS)
                sy.dma_start(out=oout[128 * tc:128 * (tc + 1), :], in_=ot[:, :]).then_inc(SEM["dQ"], 16)
            SY(_); C["dQ"] += 16
            outdma[tc] = C["dQ"]

        am2 = sb("am2F", [1, 2])
        def _(v):
            v.wait_ge(SEM["sV"], C["sV"])
            v.tensor_copy(out=am2[0:1, 0:1], in_=amax[:, :])
            v.tensor_copy(out=am2[0:1, 1:2], in_=amax[:, :]).then_inc(SEM["sV"], 1)
        VE(_); C["sV"] += 1
        def _(sy):
            sy.wait_ge(SEM["sV"], C["sV"])
            sy.dma_start(out=oam, in_=am2[:, :]).then_inc(SEM["dQ"], 16)
            sy.wait_ge(SEM["dQ"], C["dQ"] + 16)
        SY(_); C["dQ"] += 16
    return nc


_NCS = {}


def _get(name, builder):
    if name not in _NCS:
        _NCS[name] = builder()
    return _NCS[name]


def kernel(**inputs):
    cores = list(range(B))
    nc = _get("LA", build_LA)
    in_maps = [host_inputs_LA(inputs, c) for c in range(B)]
    res = run_bass_kernel_spmd(nc, in_maps, core_ids=cores)
    x1 = [np.asarray(res.results[c]["x1"]) for c in range(B)]
    hsig = [np.asarray(res.results[c]["hsig"]) for c in range(B)]
    habs = [np.asarray(res.results[c]["habs"]) for c in range(B)]
    cnts = np.stack([np.asarray(res.results[c]["cnt1"])[0] for c in range(B)])
    brk = np.array([[0.0, QCAP1 / NBIN]], np.float32)

    nc = _get("q1r2", lambda: build_round(N, D, NBIN, KHI1))
    res = run_bass_kernel_spmd(nc, [{"src": habs[c], "cin": cnts.astype(np.float32),
                                     "bin": brk.astype(np.float32)} for c in range(B)], core_ids=cores)
    cnts = np.stack([np.asarray(res.results[c]["cout"])[0] for c in range(B)])
    brk = np.asarray(res.results[0]["bout"])

    nc = _get("q1x", lambda: build_extract(N, D, NBR, KHI1, kext=4, candw=CANDW1))
    res = run_bass_kernel_spmd(nc, [{"src": habs[c], "cin": cnts.astype(np.float32),
                                     "bin": brk.astype(np.float32)} for c in range(B)], core_ids=cores)
    cands = np.concatenate([np.asarray(res.results[c]["cand"]) for c in range(B)], axis=0)
    metas = np.stack([np.asarray(res.results[c]["meta"])[0] for c in range(B)])

    w1T = np.ascontiguousarray(np.asarray(inputs["L0_ffn_w1"], np.float32).T.reshape(8, 128, FF))
    nc = _get("LD", build_LD)
    eye = np.eye(128, dtype=np.float32)
    res = run_bass_kernel_spmd(nc, [{"cands": cands, "metas": metas, "hsig": hsig[c], "w1T": w1T, "eye": eye}
                                    for c in range(B)], core_ids=cores)
    h1 = [np.asarray(res.results[c]["h1"]) for c in range(B)]
    cnts = np.stack([np.asarray(res.results[c]["cnt2"])[0] for c in range(B)])
    brk = np.array([[0.0, QCAP2 / NBIN]], np.float32)

    nc = _get("q2r2", lambda: build_round(FF, N, NBIN, KHI2))
    res = run_bass_kernel_spmd(nc, [{"src": h1[c], "cin": cnts.astype(np.float32),
                                     "bin": brk.astype(np.float32)} for c in range(B)], core_ids=cores)
    cnts = np.stack([np.asarray(res.results[c]["cout"])[0] for c in range(B)])
    brk = np.asarray(res.results[0]["bout"])

    nc = _get("q2x", lambda: build_extract(FF, N, NBR, KHI2, kext=4, candw=CANDW2))
    res = run_bass_kernel_spmd(nc, [{"src": h1[c], "cin": cnts.astype(np.float32),
                                     "bin": brk.astype(np.float32)} for c in range(B)], core_ids=cores)
    cands = np.concatenate([np.asarray(res.results[c]["cand"]) for c in range(B)], axis=0)
    metas = np.stack([np.asarray(res.results[c]["meta"])[0] for c in range(B)])

    w2T = np.ascontiguousarray(np.asarray(inputs["L0_ffn_w2"], np.float32).T.reshape(32, 128, D))
    nc = _get("LF", build_LF)
    res = run_bass_kernel_spmd(nc, [{"cands": cands, "metas": metas, "h1": h1[c], "x1": x1[c], "w2T": w2T}
                                    for c in range(B)], core_ids=cores)
    out = np.stack([np.asarray(res.results[c]["out"]) for c in range(B)])
    return out
